# version 04
# baseline (speedup 1.0000x reference)
import contextlib
import numpy as np
import concourse.bass as bass
import concourse.mybir as mybir
from concourse.bass_utils import run_bass_kernel_spmd

F32 = mybir.dt.float32
BF16 = mybir.dt.bfloat16
AF = mybir.ActivationFunctionType
ALU = mybir.AluOpType
AX = mybir.AxisListType

NCORE = 8
TOK = 2048
D = 2048
KB = 16
H = 5632
HBK = 44
MT = 16
HALF = 1024
G = 4
NG = HBK // G
EPS = 1e-5
EPS16 = 16.0 * EPS
WELEM = float(H * D)

_NC_CACHE = {}
_DEBUG_Y2 = False


class Emitter:

    def __init__(self, sems, counts=None):
        self.q = {k: [] for k in ("sync", "scalar", "vector", "tensor", "gpsimd")}
        self.n = dict(counts) if counts else {name: 0 for name in sems}
        self.sems = sems

    def emit(self, eng, fn, waits=(), incs=(), ms=False):
        waits = [(s, v) for (s, v) in waits if v is not None and v > 0]
        incs = list(incs)
        after = None
        for s, k in incs:
            self.n[s] += k
            after = self.n[s]
        self.q[eng].append((fn, waits, incs))
        return after if ms else None

    def replay(self, eng_name, eng):
        for fn, waits, incs in self.q[eng_name]:
            for s, v in waits:
                eng.wait_ge(self.sems[s], v)
            ins = fn(eng)
            for s, k in incs:
                ins.then_inc(self.sems[s], k)

    def run_block(self, nc):
        with nc.Block() as block:
            @block.sync
            def _(e):
                self.replay("sync", e)

            @block.scalar
            def _(e):
                self.replay("scalar", e)

            @block.vector
            def _(e):
                self.replay("vector", e)

            @block.tensor
            def _(e):
                self.replay("tensor", e)

            @block.gpsimd
            def _(e):
                self.replay("gpsimd", e)


def build_program():
    nc = bass.Bass("TRN2", target_bir_lowering=False, debug=False)

    x_d = nc.dram_tensor("x", [TOK, D], F32, kind="ExternalInput").ap()
    w0b_d = nc.dram_tensor("w0b", [128, D], F32, kind="ExternalInput").ap()
    w1b_d = nc.dram_tensor("w1b", [128, D], F32, kind="ExternalInput").ap()
    biasb_d = nc.dram_tensor("biasb", [128, D], F32, kind="ExternalInput").ap()
    dvrep_d = nc.dram_tensor("dvrep", [128, 2], F32, kind="ExternalInput").ap()
    onesrow_d = nc.dram_tensor("onesrow", [1, 128], F32, kind="ExternalInput").ap()
    onescol_d = nc.dram_tensor("onescol", [128, 1], F32, kind="ExternalInput").ap()
    epsh_d = nc.dram_tensor("epsh", [1, 1], F32, kind="ExternalInput").ap()
    w1f_d = nc.dram_tensor("w1f", [H, D], F32, kind="ExternalInput").ap()
    w2f_d = nc.dram_tensor("w2f", [H, D], F32, kind="ExternalInput").ap()
    w3f_d = nc.dram_tensor("w3f", [D, H], F32, kind="ExternalInput").ap()
    out_d = nc.dram_tensor("out", [TOK, D], F32, kind="ExternalOutput").ap()
    q1t_d = nc.dram_tensor("q1t", [H, D], BF16, kind="Internal").ap()
    q2t_d = nc.dram_tensor("q2t", [H, D], BF16, kind="Internal").ap()
    q3t_d = nc.dram_tensor("q3t", [H, D], BF16, kind="Internal").ap()
    h1t_d = [nc.dram_tensor(f"h1t{h}", [KB * 128, HALF], BF16, kind="Internal").ap()
             for h in range(2)]
    h1n_d = nc.dram_tensor("h1n", [TOK, D], BF16, kind="Internal").ap()
    dbgy2_d = nc.dram_tensor("dbgy2", [TOK, D], F32,
                             kind="ExternalOutput" if _DEBUG_Y2 else "Internal").ap()

    sem_names = (["cv", "cs", "ct", "cg", "sK", "sDBG"] +
                 [f"sX{i}" for i in range(2)] + [f"sG{i}" for i in range(2)] +
                 [f"sP{i}" for i in range(3)] + [f"sT{i}" for i in range(2)] +
                 [f"sS{i}" for i in range(2)] + [f"sHT{i}" for i in range(2)] +
                 [f"sHS{i}" for i in range(2)] + [f"sHN{i}" for i in range(2)] +
                 ["sH1"] + [f"sW1{i}" for i in range(2)] +
                 [f"sW2{i}" for i in range(2)] + [f"sW3{i}" for i in range(2)] +
                 [f"sHB{i}" for i in range(2)] + [f"sO{i}" for i in range(2)])
    stack = contextlib.ExitStack()
    sems = {nm: stack.enter_context(nc.semaphore(nm)) for nm in sem_names}

    with stack:
        thc = stack.enter_context(nc.sbuf_tensor("thc", [128, 8], F32))
        onesr = stack.enter_context(nc.sbuf_tensor("onesr", [1, 128], F32))
        onesc = stack.enter_context(nc.sbuf_tensor("onesc", [128, 1], F32))
        epshs = stack.enter_context(nc.sbuf_tensor("epshs", [1, 4], F32))
        gacc = stack.enter_context(nc.sbuf_tensor("gacc", [128, 4], F32))
        thsml = stack.enter_context(nc.sbuf_tensor("thsml", [1, 4], F32))
        if True:
            totals_b1 = _block1(nc, sems, x_d, w0b_d, w1b_d, biasb_d, dvrep_d,
                                onesrow_d, onescol_d, epsh_d,
                                [w1f_d, w2f_d, w3f_d],
                                [q1t_d, q2t_d, q3t_d], h1t_d, h1n_d,
                                thc, onesr, onesc, epshs, gacc, thsml)
            _block3(nc, sems, totals_b1, q1t_d, q2t_d, q3t_d, h1t_d, h1n_d, out_d,
                    dbgy2_d)
    return nc


def _block1(nc, sems, x_d, w0b_d, w1b_d, biasb_d, dvrep_d, onesrow_d, onescol_d,
            epsh_d, wf_ds, qt_ds, h1t_d, h1n_d, thc, onesr, onesc, epshs, gacc, thsml):
    CH = 2048
    NCHW = (H * D) // (128 * CH)
    _st = contextlib.ExitStack()
    w0bs = _st.enter_context(nc.sbuf_tensor("w0bs", [128, D], F32))
    w1bs = _st.enter_context(nc.sbuf_tensor("w1bs", [128, D], F32))
    biasbs = _st.enter_context(nc.sbuf_tensor("biasbs", [128, D], F32))
    dvreps = _st.enter_context(nc.sbuf_tensor("dvreps", [128, 2], F32))
    dvc = _st.enter_context(nc.sbuf_tensor("dvc", [128, 2], F32))
    dv0t = _st.enter_context(nc.sbuf_tensor("dv0t", [128, D], F32))
    dv1t = _st.enter_context(nc.sbuf_tensor("dv1t", [128, D], F32))
    xin = _st.enter_context(nc.sbuf_tensor("xin", [128, 2 * D], F32))
    sA = _st.enter_context(nc.sbuf_tensor("sA", [128, D], F32))
    sB = _st.enter_context(nc.sbuf_tensor("sB", [128, D], F32))
    sC = _st.enter_context(nc.sbuf_tensor("sC", [128, D], F32))
    sD = _st.enter_context(nc.sbuf_tensor("sD", [128, D], F32))
    ubf = _st.enter_context(nc.sbuf_tensor("ubf", [128, 3 * D], BF16))
    h1nb = _st.enter_context(nc.sbuf_tensor("h1nb", [128, 2 * D], BF16))
    tbuf = _st.enter_context(nc.sbuf_tensor("tbuf", [128, 2 * D], BF16))
    gb = _st.enter_context(nc.sbuf_tensor("gb", [128, 2 * CH], F32))
    gjunk = _st.enter_context(nc.sbuf_tensor("gjunk", [128, CH], BF16))
    gpart = _st.enter_context(nc.sbuf_tensor("gpart", [128, 2], F32))
    stats = _st.enter_context(nc.sbuf_tensor("stats", [128, 16], F32))
    pin = _st.enter_context(nc.sbuf_tensor("pin", [128, 3 * D], F32))
    s1b = _st.enter_context(nc.sbuf_tensor("s1b", [128, 2 * D], BF16))
    v2b = _st.enter_context(nc.sbuf_tensor("v2b", [128, 2 * D], BF16))
    qnb = _st.enter_context(nc.sbuf_tensor("qnb", [128, 2 * D], BF16))
    qtb = _st.enter_context(nc.sbuf_tensor("qtb", [128, 2 * D], BF16))
    p_th = _st.enter_context(nc.psum_tensor("p_th", [1, 4], F32))
    p_bc = _st.enter_context(nc.psum_tensor("p_bc", [128, 4], F32))
    with _st:
        E = Emitter(sems)

        d_rows = None
        for dst, src in ((w0bs[:], w0b_d), (w1bs[:], w1b_d), (biasbs[:], biasb_d),
                         (dvreps[:], dvrep_d), (onesr[:], onesrow_d),
                         (onesc[:], onescol_d), (epshs[0:1, 0:1], epsh_d)):
            d_rows = E.emit("scalar",
                            lambda e, dst=dst, src=src: e.dma_start(out=dst, in_=src),
                            incs=[("sK", 16)], ms=True)
        ms_dvc = E.emit("vector",
               lambda e: e.tensor_scalar(dvc[:], dvreps[:], 0.9, 1.0, ALU.max, ALU.min),
               waits=[("sK", d_rows)], incs=[("cv", 1)], ms=True)
        E.emit("vector",
               lambda e: e.tensor_scalar(dv0t[:], w0bs[:], 0.0, dvc[:, 0:1],
                                         ALU.mult, ALU.add),
               waits=[("cv", ms_dvc)], incs=[("cv", 1)])
        ms_dv = E.emit("vector",
                       lambda e: e.tensor_scalar(dv1t[:], w0bs[:], 0.0, dvc[:, 1:2],
                                                 ALU.mult, ALU.add),
                       incs=[("cv", 1)], ms=True)
        ms_gz = E.emit("gpsimd", lambda e: e.memset(gacc[:], 0.0),
                       incs=[("cg", 1)], ms=True)

        flatw = [w.rearrange("a b -> (a b)") for w in wf_ds]

        ms_x = [None] * MT
        ms_xfree = [None] * MT
        ms_norm = [None] * MT
        ms_h1free = [None] * MT
        ms_h1tp = [None] * MT

        def emit_x_dma(m):
            waits = [("cv", ms_xfree[m - 2])] if m >= 2 else []
            ms_x[m] = E.emit(
                "scalar",
                lambda e, m=m: e.dma_start(
                    out=xin[:, (m % 2) * D:(m % 2) * D + D],
                    in_=x_d[m * 128:(m + 1) * 128, :]),
                waits=waits, incs=[(f"sX{m % 2}", 16)], ms=True)

        def emit_attn_tile(m):
            xm = xin[:, (m % 2) * D:(m % 2) * D + D]
            ub = ubf[:, (m % 3) * D:(m % 3) * D + D]
            h1b = h1nb[:, (m % 2) * D:(m % 2) * D + D]
            waits = [(f"sX{m % 2}", ms_x[m]), ("cv", ms_dv)]
            E.emit("vector", lambda e: e.tensor_mul(sA[:], xm, w0bs[:]),
                   waits=waits, incs=[("cv", 1)])
            E.emit("vector",
                   lambda e: e.tensor_tensor_scan(sB[:], dv0t[:], sA[:], 0.0,
                                                  ALU.mult, ALU.add),
                   incs=[("cv", 1)])
            E.emit("vector",
                   lambda e: e.tensor_tensor_scan(sC[:], dv1t[:], xm, 0.0,
                                                  ALU.mult, ALU.add),
                   incs=[("cv", 1)])
            E.emit("vector", lambda e: e.tensor_mul(sD[:], sC[:], w1bs[:]),
                   incs=[("cv", 1)])
            E.emit("vector", lambda e: e.tensor_add(sC[:], sB[:], sD[:]),
                   incs=[("cv", 1)])
            ms_xfree[m] = E.emit("vector",
                                 lambda e: e.tensor_add(sA[:], xm, biasbs[:]),
                                 incs=[("cv", 1)], ms=True)
            wu = [("cv", ms_norm[m - 3])] if m >= 3 else []
            E.emit("vector",
                   lambda e: e.scalar_tensor_tensor(ub, sC[:], 1.0, sA[:],
                                                    ALU.bypass, ALU.add,
                                                    accum_out=stats[:, 0:1]),
                   waits=wu, incs=[("cv", 1)])
            mq = E.emit("vector",
                   lambda e: e.scalar_tensor_tensor(sB[:], ub, 1.0, ub,
                                                    ALU.bypass, ALU.mult,
                                                    accum_out=stats[:, 1:2]),
                   incs=[("cv", 1)], ms=True)
            mq = E.emit("vector",
                   lambda e: e.tensor_scalar(stats[:, 2:3], stats[:, 0:1],
                                             -1.0 / D, None, ALU.mult),
                   waits=[("cv", mq)], incs=[("cv", 1)], ms=True)
            mq = E.emit("vector",
                   lambda e: e.tensor_scalar(stats[:, 3:4], stats[:, 1:2],
                                             1.0 / D, EPS, ALU.mult, ALU.add),
                   waits=[("cv", mq)], incs=[("cv", 1)], ms=True)
            mq = E.emit("vector",
                   lambda e: e.tensor_scalar(stats[:, 4:5], stats[:, 2:3],
                                             stats[:, 2:3], None, ALU.mult),
                   waits=[("cv", mq)], incs=[("cv", 1)], ms=True)
            mq = E.emit("vector",
                   lambda e: e.tensor_sub(stats[:, 5:6], stats[:, 3:4], stats[:, 4:5]),
                   waits=[("cv", mq)], incs=[("cv", 1)], ms=True)
            msv = E.emit("vector",
                         lambda e: e.tensor_scalar(stats[:, 5:6], stats[:, 5:6],
                                                   1e-30, None, ALU.max),
                         waits=[("cv", mq)], incs=[("cv", 1)], ms=True)
            mss = E.emit("scalar",
                         lambda e: e.activation(stats[:, 6:7], stats[:, 5:6], AF.Sqrt),
                         waits=[("cv", msv)], incs=[("cs", 1)], ms=True)
            msr = E.emit("vector", lambda e: e.reciprocal(stats[:, 7:8], stats[:, 6:7]),
                   waits=[("cs", mss)], incs=[("cv", 1)], ms=True)
            wh = ([(f"sHN{m % 2}", ms_h1free[m - 2])] if m >= 2 else []) + [("cv", msr)]
            ms_norm[m] = E.emit("vector",
                                lambda e: e.tensor_scalar(h1b, ub, stats[:, 2:3],
                                                          stats[:, 7:8],
                                                          ALU.add, ALU.mult),
                                waits=wh, incs=[("cv", 1)], ms=True)
            w_tp = [("cv", ms_norm[m])] +                    ([(f"sHS{m % 2}", ms_h1tp[m - 2])] if m >= 2 else [])
            mtp = E.emit("scalar",
                   lambda e, m=m: e.dma_start_transpose(
                       tbuf[:, (m % 2) * D:(m % 2) * D + D]
                           .rearrange("p (t q) -> p t q", t=KB),
                       h1b),
                   waits=w_tp, incs=[(f"sHT{m % 2}", 16)], ms=True)
            ms_h1tp[m] = E.emit("scalar",
                   lambda e, m=m: e.dma_start(
                       out=h1t_d[m // 8][:, (m % 8) * 128:(m % 8) * 128 + 128]
                           .rearrange("(k p) q -> p k q", p=128),
                       in_=tbuf[:, (m % 2) * D:(m % 2) * D + D]
                           .rearrange("p (t q) -> p t q", t=KB)),
                   waits=[(f"sHT{m % 2}", mtp)], incs=[(f"sHS{m % 2}", 16)], ms=True)
            ms_h1free[m] = E.emit("scalar",
                                  lambda e, m=m: e.dma_start(
                                      out=h1n_d[m * 128:(m + 1) * 128, :], in_=h1b),
                                  incs=[(f"sHN{m % 2}", 16)], ms=True)

        jobs = []
        for hb in range(HBK):
            jobs.append((0, wf_ds[0][hb * 128:(hb + 1) * 128, :], D,
                         qt_ds[0][hb * 128:(hb + 1) * 128, :], False))
        for hb in range(HBK):
            jobs.append((1, wf_ds[1][hb * 128:(hb + 1) * 128, :], D,
                         qt_ds[1][hb * 128:(hb + 1) * 128, :], False))
        for b in range(KB):
            for cc, Fc in ((0, 2048), (1, 2048), (2, 1536)):
                jobs.append((2, wf_ds[2][b * 128:(b + 1) * 128, cc * 2048:cc * 2048 + Fc],
                             Fc,
                             qt_ds[2][cc * 2048:cc * 2048 + Fc, b * 128:(b + 1) * 128]
                             .rearrange("(t p) q -> p t q", p=128),
                             True))
        NJ = len(jobs)
        ms_in = [None] * NJ
        ms_s1 = [None] * NJ
        ms_v2 = [None] * NJ
        ms_cmb = [None] * NJ
        ms_tp = [None] * NJ
        ms_st = [None] * NJ
        ms_th = [None, None, None]

        def emit_quant_job(ji):
            wi, src, F, dst, is3 = jobs[ji]
            pin_s = pin[:, (ji % 3) * D:(ji % 3) * D + F]
            s1_s = s1b[:, (ji % 2) * D:(ji % 2) * D + F]
            v2_s = v2b[:, (ji % 2) * D:(ji % 2) * D + F]
            qn_s = qnb[:, (ji % 2) * D:(ji % 2) * D + F]
            qt_s = qtb[:, (ji % 2) * D:(ji % 2) * D + F]
            w_in = []
            if ji >= 3:
                w_in += [("cs", ms_s1[ji - 3]), ("cv", ms_v2[ji - 3])]
            ms_in[ji] = E.emit("sync",
                               lambda e, pin_s=pin_s, src=src: e.dma_start(
                                   out=pin_s, in_=src),
                               waits=w_in, incs=[(f"sP{ji % 3}", 16)], ms=True)
            w_s = [(f"sP{ji % 3}", ms_in[ji]), ("cv", ms_th[wi])] + \
                  ([("cv", ms_cmb[ji - 2])] if ji >= 2 else [])
            ms_s1[ji] = E.emit("scalar",
                               lambda e, s1_s=s1_s, pin_s=pin_s, wi=wi: e.sign(
                                   s1_s, pin_s, bias=thc[:, 4 + wi:5 + wi]),
                               waits=w_s, incs=[("cs", 1)], ms=True)
            w_v = [(f"sP{ji % 3}", ms_in[ji]), ("cv", ms_th[wi])] + \
                  ([("cv", ms_cmb[ji - 2])] if ji >= 2 else [])
            ms_v2[ji] = E.emit("vector",
                               lambda e, v2_s=v2_s, pin_s=pin_s, wi=wi: e.tensor_scalar(
                                   v2_s, pin_s, thc[:, 4 + wi:5 + wi], 0.5,
                                   ALU.is_lt, ALU.subtract),
                               waits=w_v, incs=[("cv", 1)], ms=True)
            w_c = [("cs", ms_s1[ji])] + ([(f"sT{ji % 2}", ms_tp[ji - 2])] if ji >= 2 else [])
            ms_cmb[ji] = E.emit("vector",
                                lambda e, qn_s=qn_s, v2_s=v2_s, s1_s=s1_s:
                                e.scalar_tensor_tensor(qn_s, v2_s, -2.0, s1_s,
                                                       ALU.mult, ALU.add),
                                waits=w_c, incs=[("cv", 1)], ms=True)
            w_t = [("cv", ms_cmb[ji])] + ([(f"sS{ji % 2}", ms_st[ji - 2])] if ji >= 2 else [])
            nt = F // 128
            ms_tp[ji] = E.emit("scalar",
                               lambda e, qt_s=qt_s, qn_s=qn_s, nt=nt: e.dma_start_transpose(
                                   qt_s.rearrange("p (t q) -> p t q", t=nt), qn_s),
                               waits=w_t, incs=[(f"sT{ji % 2}", 16)], ms=True)
            src_st = qt_s.rearrange("p (t q) -> p t q", t=nt) if is3 else qt_s
            ms_st[ji] = E.emit("sync",
                               lambda e, dst=dst, src_st=src_st: e.dma_start(
                                   out=dst, in_=src_st),
                               waits=[(f"sT{ji % 2}", ms_tp[ji])], incs=[(f"sS{ji % 2}", 16)], ms=True)

        def emit_th(wi, ms_gdone):
            mtp_ = E.emit(
                "tensor",
                lambda e, wi=wi: e.matmul(p_th[0:1, wi:wi + 1], gacc[:, wi:wi + 1],
                                          onesc[:], start=True, stop=True),
                waits=[("cg", ms_gdone), ("sK", d_rows)], incs=[("ct", 1)], ms=True)
            mthr = E.emit(
                "vector",
                lambda e, wi=wi: e.tensor_scalar(thsml[0:1, wi:wi + 1],
                                                 p_th[0:1, wi:wi + 1],
                                                 0.5 / WELEM, 0.5e-8, ALU.mult, ALU.add),
                waits=[("ct", mtp_)], incs=[("cv", 1)], ms=True)
            mthb = E.emit(
                "tensor",
                lambda e, wi=wi: e.matmul(p_bc[:, wi:wi + 1], onesr[:],
                                          thsml[0:1, wi:wi + 1],
                                          start=True, stop=True),
                waits=[("cv", mthr)], incs=[("ct", 1)], ms=True)
            mcp = E.emit("vector",
                         lambda e, wi=wi: e.tensor_copy(thc[:, wi:wi + 1],
                                                        p_bc[:, wi:wi + 1]),
                         waits=[("ct", mthb)], incs=[("cv", 1)], ms=True)
            ms_th[wi] = E.emit(
                "vector",
                lambda e, wi=wi: e.tensor_scalar(thc[:, 4 + wi:5 + wi],
                                                 thc[:, wi:wi + 1],
                                                 -1.0, None, ALU.mult),
                waits=[("cv", mcp)], incs=[("cv", 1)], ms=True)

        emit_x_dma(0)
        emit_x_dma(1)
        ms_gch = [None] * (3 * NCHW)
        ms_ab = [None] * (3 * NCHW)
        ms_gadd = [None] * (3 * NCHW)
        att_next = 0
        job_next = 0

        def emit_gamma_chunk(wi, c, gi):
            w_g = [("cs", ms_ab[gi - 2])] if gi >= 2 else []
            ms_gch[gi] = E.emit(
                "gpsimd",
                lambda e, wi=wi, c=c, gi=gi: e.dma_start(
                    out=gb[:, (gi % 2) * CH:(gi % 2) * CH + CH],
                    in_=flatw[wi][c * 128 * CH:(c + 1) * 128 * CH]
                        .rearrange("(p n) -> p n", p=128)),
                waits=w_g, incs=[(f"sG{gi % 2}", 16)], ms=True)
            w_ab = [(f"sG{gi % 2}", ms_gch[gi])]
            if gi >= 2:
                w_ab.append(("cg", ms_gadd[gi - 2]))
            ms_ab[gi] = E.emit("scalar",
                               lambda e, gi=gi: e.activation(
                                   gjunk[:], gb[:, (gi % 2) * CH:(gi % 2) * CH + CH],
                                   AF.Abs, accum_out=gpart[:, gi % 2:gi % 2 + 1]),
                               waits=w_ab, incs=[("cs", 1)], ms=True)
            w_gr = [("cs", ms_ab[gi]), ("cg", ms_gz)]
            if gi >= 1:
                w_gr.append(("cg", ms_gadd[gi - 1]))
            ms_gadd[gi] = E.emit(
                "gpsimd",
                lambda e, wi=wi, gi=gi: e.tensor_add(
                    gacc[:, wi:wi + 1], gacc[:, wi:wi + 1],
                    gpart[:, gi % 2:gi % 2 + 1]),
                waits=w_gr, incs=[("cg", 1)], ms=True)

        for wi in range(3):
            for c in range(NCHW):
                gi = wi * NCHW + c
                emit_gamma_chunk(wi, c, gi)
                if gi % 4 == 3 and att_next < MT:
                    m = att_next
                    emit_attn_tile(m)
                    if m + 2 < MT:
                        emit_x_dma(m + 2)
                    att_next += 1
                if wi >= 1 and job_next < wi * HBK:
                    emit_quant_job(job_next)
                    job_next += 1
            emit_th(wi, ms_gadd[wi * NCHW + NCHW - 1])
        while att_next < MT:
            m = att_next
            emit_attn_tile(m)
            if m + 2 < MT:
                emit_x_dma(m + 2)
            att_next += 1
        while job_next < NJ:
            emit_quant_job(job_next)
            job_next += 1

        E.run_block(nc)
        return dict(E.n)


def _block3(nc, sems, totals_b2, q1t_d, q2t_d, q3t_d, h1t_d, h1n_d, out_d,
            dbgy2_d=None):
    _st = contextlib.ExitStack()
    h1Th = _st.enter_context(nc.sbuf_tensor("h1Th", [128, KB * HALF], BF16))
    y2acc = _st.enter_context(nc.sbuf_tensor("y2acc", [128, 8 * D], F32))
    hTg = _st.enter_context(nc.sbuf_tensor("hTg", [128, 2 * G * HALF], BF16))
    w3t = _st.enter_context(nc.sbuf_tensor("w3t", [128, 2 * G * D], BF16))
    w1t = _st.enter_context(nc.sbuf_tensor("w1t", [128, 2 * D], BF16))
    w2t = _st.enter_context(nc.sbuf_tensor("w2t", [128, 2 * D], BF16))
    sgb = _st.enter_context(nc.sbuf_tensor("sgb", [128, 2 * HALF], BF16))
    h1nb2 = _st.enter_context(nc.sbuf_tensor("h1nb2", [128, 2 * D], BF16))
    uU = _st.enter_context(nc.sbuf_tensor("uU", [128, D], F32))
    outb = _st.enter_context(nc.sbuf_tensor("outb", [128, 2 * D], F32))
    st2 = _st.enter_context(nc.sbuf_tensor("st2", [128, 8], F32))
    px1 = _st.enter_context(nc.psum_tensor("px1", [128, HALF], F32))
    px2 = _st.enter_context(nc.psum_tensor("px2", [128, HALF], F32))
    pyA = _st.enter_context(nc.psum_tensor("pyA", [128, HALF], F32))
    pyB = _st.enter_context(nc.psum_tensor("pyB", [128, HALF], F32))
    with _st:
        E = Emitter(sems, totals_b2)
        b1_dma_sems = (["sK"] + [f"sX{i}" for i in range(2)] +
                       [f"sG{i}" for i in range(2)] + [f"sP{i}" for i in range(3)] +
                       [f"sT{i}" for i in range(2)] + [f"sS{i}" for i in range(2)] +
                       [f"sHT{i}" for i in range(2)] + [f"sHS{i}" for i in range(2)] +
                       [f"sHN{i}" for i in range(2)])
        first_sync = [(nm, totals_b2[nm]) for nm in b1_dma_sems]
        sync_started = [False]

        def syncw(extra):
            if not sync_started[0]:
                sync_started[0] = True
                return list(first_sync) + [w for w in extra if w is not None]
            return [w for w in extra if w is not None]

        prev = {"p1": [None, None], "p2": [None, None], "f3": [None, None],
                "sig": None, "hT": None, "ev": [None, None], "ffn2_last": None}

        for hf in range(2):
            w_h = syncw([("ct", prev["ffn2_last"])] if prev["ffn2_last"] else [])
            ms_h1Th = E.emit(
                "sync",
                lambda e, hf=hf: e.dma_start(
                    out=h1Th[:].rearrange("p (k n) -> p k n", k=KB),
                    in_=h1t_d[hf].rearrange("(k p) n -> p k n", k=KB)),
                waits=w_h, incs=[("sH1", 16)], ms=True)

            ms_w1 = [None] * HBK
            ms_w2 = [None] * HBK
            ms_w3g = [None] * NG
            ms_p1 = [None] * HBK
            ms_p2 = [None] * HBK
            ms_sig = [None] * HBK
            ms_hT = [[None] * G for _ in range(NG)]
            ms_f3 = [None] * NG
            ms_ev = {}
            ms_y2m = [None] * 8

            def emit_w_load(hb):
                w_1 = [("ct", ms_p1[hb - 2] if hb >= 2 else prev["p1"][hb])]
                ms_w1[hb] = E.emit(
                    "sync",
                    lambda e, hb=hb: e.dma_start(
                        out=w1t[:, (hb % 2) * D:(hb % 2) * D + D],
                        in_=q1t_d[hb * 128:(hb + 1) * 128, :]),
                    waits=syncw(w_1 if (hb >= 2 or prev["p1"][hb] is not None) else []),
                    incs=[(f"sW1{hb % 2}", 16)], ms=True)
                w_2 = [("ct", ms_p2[hb - 2] if hb >= 2 else prev["p2"][hb])]
                ms_w2[hb] = E.emit(
                    "sync",
                    lambda e, hb=hb: e.dma_start(
                        out=w2t[:, (hb % 2) * D:(hb % 2) * D + D],
                        in_=q2t_d[hb * 128:(hb + 1) * 128, :]),
                    waits=w_2 if (hb >= 2 or prev["p2"][hb] is not None) else [],
                    incs=[(f"sW2{hb % 2}", 16)], ms=True)

            def emit_w3_load(g):
                w_3 = [("ct", ms_f3[g - 2] if g >= 2 else prev["f3"][g])]
                ms_w3g[g] = E.emit(
                    "sync",
                    lambda e, g=g: e.dma_start(
                        out=w3t[:, (g % 2) * G * D:(g % 2) * G * D + G * D]
                            .rearrange("p (t n) -> p t n", t=G),
                        in_=q3t_d[g * G * 128:(g + 1) * G * 128, :]
                            .rearrange("(t p) n -> p t n", p=128)),
                    waits=syncw(w_3 if (g >= 2 or prev["f3"][g] is not None) else []),
                    incs=[(f"sW3{g % 2}", 16)], ms=True)

            def emit_f12(g, j):
                hb = g * G + j
                wslab1 = w1t[:, (hb % 2) * D:(hb % 2) * D + D]
                wslab2 = w2t[:, (hb % 2) * D:(hb % 2) * D + D]
                w_mm = [(f"sW1{hb % 2}", ms_w1[hb]), ("sH1", ms_h1Th)]
                if hb >= 1:
                    w_mm.append(("cs", ms_sig[hb - 1]))
                elif prev["sig"] is not None:
                    w_mm.append(("cs", prev["sig"]))
                for n in range(2):
                    for k in range(KB):
                        def mk(e, n=n, k=k, wslab=wslab1):
                            return e.matmul(
                                px1[:, n * 512:(n + 1) * 512],
                                wslab[:, k * 128:(k + 1) * 128],
                                h1Th[:, k * HALF + n * 512: k * HALF + n * 512 + 512],
                                start=(k == 0), stop=(k == KB - 1))
                        if n == 0 and k == 0:
                            E.emit("tensor", mk, waits=w_mm)
                        elif n == 1 and k == KB - 1:
                            ms_p1[hb] = E.emit("tensor", mk, incs=[("ct", 1)], ms=True)
                        else:
                            E.emit("tensor", mk)
                w_mm2 = [(f"sW2{hb % 2}", ms_w2[hb])]
                if hb >= 1:
                    w_mm2.append(("cv", ms_hT[(hb - 1) // G][(hb - 1) % G]))
                elif prev["hT"] is not None:
                    w_mm2.append(("cv", prev["hT"]))
                for n in range(2):
                    for k in range(KB):
                        def mk2(e, n=n, k=k, wslab=wslab2):
                            return e.matmul(
                                px2[:, n * 512:(n + 1) * 512],
                                wslab[:, k * 128:(k + 1) * 128],
                                h1Th[:, k * HALF + n * 512: k * HALF + n * 512 + 512],
                                start=(k == 0), stop=(k == KB - 1))
                        if n == 0 and k == 0:
                            E.emit("tensor", mk2, waits=w_mm2)
                        elif n == 1 and k == KB - 1:
                            ms_p2[hb] = E.emit("tensor", mk2, incs=[("ct", 1)], ms=True)
                        else:
                            E.emit("tensor", mk2)
                ms_sig[hb] = E.emit(
                    "scalar",
                    lambda e, hb=hb: e.activation(
                        sgb[:, (hb % 2) * HALF:(hb % 2) * HALF + HALF],
                        px1[:], AF.Sigmoid, scale=0.5),
                    waits=[("ct", ms_p1[hb])], incs=[("cs", 1)], ms=True)
                w_ht = [("cs", ms_sig[hb]), ("ct", ms_p2[hb])]
                if g >= 2:
                    w_ht.append(("ct", ms_f3[g - 2]))
                elif prev["f3"][g] is not None:
                    w_ht.append(("ct", prev["f3"][g]))
                ms_hT[g][j] = E.emit(
                    "vector",
                    lambda e, g=g, j=j, hb=hb: e.tensor_mul(
                        hTg[:, (g % 2) * G * HALF + j * HALF:
                               (g % 2) * G * HALF + j * HALF + HALF],
                        sgb[:, (hb % 2) * HALF:(hb % 2) * HALF + HALF],
                        px2[:]),
                    waits=w_ht, incs=[("cv", 1)], ms=True)

            def emit_f3(g):
                unit = 0
                last = None
                for m in range(8):
                    for dh in range(2):
                        pp = pyA if unit % 2 == 0 else pyB
                        w_u = [("cv", ms_hT[g][G - 1]), (f"sW3{g % 2}", ms_w3g[g])]
                        if unit >= 2:
                            w_u.append(("cv", ms_ev[(g, unit - 2)]))
                        elif g >= 1:
                            w_u.append(("cv", ms_ev[(g - 1, 14 + unit)]))
                        elif prev["ev"][unit] is not None:
                            w_u.append(("cv", prev["ev"][unit]))
                        for kk in range(G):
                            for n in range(2):
                                def mk3(e, kk=kk, n=n, m=m, dh=dh, g=g, pp=pp):
                                    return e.matmul(
                                        pp[:, n * 512:(n + 1) * 512],
                                        hTg[:, (g % 2) * G * HALF + kk * HALF + m * 128:
                                               (g % 2) * G * HALF + kk * HALF + (m + 1) * 128],
                                        w3t[:, (g % 2) * G * D + kk * D + dh * HALF + n * 512:
                                               (g % 2) * G * D + kk * D + dh * HALF + n * 512 + 512],
                                        start=(kk == 0), stop=(kk == G - 1))
                                if kk == 0 and n == 0:
                                    E.emit("tensor", mk3, waits=w_u)
                                elif kk == G - 1 and n == 1:
                                    msu = E.emit("tensor", mk3, incs=[("ct", 1)], ms=True)
                                else:
                                    E.emit("tensor", mk3)
                        last = msu
                        ydst = y2acc[:, m * D + dh * HALF: m * D + dh * HALF + HALF]
                        if g == 0:
                            mse = E.emit("vector",
                                         lambda e, ydst=ydst, pp=pp: e.tensor_copy(
                                             ydst, pp[:]),
                                         waits=[("ct", msu)], incs=[("cv", 1)], ms=True)
                        else:
                            mse = E.emit("vector",
                                         lambda e, ydst=ydst, pp=pp: e.scalar_tensor_tensor(
                                             ydst, pp[:], 1.0, ydst, ALU.bypass, ALU.add),
                                         waits=[("ct", msu)], incs=[("cv", 1)], ms=True)
                        ms_ev[(g, unit)] = mse
                        if g == NG - 1:
                            ms_y2m[m] = mse
                        unit += 1
                ms_f3[g] = last

            emit_w_load(0)
            emit_w_load(1)
            emit_w3_load(0)
            emit_w3_load(1)
            for g in range(NG):
                for j in range(G):
                    hb = g * G + j
                    emit_f12(g, j)
                    if hb + 2 < HBK:
                        emit_w_load(hb + 2)
                if g >= 1:
                    emit_f3(g - 1)
                    if g + 1 < NG:
                        emit_w3_load(g + 1)
            emit_f3(NG - 1)

            prev = {"p1": [ms_p1[HBK - 2], ms_p1[HBK - 1]],
                    "p2": [ms_p2[HBK - 2], ms_p2[HBK - 1]],
                    "f3": [ms_f3[NG - 1], ms_f3[NG - 2]],
                    "sig": ms_sig[HBK - 1],
                    "hT": ms_hT[NG - 1][G - 1],
                    "ev": [ms_ev[(NG - 1, 14)], ms_ev[(NG - 1, 15)]],
                    "ffn2_last": ms_p2[HBK - 1]}

            ms_ln2free = [None] * 8
            ms_outd = [None] * 8
            for m in range(8):
                w_hn = [("cv", ms_ln2free[m - 2])] if m >= 2 else []
                ms_hnm = E.emit(
                    "sync",
                    lambda e, m=m, hf=hf: e.dma_start(
                        out=h1nb2[:, (m % 2) * D:(m % 2) * D + D],
                        in_=h1n_d[hf * HALF + m * 128: hf * HALF + (m + 1) * 128, :]),
                    waits=w_hn, incs=[(f"sHB{m % 2}", 16)], ms=True)
                if _DEBUG_Y2:
                    E.emit("sync",
                           lambda e, m=m, hf=hf: e.dma_start(
                               out=dbgy2_d[hf * HALF + m * 128: hf * HALF + (m + 1) * 128, :],
                               in_=y2acc[:, m * D:(m + 1) * D]),
                           waits=[("cv", ms_y2m[m])], incs=[("sDBG", 16)], ms=True)
                h1s = h1nb2[:, (m % 2) * D:(m % 2) * D + D]
                yv = y2acc[:, m * D:(m + 1) * D]
                ob = outb[:, (m % 2) * D:(m % 2) * D + D]
                E.emit("vector",
                       lambda e, h1s=h1s, yv=yv: e.scalar_tensor_tensor(
                           uU[:], h1s, 4.0, yv, ALU.mult, ALU.add,
                           accum_out=st2[:, 0:1]),
                       waits=[(f"sHB{m % 2}", ms_hnm), ("cv", ms_y2m[m])], incs=[("cv", 1)])
                w_sq = [(f"sO{m % 2}", ms_outd[m - 2])] if m >= 2 else []
                mq2 = E.emit("vector",
                       lambda e, ob=ob: e.scalar_tensor_tensor(
                           ob, uU[:], 1.0, uU[:], ALU.bypass, ALU.mult,
                           accum_out=st2[:, 1:2]),
                       waits=w_sq, incs=[("cv", 1)], ms=True)
                mq2 = E.emit("vector",
                       lambda e: e.tensor_scalar(st2[:, 2:3], st2[:, 0:1],
                                                 -1.0 / D, None, ALU.mult),
                       waits=[("cv", mq2)], incs=[("cv", 1)], ms=True)
                mq2 = E.emit("vector",
                       lambda e: e.tensor_scalar(st2[:, 3:4], st2[:, 1:2],
                                                 1.0 / D, EPS16, ALU.mult, ALU.add),
                       waits=[("cv", mq2)], incs=[("cv", 1)], ms=True)
                mq2 = E.emit("vector",
                       lambda e: e.tensor_scalar(st2[:, 4:5], st2[:, 2:3],
                                                 st2[:, 2:3], None, ALU.mult),
                       waits=[("cv", mq2)], incs=[("cv", 1)], ms=True)
                mq2 = E.emit("vector",
                       lambda e: e.tensor_sub(st2[:, 5:6], st2[:, 3:4], st2[:, 4:5]),
                       waits=[("cv", mq2)], incs=[("cv", 1)], ms=True)
                msv2 = E.emit("vector",
                              lambda e: e.tensor_scalar(st2[:, 5:6], st2[:, 5:6],
                                                        1e-30, None, ALU.max),
                              waits=[("cv", mq2)], incs=[("cv", 1)], ms=True)
                mss2 = E.emit("scalar",
                              lambda e: e.activation(st2[:, 6:7], st2[:, 5:6], AF.Sqrt),
                              waits=[("cv", msv2)], incs=[("cs", 1)], ms=True)
                msr2 = E.emit("vector", lambda e: e.reciprocal(st2[:, 7:8], st2[:, 6:7]),
                       waits=[("cs", mss2)], incs=[("cv", 1)], ms=True)
                ms_ln2free[m] = E.emit(
                    "vector",
                    lambda e, ob=ob: e.tensor_scalar(ob, uU[:], st2[:, 2:3],
                                                     st2[:, 7:8], ALU.add, ALU.mult),
                    waits=[("cv", msr2)], incs=[("cv", 1)], ms=True)
                ms_outd[m] = E.emit(
                    "sync",
                    lambda e, m=m, hf=hf, ob=ob: e.dma_start(
                        out=out_d[hf * HALF + m * 128: hf * HALF + (m + 1) * 128, :],
                        in_=ob),
                    waits=[("cv", ms_ln2free[m])], incs=[(f"sO{m % 2}", 16)], ms=True)

        all_dma = (b1_dma_sems + ["sH1", "sDBG"] +
                   [f"sW1{i}" for i in range(2)] + [f"sW2{i}" for i in range(2)] +
                   [f"sW3{i}" for i in range(2)] + [f"sHB{i}" for i in range(2)] +
                   [f"sO{i}" for i in range(2)])
        E.emit("sync", lambda e: e.nop(),
               waits=[(nm, E.n[nm]) for nm in all_dma])
        E.run_block(nc)
        return dict(E.n)


def kernel(L0_attn_x, L0_attn_weight, L0_attn_bias, L0_attn_decay_value,
           L0_ffn_w1, L0_ffn_w2, L0_ffn_w3):
    x = np.ascontiguousarray(np.asarray(L0_attn_x, dtype=np.float32)
                             .reshape(NCORE * TOK, D))
    wgt = np.asarray(L0_attn_weight, dtype=np.float32)
    bias = np.asarray(L0_attn_bias, dtype=np.float32)
    dv = np.asarray(L0_attn_decay_value, dtype=np.float32)
    w1 = np.ascontiguousarray(np.asarray(L0_ffn_w1, dtype=np.float32))
    w2 = np.ascontiguousarray(np.asarray(L0_ffn_w2, dtype=np.float32))
    w3 = np.ascontiguousarray(np.asarray(L0_ffn_w3, dtype=np.float32))

    w0b = np.ascontiguousarray(np.broadcast_to(wgt[0:1, :], (128, D)))
    w1b = np.ascontiguousarray(np.broadcast_to(wgt[1:2, :], (128, D)))
    biasb = np.ascontiguousarray(np.broadcast_to(bias.reshape(1, D), (128, D)))
    dvrep = np.ascontiguousarray(np.broadcast_to(dv.reshape(1, 2), (128, 2)))
    onesrow = np.ones((1, 128), np.float32)
    onescol = np.ones((128, 1), np.float32)
    epsh = np.full((1, 1), 0.5e-8, np.float32)

    if "prog" not in _NC_CACHE:
        _NC_CACHE["prog"] = build_program()
    nc = _NC_CACHE["prog"]

    in_maps = []
    for c in range(NCORE):
        in_maps.append({
            "x": x[c * TOK:(c + 1) * TOK],
            "w0b": w0b, "w1b": w1b, "biasb": biasb, "dvrep": dvrep,
            "onesrow": onesrow, "onescol": onescol, "epsh": epsh,
            "w1f": w1, "w2f": w2, "w3f": w3,
        })
    res = run_bass_kernel_spmd(nc, in_maps, core_ids=list(range(NCORE)))
    out = np.concatenate([np.asarray(r["out"]) for r in res.results], axis=0)
    return out.reshape(2, 8192, D)


if __name__ == "__main__":
    print("building program ...")
    build_program()
    print("built ok")


# version 05 (final)
# speedup vs baseline: 1.3377x; 1.3377x over previous
import contextlib
import numpy as np
import concourse.bass as bass
import concourse.mybir as mybir
from concourse.bass_utils import run_bass_kernel_spmd

F32 = mybir.dt.float32
BF16 = mybir.dt.bfloat16
AF = mybir.ActivationFunctionType
ALU = mybir.AluOpType
AX = mybir.AxisListType

NCORE = 8
TOK = 2048
D = 2048
KB = 16
H = 5632
HBK = 44
MT = 16
HALF = 1024
G = 4
NG = HBK // G
EPS = 1e-5
EPS16 = 16.0 * EPS
WELEM = float(H * D)

_NC_CACHE = {}
_DEBUG_Y2 = False


class Emitter:

    def __init__(self, sems, counts=None):
        self.q = {k: [] for k in ("sync", "scalar", "vector", "tensor", "gpsimd")}
        self.n = dict(counts) if counts else {name: 0 for name in sems}
        self.sems = sems

    def emit(self, eng, fn, waits=(), incs=(), ms=False):
        waits = [(s, v) for (s, v) in waits if v is not None and v > 0]
        incs = list(incs)
        after = None
        for s, k in incs:
            self.n[s] += k
            after = self.n[s]
        self.q[eng].append((fn, waits, incs))
        return after if ms else None

    def replay(self, eng_name, eng):
        for fn, waits, incs in self.q[eng_name]:
            for s, v in waits:
                eng.wait_ge(self.sems[s], v)
            ins = fn(eng)
            for s, k in incs:
                ins.then_inc(self.sems[s], k)

    def run_block(self, nc):
        with nc.Block() as block:
            @block.sync
            def _(e):
                self.replay("sync", e)

            @block.scalar
            def _(e):
                self.replay("scalar", e)

            @block.vector
            def _(e):
                self.replay("vector", e)

            @block.tensor
            def _(e):
                self.replay("tensor", e)

            @block.gpsimd
            def _(e):
                self.replay("gpsimd", e)


def build_program():
    nc = bass.Bass("TRN2", target_bir_lowering=False, debug=False)

    x_d = nc.dram_tensor("x", [TOK, D], F32, kind="ExternalInput").ap()
    w0b_d = nc.dram_tensor("w0b", [128, D], F32, kind="ExternalInput").ap()
    w1b_d = nc.dram_tensor("w1b", [128, D], F32, kind="ExternalInput").ap()
    biasb_d = nc.dram_tensor("biasb", [128, D], F32, kind="ExternalInput").ap()
    dvrep_d = nc.dram_tensor("dvrep", [128, 2], F32, kind="ExternalInput").ap()
    onesrow_d = nc.dram_tensor("onesrow", [1, 128], F32, kind="ExternalInput").ap()
    onescol_d = nc.dram_tensor("onescol", [128, 1], F32, kind="ExternalInput").ap()
    epsh_d = nc.dram_tensor("epsh", [1, 1], F32, kind="ExternalInput").ap()
    w1f_d = nc.dram_tensor("w1f", [H, D], F32, kind="ExternalInput").ap()
    w2f_d = nc.dram_tensor("w2f", [H, D], F32, kind="ExternalInput").ap()
    w3f_d = nc.dram_tensor("w3f", [D, H], F32, kind="ExternalInput").ap()
    out_d = nc.dram_tensor("out", [TOK, D], F32, kind="ExternalOutput").ap()
    q1t_d = nc.dram_tensor("q1t", [H, D], BF16, kind="Internal").ap()
    q2t_d = nc.dram_tensor("q2t", [H, D], BF16, kind="Internal").ap()
    q3t_d = nc.dram_tensor("q3t", [H, D], BF16, kind="Internal").ap()
    h1t_d = [nc.dram_tensor(f"h1t{h}", [KB * 128, HALF], BF16, kind="Internal").ap()
             for h in range(2)]
    h1n_d = nc.dram_tensor("h1n", [TOK, D], BF16, kind="Internal").ap()
    dbgy2_d = nc.dram_tensor("dbgy2", [TOK, D], F32,
                             kind="ExternalOutput" if _DEBUG_Y2 else "Internal").ap()

    sem_names = (["cv", "cs", "ct", "cg", "sK", "sDBG"] +
                 [f"sX{i}" for i in range(2)] + [f"sG{i}" for i in range(2)] +
                 [f"sP{i}" for i in range(3)] + [f"sT{i}" for i in range(2)] +
                 [f"sS{i}" for i in range(2)] + [f"sHT{i}" for i in range(2)] +
                 [f"sHS{i}" for i in range(2)] + [f"sHN{i}" for i in range(2)] +
                 ["sH1"] + [f"sW1{i}" for i in range(2)] +
                 [f"sW2{i}" for i in range(2)] + [f"sW3{i}" for i in range(2)] +
                 [f"sHB{i}" for i in range(2)] + [f"sO{i}" for i in range(2)])
    stack = contextlib.ExitStack()
    sems = {nm: stack.enter_context(nc.semaphore(nm)) for nm in sem_names}

    with stack:
        thc = stack.enter_context(nc.sbuf_tensor("thc", [128, 8], F32))
        onesr = stack.enter_context(nc.sbuf_tensor("onesr", [1, 128], F32))
        onesc = stack.enter_context(nc.sbuf_tensor("onesc", [128, 1], F32))
        epshs = stack.enter_context(nc.sbuf_tensor("epshs", [1, 4], F32))
        gacc = stack.enter_context(nc.sbuf_tensor("gacc", [128, 4], F32))
        thsml = stack.enter_context(nc.sbuf_tensor("thsml", [1, 4], F32))
        if True:
            totals_b1 = _block1(nc, sems, x_d, w0b_d, w1b_d, biasb_d, dvrep_d,
                                onesrow_d, onescol_d, epsh_d,
                                [w1f_d, w2f_d, w3f_d],
                                [q1t_d, q2t_d, q3t_d], h1t_d, h1n_d,
                                thc, onesr, onesc, epshs, gacc, thsml)
            _block3(nc, sems, totals_b1, q1t_d, q2t_d, q3t_d, h1t_d, h1n_d, out_d,
                    dbgy2_d)
    return nc


def _block1(nc, sems, x_d, w0b_d, w1b_d, biasb_d, dvrep_d, onesrow_d, onescol_d,
            epsh_d, wf_ds, qt_ds, h1t_d, h1n_d, thc, onesr, onesc, epshs, gacc, thsml):
    CH = 2048
    NCHW = (H * D) // (128 * CH)
    NG3 = 3 * NCHW
    _st = contextlib.ExitStack()
    w0bs = _st.enter_context(nc.sbuf_tensor("w0bs", [128, D], F32))
    w1bs = _st.enter_context(nc.sbuf_tensor("w1bs", [128, D], F32))
    biasbs = _st.enter_context(nc.sbuf_tensor("biasbs", [128, D], F32))
    dvreps = _st.enter_context(nc.sbuf_tensor("dvreps", [128, 2], F32))
    dvc = _st.enter_context(nc.sbuf_tensor("dvc", [128, 2], F32))
    dv0t = _st.enter_context(nc.sbuf_tensor("dv0t", [128, D], F32))
    dv1t = _st.enter_context(nc.sbuf_tensor("dv1t", [128, D], F32))
    xin = _st.enter_context(nc.sbuf_tensor("xin", [128, 2 * D], F32))
    sA = _st.enter_context(nc.sbuf_tensor("sA", [128, D], F32))
    sB = _st.enter_context(nc.sbuf_tensor("sB", [128, D], F32))
    sC = _st.enter_context(nc.sbuf_tensor("sC", [128, D], F32))
    sD = _st.enter_context(nc.sbuf_tensor("sD", [128, D], F32))
    ubf = _st.enter_context(nc.sbuf_tensor("ubf", [128, 3 * D], BF16))
    h1nb = _st.enter_context(nc.sbuf_tensor("h1nb", [128, 2 * D], BF16))
    tbuf = _st.enter_context(nc.sbuf_tensor("tbuf", [128, 2 * D], BF16))
    gb = _st.enter_context(nc.sbuf_tensor("gb", [128, 3 * CH], F32))
    gjunk = _st.enter_context(nc.sbuf_tensor("gjunk", [128, CH], BF16))
    gpart = _st.enter_context(nc.sbuf_tensor("gpart", [128, 2], F32))
    stats = _st.enter_context(nc.sbuf_tensor("stats", [128, 16], F32))
    pin = _st.enter_context(nc.sbuf_tensor("pin", [128, 3 * D], F32))
    s1b = _st.enter_context(nc.sbuf_tensor("s1b", [128, 2 * D], BF16))
    v2b = _st.enter_context(nc.sbuf_tensor("v2b", [128, 2 * D], BF16))
    qnb = _st.enter_context(nc.sbuf_tensor("qnb", [128, 2 * D], BF16))
    qtb = _st.enter_context(nc.sbuf_tensor("qtb", [128, 2 * D], BF16))
    p_th = _st.enter_context(nc.psum_tensor("p_th", [1, 4], F32))
    p_bc = _st.enter_context(nc.psum_tensor("p_bc", [128, 4], F32))
    with _st:
        E = Emitter(sems)

        d_rows = None
        for dst, src in ((w0bs[:], w0b_d), (w1bs[:], w1b_d), (biasbs[:], biasb_d),
                         (dvreps[:], dvrep_d), (onesr[:], onesrow_d),
                         (onesc[:], onescol_d), (epshs[0:1, 0:1], epsh_d)):
            d_rows = E.emit("scalar",
                            lambda e, dst=dst, src=src: e.dma_start(out=dst, in_=src),
                            incs=[("sK", 16)], ms=True)
        ms_dvc = E.emit("vector",
               lambda e: e.tensor_scalar(dvc[:], dvreps[:], 0.9, 1.0, ALU.max, ALU.min),
               waits=[("sK", d_rows)], incs=[("cv", 1)], ms=True)
        E.emit("vector",
               lambda e: e.tensor_scalar(dv0t[:], w0bs[:], 0.0, dvc[:, 0:1],
                                         ALU.mult, ALU.add),
               waits=[("cv", ms_dvc)], incs=[("cv", 1)])
        ms_dv = E.emit("vector",
                       lambda e: e.tensor_scalar(dv1t[:], w0bs[:], 0.0, dvc[:, 1:2],
                                                 ALU.mult, ALU.add),
                       incs=[("cv", 1)], ms=True)
        ms_gz = E.emit("gpsimd", lambda e: e.memset(gacc[:], 0.0),
                       incs=[("cg", 1)], ms=True)

        flatw = [w.rearrange("a b -> (a b)") for w in wf_ds]

        ms_x = [None] * MT
        ms_xfree = [None] * MT
        ms_norm = [None] * MT
        ms_h1free = [None] * MT
        ms_h1tp = [None] * MT
        ms_var = [None] * MT
        ms_sqrt = [None] * MT

        def emit_x_dma(m):
            if m >= MT:
                return
            waits = [("cv", ms_xfree[m - 2])] if m >= 2 else []
            ms_x[m] = E.emit(
                "scalar",
                lambda e, m=m: e.dma_start(
                    out=xin[:, (m % 2) * D:(m % 2) * D + D],
                    in_=x_d[m * 128:(m + 1) * 128, :]),
                waits=waits, incs=[(f"sX{m % 2}", 16)], ms=True)

        def attn_v1(m):
            xm = xin[:, (m % 2) * D:(m % 2) * D + D]
            ub = ubf[:, (m % 3) * D:(m % 3) * D + D]
            waits = [(f"sX{m % 2}", ms_x[m]), ("cv", ms_dv)]
            E.emit("vector", lambda e: e.tensor_mul(sA[:], xm, w0bs[:]),
                   waits=waits, incs=[("cv", 1)])
            E.emit("vector",
                   lambda e: e.tensor_tensor_scan(sB[:], dv0t[:], sA[:], 0.0,
                                                  ALU.mult, ALU.add),
                   incs=[("cv", 1)])
            E.emit("vector",
                   lambda e: e.tensor_tensor_scan(sC[:], dv1t[:], xm, 0.0,
                                                  ALU.mult, ALU.add),
                   incs=[("cv", 1)])
            E.emit("vector", lambda e: e.tensor_mul(sD[:], sC[:], w1bs[:]),
                   incs=[("cv", 1)])
            E.emit("vector", lambda e: e.tensor_add(sC[:], sB[:], sD[:]),
                   incs=[("cv", 1)])
            ms_xfree[m] = E.emit("vector",
                                 lambda e: e.tensor_add(sA[:], xm, biasbs[:]),
                                 incs=[("cv", 1)], ms=True)
            wu = [("cv", ms_norm[m - 3])] if m >= 3 else []
            E.emit("vector",
                   lambda e: e.scalar_tensor_tensor(ub, sC[:], 1.0, sA[:],
                                                    ALU.bypass, ALU.add,
                                                    accum_out=stats[:, 0:1]),
                   waits=wu, incs=[("cv", 1)])
            mq = E.emit("vector",
                   lambda e: e.scalar_tensor_tensor(sB[:], ub, 1.0, ub,
                                                    ALU.bypass, ALU.mult,
                                                    accum_out=stats[:, 1:2]),
                   incs=[("cv", 1)], ms=True)
            mq = E.emit("vector",
                   lambda e, m=m: e.tensor_scalar(stats[:, 8 + (m % 2) * 4:9 + (m % 2) * 4],
                                                  stats[:, 0:1],
                                             -1.0 / D, None, ALU.mult),
                   waits=[("cv", mq)], incs=[("cv", 1)], ms=True)
            mq = E.emit("vector",
                   lambda e: e.tensor_scalar(stats[:, 3:4], stats[:, 1:2],
                                             1.0 / D, EPS, ALU.mult, ALU.add),
                   waits=[("cv", mq)], incs=[("cv", 1)], ms=True)
            mq = E.emit("vector",
                   lambda e, m=m: e.tensor_scalar(stats[:, 4:5],
                                                  stats[:, 8 + (m % 2) * 4:9 + (m % 2) * 4],
                                                  stats[:, 8 + (m % 2) * 4:9 + (m % 2) * 4],
                                                  None, ALU.mult),
                   waits=[("cv", mq)], incs=[("cv", 1)], ms=True)
            mq = E.emit("vector",
                   lambda e: e.tensor_sub(stats[:, 5:6], stats[:, 3:4], stats[:, 4:5]),
                   waits=[("cv", mq)], incs=[("cv", 1)], ms=True)
            ms_var[m] = E.emit("vector",
                         lambda e, m=m: e.tensor_scalar(
                             stats[:, 10 + (m % 2) * 4:11 + (m % 2) * 4],
                             stats[:, 5:6], 1e-30, None, ALU.max),
                         waits=[("cv", mq)], incs=[("cv", 1)], ms=True)

        def attn_s1(m):
            ms_sqrt[m] = E.emit("scalar",
                         lambda e, m=m: e.activation(
                             stats[:, 11 + (m % 2) * 4:12 + (m % 2) * 4],
                             stats[:, 10 + (m % 2) * 4:11 + (m % 2) * 4], AF.Sqrt),
                         waits=[("cv", ms_var[m])], incs=[("cs", 1)], ms=True)

        def attn_v2(m):
            ub = ubf[:, (m % 3) * D:(m % 3) * D + D]
            h1b = h1nb[:, (m % 2) * D:(m % 2) * D + D]
            msr = E.emit("vector",
                   lambda e, m=m: e.reciprocal(
                       stats[:, 9 + (m % 2) * 4:10 + (m % 2) * 4],
                       stats[:, 11 + (m % 2) * 4:12 + (m % 2) * 4]),
                   waits=[("cs", ms_sqrt[m])], incs=[("cv", 1)], ms=True)
            wh = ([(f"sHN{m % 2}", ms_h1free[m - 2])] if m >= 2 else []) + [("cv", msr)]
            ms_norm[m] = E.emit("vector",
                                lambda e, m=m: e.tensor_scalar(
                                    h1b, ub,
                                    stats[:, 8 + (m % 2) * 4:9 + (m % 2) * 4],
                                    stats[:, 9 + (m % 2) * 4:10 + (m % 2) * 4],
                                    ALU.add, ALU.mult),
                                waits=wh, incs=[("cv", 1)], ms=True)

        def attn_s2(m):
            h1b = h1nb[:, (m % 2) * D:(m % 2) * D + D]
            w_tp = [("cv", ms_norm[m])] + \
                   ([(f"sHS{m % 2}", ms_h1tp[m - 2])] if m >= 2 else [])
            mtp = E.emit("scalar",
                   lambda e, m=m: e.dma_start_transpose(
                       tbuf[:, (m % 2) * D:(m % 2) * D + D]
                           .rearrange("p (t q) -> p t q", t=KB),
                       h1b),
                   waits=w_tp, incs=[(f"sHT{m % 2}", 16)], ms=True)
            ms_h1tp[m] = E.emit("scalar",
                   lambda e, m=m: e.dma_start(
                       out=h1t_d[m // 8][:, (m % 8) * 128:(m % 8) * 128 + 128]
                           .rearrange("(k p) q -> p k q", p=128),
                       in_=tbuf[:, (m % 2) * D:(m % 2) * D + D]
                           .rearrange("p (t q) -> p t q", t=KB)),
                   waits=[(f"sHT{m % 2}", mtp)], incs=[(f"sHS{m % 2}", 16)], ms=True)
            ms_h1free[m] = E.emit("scalar",
                                  lambda e, m=m: e.dma_start(
                                      out=h1n_d[m * 128:(m + 1) * 128, :], in_=h1b),
                                  incs=[(f"sHN{m % 2}", 16)], ms=True)

        jobs = []
        for hb in range(HBK):
            jobs.append((0, wf_ds[0][hb * 128:(hb + 1) * 128, :], D,
                         qt_ds[0][hb * 128:(hb + 1) * 128, :], False))
        for hb in range(HBK):
            jobs.append((1, wf_ds[1][hb * 128:(hb + 1) * 128, :], D,
                         qt_ds[1][hb * 128:(hb + 1) * 128, :], False))
        for cc, Fc in ((0, 2048), (1, 2048), (2, 1536)):
            for b in range(KB):
                jobs.append((2, wf_ds[2][b * 128:(b + 1) * 128, cc * 2048:cc * 2048 + Fc],
                             Fc,
                             qt_ds[2][cc * 2048:cc * 2048 + Fc, b * 128:(b + 1) * 128]
                             .rearrange("(t p) q -> p t q", p=128),
                             True))
        NJ = len(jobs)
        ms_in = [None] * NJ
        ms_s1 = [None] * NJ
        ms_v2 = [None] * NJ
        ms_cmb = [None] * NJ
        ms_tp = [None] * NJ
        ms_st = [None] * NJ
        ms_th = [None, None, None]

        def q_in(ji):
            if ji >= NJ:
                return
            wi, src, F, dst, is3 = jobs[ji]
            pin_s = pin[:, (ji % 3) * D:(ji % 3) * D + F]
            w_in = []
            if ji >= 3:
                w_in += [("cs", ms_s1[ji - 3]), ("cv", ms_v2[ji - 3])]
            ms_in[ji] = E.emit("sync",
                               lambda e, pin_s=pin_s, src=src: e.dma_start(
                                   out=pin_s, in_=src),
                               waits=w_in, incs=[(f"sP{ji % 3}", 16)], ms=True)

        def q_sgn(ji):
            wi, src, F, dst, is3 = jobs[ji]
            pin_s = pin[:, (ji % 3) * D:(ji % 3) * D + F]
            s1_s = s1b[:, (ji % 2) * D:(ji % 2) * D + F]
            w_s = [(f"sP{ji % 3}", ms_in[ji]), ("cv", ms_th[wi])] + \
                  ([("cv", ms_cmb[ji - 2])] if ji >= 2 else [])
            ms_s1[ji] = E.emit("scalar",
                               lambda e, s1_s=s1_s, pin_s=pin_s, wi=wi: e.sign(
                                   s1_s, pin_s, bias=thc[:, 4 + wi:5 + wi]),
                               waits=w_s, incs=[("cs", 1)], ms=True)

        def q_lt(ji):
            wi, src, F, dst, is3 = jobs[ji]
            pin_s = pin[:, (ji % 3) * D:(ji % 3) * D + F]
            v2_s = v2b[:, (ji % 2) * D:(ji % 2) * D + F]
            w_v = [(f"sP{ji % 3}", ms_in[ji]), ("cv", ms_th[wi])] + \
                  ([("cv", ms_cmb[ji - 2])] if ji >= 2 else [])
            ms_v2[ji] = E.emit("vector",
                               lambda e, v2_s=v2_s, pin_s=pin_s, wi=wi: e.tensor_scalar(
                                   v2_s, pin_s, thc[:, 4 + wi:5 + wi], 0.5,
                                   ALU.is_lt, ALU.subtract),
                               waits=w_v, incs=[("cv", 1)], ms=True)

        def q_cmb(ji):
            if ji < 0 or ji >= NJ:
                return
            wi, src, F, dst, is3 = jobs[ji]
            s1_s = s1b[:, (ji % 2) * D:(ji % 2) * D + F]
            v2_s = v2b[:, (ji % 2) * D:(ji % 2) * D + F]
            qn_s = qnb[:, (ji % 2) * D:(ji % 2) * D + F]
            w_c = [("cs", ms_s1[ji])] + ([(f"sT{ji % 2}", ms_tp[ji - 2])] if ji >= 2 else [])
            ms_cmb[ji] = E.emit("vector",
                                lambda e, qn_s=qn_s, v2_s=v2_s, s1_s=s1_s:
                                e.scalar_tensor_tensor(qn_s, v2_s, -2.0, s1_s,
                                                       ALU.mult, ALU.add),
                                waits=w_c, incs=[("cv", 1)], ms=True)

        def q_tp(ji):
            if ji < 0 or ji >= NJ:
                return
            wi, src, F, dst, is3 = jobs[ji]
            qn_s = qnb[:, (ji % 2) * D:(ji % 2) * D + F]
            qt_s = qtb[:, (ji % 2) * D:(ji % 2) * D + F]
            w_t = [("cv", ms_cmb[ji])] + ([(f"sS{ji % 2}", ms_st[ji - 2])] if ji >= 2 else [])
            nt = F // 128
            ms_tp[ji] = E.emit("scalar",
                               lambda e, qt_s=qt_s, qn_s=qn_s, nt=nt: e.dma_start_transpose(
                                   qt_s.rearrange("p (t q) -> p t q", t=nt), qn_s),
                               waits=w_t, incs=[(f"sT{ji % 2}", 16)], ms=True)

        def q_st(ji):
            if ji < 0 or ji >= NJ:
                return
            wi, src, F, dst, is3 = jobs[ji]
            qt_s = qtb[:, (ji % 2) * D:(ji % 2) * D + F]
            nt = F // 128
            src_st = qt_s.rearrange("p (t q) -> p t q", t=nt) if is3 else qt_s
            ms_st[ji] = E.emit("sync",
                               lambda e, dst=dst, src_st=src_st: e.dma_start(
                                   out=dst, in_=src_st),
                               waits=[(f"sT{ji % 2}", ms_tp[ji])],
                               incs=[(f"sS{ji % 2}", 16)], ms=True)

        def q_tick(ji):
            if ji == 0:
                q_in(0)
                q_in(1)
            q_in(ji + 2)
            q_st(ji - 2)
            q_sgn(ji)
            q_lt(ji)
            q_cmb(ji - 1)
            q_tp(ji - 1)

        ms_gch = [None] * NG3
        ms_ab = [None] * NG3
        ms_gadd = [None] * NG3

        def g_dma(gi):
            if gi >= NG3:
                return
            wi, c = gi // NCHW, gi % NCHW
            w_g = [("cs", ms_ab[gi - 3])] if gi >= 3 else []
            ms_gch[gi] = E.emit(
                "gpsimd",
                lambda e, wi=wi, c=c, gi=gi: e.dma_start(
                    out=gb[:, (gi % 3) * CH:(gi % 3) * CH + CH],
                    in_=flatw[wi][c * 128 * CH:(c + 1) * 128 * CH]
                        .rearrange("(p n) -> p n", p=128)),
                waits=w_g, incs=[(f"sG{gi % 2}", 16)], ms=True)

        def g_abs(gi):
            w_ab = [(f"sG{gi % 2}", ms_gch[gi])]
            if gi >= 2:
                w_ab.append(("cg", ms_gadd[gi - 2]))
            ms_ab[gi] = E.emit("scalar",
                               lambda e, gi=gi: e.activation(
                                   gjunk[:], gb[:, (gi % 3) * CH:(gi % 3) * CH + CH],
                                   AF.Abs, accum_out=gpart[:, gi % 2:gi % 2 + 1]),
                               waits=w_ab, incs=[("cs", 1)], ms=True)

        def g_add(gi):
            if gi < 0:
                return
            wi = gi // NCHW
            w_gr = [("cs", ms_ab[gi]), ("cg", ms_gz)]
            if gi >= 1:
                w_gr.append(("cg", ms_gadd[gi - 1]))
            ms_gadd[gi] = E.emit(
                "gpsimd",
                lambda e, wi=wi, gi=gi: e.tensor_add(
                    gacc[:, wi:wi + 1], gacc[:, wi:wi + 1],
                    gpart[:, gi % 2:gi % 2 + 1]),
                waits=w_gr, incs=[("cg", 1)], ms=True)

        def emit_th(wi, ms_gdone):
            mtp_ = E.emit(
                "tensor",
                lambda e, wi=wi: e.matmul(p_th[0:1, wi:wi + 1], gacc[:, wi:wi + 1],
                                          onesc[:], start=True, stop=True),
                waits=[("cg", ms_gdone), ("sK", d_rows)], incs=[("ct", 1)], ms=True)
            mthr = E.emit(
                "vector",
                lambda e, wi=wi: e.tensor_scalar(thsml[0:1, wi:wi + 1],
                                                 p_th[0:1, wi:wi + 1],
                                                 0.5 / WELEM, 0.5e-8, ALU.mult, ALU.add),
                waits=[("ct", mtp_)], incs=[("cv", 1)], ms=True)
            mthb = E.emit(
                "tensor",
                lambda e, wi=wi: e.matmul(p_bc[:, wi:wi + 1], onesr[:],
                                          thsml[0:1, wi:wi + 1],
                                          start=True, stop=True),
                waits=[("cv", mthr)], incs=[("ct", 1)], ms=True)
            mcp = E.emit("vector",
                         lambda e, wi=wi: e.tensor_copy(thc[:, wi:wi + 1],
                                                        p_bc[:, wi:wi + 1]),
                         waits=[("ct", mthb)], incs=[("cv", 1)], ms=True)
            ms_th[wi] = E.emit(
                "vector",
                lambda e, wi=wi: e.tensor_scalar(thc[:, 4 + wi:5 + wi],
                                                 thc[:, wi:wi + 1],
                                                 -1.0, None, ALU.mult),
                waits=[("cv", mcp)], incs=[("cv", 1)], ms=True)

        emit_x_dma(0)
        emit_x_dma(1)
        g_dma(0)
        g_dma(1)
        g_dma(2)
        att_v1_next = 0
        att_fin_next = 0
        job_next = 0

        def attn_tick():
            nonlocal att_v1_next, att_fin_next
            if att_fin_next < att_v1_next and att_fin_next < MT:
                m = att_fin_next
                attn_s1(m)
                attn_v2(m)
                attn_s2(m)
                att_fin_next += 1
            if att_v1_next < MT:
                m = att_v1_next
                attn_v1(m)
                emit_x_dma(m + 2)
                att_v1_next += 1

        for gi in range(NG3):
            wi = gi // NCHW
            g_abs(gi)
            g_dma(gi + 3)
            g_add(gi)
            if gi % 4 == 3:
                attn_tick()
            if wi >= 1 and job_next < wi * HBK:
                q_tick(job_next)
                job_next += 1
            if gi % NCHW == NCHW - 1:
                emit_th(wi, ms_gadd[gi])
        while att_fin_next < MT:
            attn_tick()
        while job_next < NJ:
            q_tick(job_next)
            job_next += 1
        q_cmb(NJ - 1)
        q_tp(NJ - 1)
        q_st(NJ - 2)
        q_st(NJ - 1)

        E.run_block(nc)
        return dict(E.n)


def _block3(nc, sems, totals_b2, q1t_d, q2t_d, q3t_d, h1t_d, h1n_d, out_d,
            dbgy2_d=None):
    _st = contextlib.ExitStack()
    h1Th = _st.enter_context(nc.sbuf_tensor("h1Th", [128, KB * HALF], BF16))
    y2acc = _st.enter_context(nc.sbuf_tensor("y2acc", [128, 8 * D], F32))
    hTg = _st.enter_context(nc.sbuf_tensor("hTg", [128, 2 * G * HALF], BF16))
    w3t = _st.enter_context(nc.sbuf_tensor("w3t", [128, 2 * G * D], BF16))
    w1t = _st.enter_context(nc.sbuf_tensor("w1t", [128, 2 * D], BF16))
    w2t = _st.enter_context(nc.sbuf_tensor("w2t", [128, 2 * D], BF16))
    sgb = _st.enter_context(nc.sbuf_tensor("sgb", [128, 2 * HALF], BF16))
    h1nb2 = _st.enter_context(nc.sbuf_tensor("h1nb2", [128, 2 * D], BF16))
    uU = _st.enter_context(nc.sbuf_tensor("uU", [128, D], F32))
    outb = _st.enter_context(nc.sbuf_tensor("outb", [128, 2 * D], F32))
    st2 = _st.enter_context(nc.sbuf_tensor("st2", [128, 8], F32))
    px1 = _st.enter_context(nc.psum_tensor("px1", [128, HALF], F32))
    px2 = _st.enter_context(nc.psum_tensor("px2", [128, HALF], F32))
    pyA = _st.enter_context(nc.psum_tensor("pyA", [128, HALF], F32))
    pyB = _st.enter_context(nc.psum_tensor("pyB", [128, HALF], F32))
    with _st:
        E = Emitter(sems, totals_b2)
        b1_dma_sems = (["sK"] + [f"sX{i}" for i in range(2)] +
                       [f"sG{i}" for i in range(2)] + [f"sP{i}" for i in range(3)] +
                       [f"sT{i}" for i in range(2)] + [f"sS{i}" for i in range(2)] +
                       [f"sHT{i}" for i in range(2)] + [f"sHS{i}" for i in range(2)] +
                       [f"sHN{i}" for i in range(2)])
        first_sync = [(nm, totals_b2[nm]) for nm in b1_dma_sems]
        sync_started = [False]

        def syncw(extra):
            if not sync_started[0]:
                sync_started[0] = True
                return list(first_sync) + [w for w in extra if w is not None]
            return [w for w in extra if w is not None]

        prev = {"p1": [None, None], "p2": [None, None], "f3": [None, None],
                "sig": None, "hT": None, "ev": [None, None], "ffn2_last": None}

        for hf in range(2):
            w_h = syncw([("ct", prev["ffn2_last"])] if prev["ffn2_last"] else [])
            ms_h1Th = E.emit(
                "sync",
                lambda e, hf=hf: e.dma_start(
                    out=h1Th[:].rearrange("p (k n) -> p k n", k=KB),
                    in_=h1t_d[hf].rearrange("(k p) n -> p k n", k=KB)),
                waits=w_h, incs=[("sH1", 16)], ms=True)

            ms_w1 = [None] * HBK
            ms_w2 = [None] * HBK
            ms_w3g = [None] * NG
            ms_p1 = [None] * HBK
            ms_p2 = [None] * HBK
            ms_sig = [None] * HBK
            ms_hT = [[None] * G for _ in range(NG)]
            ms_f3 = [None] * NG
            ms_ev = {}
            ms_y2m = [None] * 8

            def emit_w_load(hb):
                w_1 = [("ct", ms_p1[hb - 2] if hb >= 2 else prev["p1"][hb])]
                ms_w1[hb] = E.emit(
                    "sync",
                    lambda e, hb=hb: e.dma_start(
                        out=w1t[:, (hb % 2) * D:(hb % 2) * D + D],
                        in_=q1t_d[hb * 128:(hb + 1) * 128, :]),
                    waits=syncw(w_1 if (hb >= 2 or prev["p1"][hb] is not None) else []),
                    incs=[(f"sW1{hb % 2}", 16)], ms=True)
                w_2 = [("ct", ms_p2[hb - 2] if hb >= 2 else prev["p2"][hb])]
                ms_w2[hb] = E.emit(
                    "sync",
                    lambda e, hb=hb: e.dma_start(
                        out=w2t[:, (hb % 2) * D:(hb % 2) * D + D],
                        in_=q2t_d[hb * 128:(hb + 1) * 128, :]),
                    waits=w_2 if (hb >= 2 or prev["p2"][hb] is not None) else [],
                    incs=[(f"sW2{hb % 2}", 16)], ms=True)

            def emit_w3_load(g):
                w_3 = [("ct", ms_f3[g - 2] if g >= 2 else prev["f3"][g])]
                ms_w3g[g] = E.emit(
                    "sync",
                    lambda e, g=g: e.dma_start(
                        out=w3t[:, (g % 2) * G * D:(g % 2) * G * D + G * D]
                            .rearrange("p (t n) -> p t n", t=G),
                        in_=q3t_d[g * G * 128:(g + 1) * G * 128, :]
                            .rearrange("(t p) n -> p t n", p=128)),
                    waits=syncw(w_3 if (g >= 2 or prev["f3"][g] is not None) else []),
                    incs=[(f"sW3{g % 2}", 16)], ms=True)

            def emit_f12(g, j):
                hb = g * G + j
                wslab1 = w1t[:, (hb % 2) * D:(hb % 2) * D + D]
                wslab2 = w2t[:, (hb % 2) * D:(hb % 2) * D + D]
                w_mm = [(f"sW1{hb % 2}", ms_w1[hb]), ("sH1", ms_h1Th)]
                if hb >= 1:
                    w_mm.append(("cs", ms_sig[hb - 1]))
                elif prev["sig"] is not None:
                    w_mm.append(("cs", prev["sig"]))
                for n in range(2):
                    for k in range(KB):
                        def mk(e, n=n, k=k, wslab=wslab1):
                            return e.matmul(
                                px1[:, n * 512:(n + 1) * 512],
                                wslab[:, k * 128:(k + 1) * 128],
                                h1Th[:, k * HALF + n * 512: k * HALF + n * 512 + 512],
                                start=(k == 0), stop=(k == KB - 1))
                        if n == 0 and k == 0:
                            E.emit("tensor", mk, waits=w_mm)
                        elif n == 1 and k == KB - 1:
                            ms_p1[hb] = E.emit("tensor", mk, incs=[("ct", 1)], ms=True)
                        else:
                            E.emit("tensor", mk)
                w_mm2 = [(f"sW2{hb % 2}", ms_w2[hb])]
                if hb >= 1:
                    w_mm2.append(("cv", ms_hT[(hb - 1) // G][(hb - 1) % G]))
                elif prev["hT"] is not None:
                    w_mm2.append(("cv", prev["hT"]))
                for n in range(2):
                    for k in range(KB):
                        def mk2(e, n=n, k=k, wslab=wslab2):
                            return e.matmul(
                                px2[:, n * 512:(n + 1) * 512],
                                wslab[:, k * 128:(k + 1) * 128],
                                h1Th[:, k * HALF + n * 512: k * HALF + n * 512 + 512],
                                start=(k == 0), stop=(k == KB - 1))
                        if n == 0 and k == 0:
                            E.emit("tensor", mk2, waits=w_mm2)
                        elif n == 1 and k == KB - 1:
                            ms_p2[hb] = E.emit("tensor", mk2, incs=[("ct", 1)], ms=True)
                        else:
                            E.emit("tensor", mk2)
                ms_sig[hb] = E.emit(
                    "scalar",
                    lambda e, hb=hb: e.activation(
                        sgb[:, (hb % 2) * HALF:(hb % 2) * HALF + HALF],
                        px1[:], AF.Sigmoid, scale=0.5),
                    waits=[("ct", ms_p1[hb])], incs=[("cs", 1)], ms=True)
                w_ht = [("cs", ms_sig[hb]), ("ct", ms_p2[hb])]
                if g >= 2:
                    w_ht.append(("ct", ms_f3[g - 2]))
                elif prev["f3"][g] is not None:
                    w_ht.append(("ct", prev["f3"][g]))
                ms_hT[g][j] = E.emit(
                    "vector",
                    lambda e, g=g, j=j, hb=hb: e.tensor_mul(
                        hTg[:, (g % 2) * G * HALF + j * HALF:
                               (g % 2) * G * HALF + j * HALF + HALF],
                        sgb[:, (hb % 2) * HALF:(hb % 2) * HALF + HALF],
                        px2[:]),
                    waits=w_ht, incs=[("cv", 1)], ms=True)

            def emit_f3(g):
                unit = 0
                last = None
                for m in range(8):
                    for dh in range(2):
                        pp = pyA if unit % 2 == 0 else pyB
                        w_u = [("cv", ms_hT[g][G - 1]), (f"sW3{g % 2}", ms_w3g[g])]
                        if unit >= 2:
                            w_u.append(("cv", ms_ev[(g, unit - 2)]))
                        elif g >= 1:
                            w_u.append(("cv", ms_ev[(g - 1, 14 + unit)]))
                        elif prev["ev"][unit] is not None:
                            w_u.append(("cv", prev["ev"][unit]))
                        for kk in range(G):
                            for n in range(2):
                                def mk3(e, kk=kk, n=n, m=m, dh=dh, g=g, pp=pp):
                                    return e.matmul(
                                        pp[:, n * 512:(n + 1) * 512],
                                        hTg[:, (g % 2) * G * HALF + kk * HALF + m * 128:
                                               (g % 2) * G * HALF + kk * HALF + (m + 1) * 128],
                                        w3t[:, (g % 2) * G * D + kk * D + dh * HALF + n * 512:
                                               (g % 2) * G * D + kk * D + dh * HALF + n * 512 + 512],
                                        start=(kk == 0), stop=(kk == G - 1))
                                if kk == 0 and n == 0:
                                    E.emit("tensor", mk3, waits=w_u)
                                elif kk == G - 1 and n == 1:
                                    msu = E.emit("tensor", mk3, incs=[("ct", 1)], ms=True)
                                else:
                                    E.emit("tensor", mk3)
                        last = msu
                        ydst = y2acc[:, m * D + dh * HALF: m * D + dh * HALF + HALF]
                        if g == 0:
                            mse = E.emit("vector",
                                         lambda e, ydst=ydst, pp=pp: e.tensor_copy(
                                             ydst, pp[:]),
                                         waits=[("ct", msu)], incs=[("cv", 1)], ms=True)
                        else:
                            mse = E.emit("vector",
                                         lambda e, ydst=ydst, pp=pp: e.scalar_tensor_tensor(
                                             ydst, pp[:], 1.0, ydst, ALU.bypass, ALU.add),
                                         waits=[("ct", msu)], incs=[("cv", 1)], ms=True)
                        ms_ev[(g, unit)] = mse
                        if g == NG - 1:
                            ms_y2m[m] = mse
                        unit += 1
                ms_f3[g] = last

            emit_w_load(0)
            emit_w_load(1)
            emit_w3_load(0)
            emit_w3_load(1)
            for g in range(NG):
                for j in range(G):
                    hb = g * G + j
                    emit_f12(g, j)
                    if hb + 2 < HBK:
                        emit_w_load(hb + 2)
                if g >= 1:
                    emit_f3(g - 1)
                    if g + 1 < NG:
                        emit_w3_load(g + 1)
            emit_f3(NG - 1)

            prev = {"p1": [ms_p1[HBK - 2], ms_p1[HBK - 1]],
                    "p2": [ms_p2[HBK - 2], ms_p2[HBK - 1]],
                    "f3": [ms_f3[NG - 1], ms_f3[NG - 2]],
                    "sig": ms_sig[HBK - 1],
                    "hT": ms_hT[NG - 1][G - 1],
                    "ev": [ms_ev[(NG - 1, 14)], ms_ev[(NG - 1, 15)]],
                    "ffn2_last": ms_p2[HBK - 1]}

            ms_ln2free = [None] * 8
            ms_outd = [None] * 8
            for m in range(8):
                w_hn = [("cv", ms_ln2free[m - 2])] if m >= 2 else []
                ms_hnm = E.emit(
                    "sync",
                    lambda e, m=m, hf=hf: e.dma_start(
                        out=h1nb2[:, (m % 2) * D:(m % 2) * D + D],
                        in_=h1n_d[hf * HALF + m * 128: hf * HALF + (m + 1) * 128, :]),
                    waits=w_hn, incs=[(f"sHB{m % 2}", 16)], ms=True)
                if _DEBUG_Y2:
                    E.emit("sync",
                           lambda e, m=m, hf=hf: e.dma_start(
                               out=dbgy2_d[hf * HALF + m * 128: hf * HALF + (m + 1) * 128, :],
                               in_=y2acc[:, m * D:(m + 1) * D]),
                           waits=[("cv", ms_y2m[m])], incs=[("sDBG", 16)], ms=True)
                h1s = h1nb2[:, (m % 2) * D:(m % 2) * D + D]
                yv = y2acc[:, m * D:(m + 1) * D]
                ob = outb[:, (m % 2) * D:(m % 2) * D + D]
                E.emit("vector",
                       lambda e, h1s=h1s, yv=yv: e.scalar_tensor_tensor(
                           uU[:], h1s, 4.0, yv, ALU.mult, ALU.add,
                           accum_out=st2[:, 0:1]),
                       waits=[(f"sHB{m % 2}", ms_hnm), ("cv", ms_y2m[m])], incs=[("cv", 1)])
                w_sq = [(f"sO{m % 2}", ms_outd[m - 2])] if m >= 2 else []
                mq2 = E.emit("vector",
                       lambda e, ob=ob: e.scalar_tensor_tensor(
                           ob, uU[:], 1.0, uU[:], ALU.bypass, ALU.mult,
                           accum_out=st2[:, 1:2]),
                       waits=w_sq, incs=[("cv", 1)], ms=True)
                mq2 = E.emit("vector",
                       lambda e: e.tensor_scalar(st2[:, 2:3], st2[:, 0:1],
                                                 -1.0 / D, None, ALU.mult),
                       waits=[("cv", mq2)], incs=[("cv", 1)], ms=True)
                mq2 = E.emit("vector",
                       lambda e: e.tensor_scalar(st2[:, 3:4], st2[:, 1:2],
                                                 1.0 / D, EPS16, ALU.mult, ALU.add),
                       waits=[("cv", mq2)], incs=[("cv", 1)], ms=True)
                mq2 = E.emit("vector",
                       lambda e: e.tensor_scalar(st2[:, 4:5], st2[:, 2:3],
                                                 st2[:, 2:3], None, ALU.mult),
                       waits=[("cv", mq2)], incs=[("cv", 1)], ms=True)
                mq2 = E.emit("vector",
                       lambda e: e.tensor_sub(st2[:, 5:6], st2[:, 3:4], st2[:, 4:5]),
                       waits=[("cv", mq2)], incs=[("cv", 1)], ms=True)
                msv2 = E.emit("vector",
                              lambda e: e.tensor_scalar(st2[:, 5:6], st2[:, 5:6],
                                                        1e-30, None, ALU.max),
                              waits=[("cv", mq2)], incs=[("cv", 1)], ms=True)
                mss2 = E.emit("scalar",
                              lambda e: e.activation(st2[:, 6:7], st2[:, 5:6], AF.Sqrt),
                              waits=[("cv", msv2)], incs=[("cs", 1)], ms=True)
                msr2 = E.emit("vector", lambda e: e.reciprocal(st2[:, 7:8], st2[:, 6:7]),
                       waits=[("cs", mss2)], incs=[("cv", 1)], ms=True)
                ms_ln2free[m] = E.emit(
                    "vector",
                    lambda e, ob=ob: e.tensor_scalar(ob, uU[:], st2[:, 2:3],
                                                     st2[:, 7:8], ALU.add, ALU.mult),
                    waits=[("cv", msr2)], incs=[("cv", 1)], ms=True)
                ms_outd[m] = E.emit(
                    "sync",
                    lambda e, m=m, hf=hf, ob=ob: e.dma_start(
                        out=out_d[hf * HALF + m * 128: hf * HALF + (m + 1) * 128, :],
                        in_=ob),
                    waits=[("cv", ms_ln2free[m])], incs=[(f"sO{m % 2}", 16)], ms=True)

        all_dma = (b1_dma_sems + ["sH1", "sDBG"] +
                   [f"sW1{i}" for i in range(2)] + [f"sW2{i}" for i in range(2)] +
                   [f"sW3{i}" for i in range(2)] + [f"sHB{i}" for i in range(2)] +
                   [f"sO{i}" for i in range(2)])
        E.emit("sync", lambda e: e.nop(),
               waits=[(nm, E.n[nm]) for nm in all_dma])
        E.run_block(nc)
        return dict(E.n)


def kernel(L0_attn_x, L0_attn_weight, L0_attn_bias, L0_attn_decay_value,
           L0_ffn_w1, L0_ffn_w2, L0_ffn_w3):
    x = np.ascontiguousarray(np.asarray(L0_attn_x, dtype=np.float32)
                             .reshape(NCORE * TOK, D))
    wgt = np.asarray(L0_attn_weight, dtype=np.float32)
    bias = np.asarray(L0_attn_bias, dtype=np.float32)
    dv = np.asarray(L0_attn_decay_value, dtype=np.float32)
    w1 = np.ascontiguousarray(np.asarray(L0_ffn_w1, dtype=np.float32))
    w2 = np.ascontiguousarray(np.asarray(L0_ffn_w2, dtype=np.float32))
    w3 = np.ascontiguousarray(np.asarray(L0_ffn_w3, dtype=np.float32))

    w0b = np.ascontiguousarray(np.broadcast_to(wgt[0:1, :], (128, D)))
    w1b = np.ascontiguousarray(np.broadcast_to(wgt[1:2, :], (128, D)))
    biasb = np.ascontiguousarray(np.broadcast_to(bias.reshape(1, D), (128, D)))
    dvrep = np.ascontiguousarray(np.broadcast_to(dv.reshape(1, 2), (128, 2)))
    onesrow = np.ones((1, 128), np.float32)
    onescol = np.ones((128, 1), np.float32)
    epsh = np.full((1, 1), 0.5e-8, np.float32)

    if "prog" not in _NC_CACHE:
        _NC_CACHE["prog"] = build_program()
    nc = _NC_CACHE["prog"]

    in_maps = []
    for c in range(NCORE):
        in_maps.append({
            "x": x[c * TOK:(c + 1) * TOK],
            "w0b": w0b, "w1b": w1b, "biasb": biasb, "dvrep": dvrep,
            "onesrow": onesrow, "onescol": onescol, "epsh": epsh,
            "w1f": w1, "w2f": w2, "w3f": w3,
        })
    res = run_bass_kernel_spmd(nc, in_maps, core_ids=list(range(NCORE)))
    out = np.concatenate([np.asarray(r["out"]) for r in res.results], axis=0)
    return out.reshape(2, 8192, D)


if __name__ == "__main__":
    print("building program ...")
    build_program()
    print("built ok")


# version 06
# speedup vs baseline: 1.3391x; 1.0010x over previous
import contextlib
import numpy as np
import concourse.bass as bass
import concourse.mybir as mybir
from concourse.bass_utils import run_bass_kernel_spmd

F32 = mybir.dt.float32
BF16 = mybir.dt.bfloat16
AF = mybir.ActivationFunctionType
ALU = mybir.AluOpType

NCORE = 8
TOK = 2048
D = 2048
KB = 16
H = 5632
HBK = 44
MT = 16
QT = 512
NQ = 4
G3 = 11
NG3 = HBK // G3
EPS = 1e-5
EPS16 = 16.0 * EPS
WELEM = float(H * D)
SLICE = H * D // NCORE
GCH = 1408
NGC = SLICE // (128 * GCH)

_NC_CACHE = {}
_DEBUG_Y2 = False


class Emitter:
    def __init__(self, sems, counts=None):
        self.q = {k: [] for k in ("sync", "scalar", "vector", "tensor", "gpsimd")}
        self.n = dict(counts) if counts else {name: 0 for name in sems}
        self.sems = sems

    def emit(self, eng, fn, waits=(), incs=(), ms=False):
        waits = [(s, v) for (s, v) in waits if v is not None and v > 0]
        incs = list(incs)
        after = None
        for s, k in incs:
            self.n[s] += k
            after = self.n[s]
        self.q[eng].append((fn, waits, incs))
        return after if ms else None

    def replay(self, eng_name, eng):
        for fn, waits, incs in self.q[eng_name]:
            for s, v in waits:
                eng.wait_ge(self.sems[s], v)
            ins = fn(eng)
            for s, k in incs:
                ins.then_inc(self.sems[s], k)

    def run_block(self, nc):
        with nc.Block() as block:
            @block.sync
            def _(e):
                self.replay("sync", e)

            @block.scalar
            def _(e):
                self.replay("scalar", e)

            @block.vector
            def _(e):
                self.replay("vector", e)

            @block.tensor
            def _(e):
                self.replay("tensor", e)

            @block.gpsimd
            def _(e):
                self.replay("gpsimd", e)


def build_program():
    nc = bass.Bass("TRN2", target_bir_lowering=False, debug=False)

    x_d = nc.dram_tensor("x", [TOK, D], F32, kind="ExternalInput").ap()
    w0b_d = nc.dram_tensor("w0b", [128, D], F32, kind="ExternalInput").ap()
    w1b_d = nc.dram_tensor("w1b", [128, D], F32, kind="ExternalInput").ap()
    biasb_d = nc.dram_tensor("biasb", [128, D], F32, kind="ExternalInput").ap()
    dvrep_d = nc.dram_tensor("dvrep", [128, 2], F32, kind="ExternalInput").ap()
    onesrow_d = nc.dram_tensor("onesrow", [1, 128], F32, kind="ExternalInput").ap()
    onescol_d = nc.dram_tensor("onescol", [128, 1], F32, kind="ExternalInput").ap()
    epsh_d = nc.dram_tensor("epsh", [1, 1], F32, kind="ExternalInput").ap()
    wsl_d = nc.dram_tensor("wsl", [3, SLICE], F32, kind="ExternalInput").ap()
    w1f_d = nc.dram_tensor("w1f", [H, D], F32, kind="ExternalInput").ap()
    w2f_d = nc.dram_tensor("w2f", [H, D], F32, kind="ExternalInput").ap()
    w3f_d = nc.dram_tensor("w3f", [D, H], F32, kind="ExternalInput").ap()
    out_d = nc.dram_tensor("out", [TOK, D], F32, kind="ExternalOutput").ap()
    q1t_d = nc.dram_tensor("q1t", [H, D], BF16, kind="Internal").ap()
    q2t_d = nc.dram_tensor("q2t", [H, D], BF16, kind="Internal").ap()
    q3t_d = nc.dram_tensor("q3t", [H, D], BF16, kind="Internal").ap()
    h1t_d = nc.dram_tensor("h1t", [KB * 128, TOK], BF16, kind="Internal").ap()
    h1n_d = nc.dram_tensor("h1n", [TOK, D], BF16, kind="Internal").ap()
    ccin_d = nc.dram_tensor("ccin", [128, 128], F32, kind="Internal").ap()
    ccout_d = nc.dram_tensor("ccout", [128, 128], F32, kind="Internal").ap()
    dbgy2_d = nc.dram_tensor("dbgy2", [TOK, D], F32,
                             kind="ExternalOutput" if _DEBUG_Y2 else "Internal").ap()

    sem_names = (["cv", "cs", "ct", "cg", "sK", "sDBG", "sH1", "sCC"] +
                 [f"sX{i}" for i in range(2)] + [f"sG{i}" for i in range(2)] +
                 [f"sP{i}" for i in range(3)] + [f"sT{i}" for i in range(2)] +
                 [f"sS{i}" for i in range(2)] + [f"sHT{i}" for i in range(2)] +
                 [f"sHS{i}" for i in range(2)] + [f"sHN{i}" for i in range(2)] +
                 [f"sW1{i}" for i in range(2)] + [f"sW2{i}" for i in range(2)] +
                 [f"sW3{i}" for i in range(2)] +
                 [f"sHB{i}" for i in range(2)] + [f"sO{i}" for i in range(2)])
    stack = contextlib.ExitStack()
    sems = {nm: stack.enter_context(nc.semaphore(nm)) for nm in sem_names}

    with stack:
        thc = stack.enter_context(nc.sbuf_tensor("thc", [128, 8], F32))
        onesr = stack.enter_context(nc.sbuf_tensor("onesr", [1, 128], F32))
        onesc = stack.enter_context(nc.sbuf_tensor("onesc", [128, 1], F32))
        epshs = stack.enter_context(nc.sbuf_tensor("epshs", [1, 4], F32))
        gacc = stack.enter_context(nc.sbuf_tensor("gacc", [128, 4], F32))
        thsml = stack.enter_context(nc.sbuf_tensor("thsml", [1, 4], F32))
        stats = stack.enter_context(nc.sbuf_tensor("stats", [128, 16], F32))
        st2 = stack.enter_context(nc.sbuf_tensor("st2", [128, 8], F32))
        h1Th = stack.enter_context(nc.sbuf_tensor("h1Th", [128, KB * QT], BF16))
        w1t = stack.enter_context(nc.sbuf_tensor("w1t", [128, 2 * D], BF16))
        w2t = stack.enter_context(nc.sbuf_tensor("w2t", [128, 2 * D], BF16))
        sgb = stack.enter_context(nc.sbuf_tensor("sgb", [128, 2 * QT], BF16))
        prep_st = contextlib.ExitStack()
        pin = prep_st.enter_context(nc.sbuf_tensor("pin", [128, 3 * D], F32))
        s1b = prep_st.enter_context(nc.sbuf_tensor("s1b", [128, 2 * D], BF16))
        v2b = prep_st.enter_context(nc.sbuf_tensor("v2b", [128, 2 * D], BF16))
        qnb = prep_st.enter_context(nc.sbuf_tensor("qnb", [128, 2 * D], BF16))
        qtb = prep_st.enter_context(nc.sbuf_tensor("qtb", [128, 2 * D], BF16))
        attn_st = contextlib.ExitStack()
        xin = attn_st.enter_context(nc.sbuf_tensor("xin", [128, 2 * D], F32))
        sA = attn_st.enter_context(nc.sbuf_tensor("sA", [128, D], F32))
        sB = attn_st.enter_context(nc.sbuf_tensor("sB", [128, D], F32))
        sC = attn_st.enter_context(nc.sbuf_tensor("sC", [128, D], F32))
        sD = attn_st.enter_context(nc.sbuf_tensor("sD", [128, D], F32))
        ubf = attn_st.enter_context(nc.sbuf_tensor("ubf", [128, 3 * D], BF16))
        h1nb = attn_st.enter_context(nc.sbuf_tensor("h1nb", [128, 2 * D], BF16))
        tbuf = attn_st.enter_context(nc.sbuf_tensor("tbuf", [128, 2 * D], BF16))
        w0bs = attn_st.enter_context(nc.sbuf_tensor("w0bs", [128, D], BF16))
        w1bs = attn_st.enter_context(nc.sbuf_tensor("w1bs", [128, D], BF16))
        biasbs = attn_st.enter_context(nc.sbuf_tensor("biasbs", [128, D], BF16))
        dvreps = attn_st.enter_context(nc.sbuf_tensor("dvreps", [128, 2], F32))
        dvc = attn_st.enter_context(nc.sbuf_tensor("dvc", [128, 2], F32))
        gb = attn_st.enter_context(nc.sbuf_tensor("gb", [128, 2 * GCH], F32))
        gjunk = attn_st.enter_context(nc.sbuf_tensor("gjunk", [128, GCH], BF16))
        gch = attn_st.enter_context(nc.sbuf_tensor("gch", [128, 32], F32))
        p_th = stack.enter_context(nc.psum_tensor("p_th", [1, 4], F32))
        p_bc = stack.enter_context(nc.psum_tensor("p_bc", [128, 4], F32))
        px1 = stack.enter_context(nc.psum_tensor("px1", [128, QT], F32))
        px2 = stack.enter_context(nc.psum_tensor("px2", [128, QT], F32))
        pyA = stack.enter_context(nc.psum_tensor("pyA", [128, 512], F32))
        pyB = stack.enter_context(nc.psum_tensor("pyB", [128, 512], F32))

        E = Emitter(sems)

        d_rows = None
        for dst, src in ((dvreps[:], dvrep_d), (onesr[:], onesrow_d),
                         (onesc[:], onescol_d), (epshs[0:1, 0:1], epsh_d)):
            E.emit("scalar",
                   lambda e, dst=dst, src=src: e.dma_start(out=dst, in_=src),
                   incs=[("sK", 16)])
        for dst, src in ((w0bs[:], w0b_d), (w1bs[:], w1b_d), (biasbs[:], biasb_d)):
            d_rows = E.emit("gpsimd",
                            lambda e, dst=dst, src=src: e.dma_start(out=dst, in_=src),
                            incs=[("sK", 16)], ms=True)
        ms_dvc = E.emit("vector",
               lambda e: e.tensor_scalar(dvc[:], dvreps[:], 0.9, 1.0, ALU.max, ALU.min),
               waits=[("sK", d_rows)], incs=[("cv", 1)], ms=True)
        pstr = dvc[:].ap[0][0]
        dv0_b = bass.AP(dvc, 0, [[pstr, 128], [0, D]])
        dv1_b = bass.AP(dvc, 1, [[pstr, 128], [0, D]])

        flsl = wsl_d.rearrange("a b -> (a b)")
        NCHUNK = 3 * NGC
        ms_gab = [None] * NCHUNK
        for gidx in range(NCHUNK):
            slot = gidx % 2
            w_g = [("cs", ms_gab[gidx - 2])] if gidx >= 2 else []
            msg = E.emit("sync",
                         lambda e, gidx=gidx, slot=slot: e.dma_start(
                             out=gb[:, slot * GCH:slot * GCH + GCH],
                             in_=flsl[gidx * 128 * GCH:(gidx + 1) * 128 * GCH]
                             .rearrange("(p n) -> p n", p=128)),
                         waits=w_g, incs=[(f"sG{slot}", 16)], ms=True)
            ms_gab[gidx] = E.emit("scalar",
                                  lambda e, slot=slot, gidx=gidx: e.activation(
                                      gjunk[:], gb[:, slot * GCH:slot * GCH + GCH],
                                      AF.Abs, accum_out=gch[:, gidx:gidx + 1]),
                                  waits=[(f"sG{slot}", msg)] +
                                        ([("cs", ms_gab[gidx - 2])] if gidx >= 2 else []),
                                  incs=[("cs", 1)], ms=True)
        mgs = None
        for wi in range(3):
            mgs = E.emit("vector",
                         lambda e, wi=wi: e.reduce_sum(
                             gacc[:, wi:wi + 1], gch[:, wi * NGC:(wi + 1) * NGC],
                             axis=mybir.AxisListType.X),
                         waits=[("cs", ms_gab[NCHUNK - 1])] if wi == 0 else [],
                         incs=[("cv", 1)], ms=True)
        ms_ci = E.emit("sync",
                       lambda e: e.dma_start(out=ccin_d[:, 0:4], in_=gacc[:, 0:4]),
                       waits=[("cv", mgs)], incs=[("sCC", 16)], ms=True)
        ms_cc = E.emit("gpsimd",
                       lambda e: e.collective_compute(
                           "AllReduce", ALU.add,
                           [list(range(NCORE))],
                           [ccin_d.opt()], [ccout_d.opt()]),
                       waits=[("sCC", ms_ci)], incs=[("cg", 1)], ms=True)
        ms_co = E.emit("sync",
                       lambda e: e.dma_start(out=gacc[:, 0:4], in_=ccout_d[:, 0:4]),
                       waits=[("cg", ms_cc)], incs=[("sCC", 16)], ms=True)
        ms_thp = None
        for wi in range(3):
            ms_thp = E.emit(
                "tensor",
                lambda e, wi=wi: e.matmul(p_th[0:1, wi:wi + 1], gacc[:, wi:wi + 1],
                                          onesc[:], start=True, stop=True),
                waits=[("sCC", ms_co), ("sK", d_rows)], incs=[("ct", 1)], ms=True)
        ms_thr = E.emit(
            "vector",
            lambda e: e.tensor_scalar(thsml[0:1, 0:3], p_th[0:1, 0:3],
                                      0.5 / WELEM, 0.5e-8, ALU.mult, ALU.add),
            waits=[("ct", ms_thp)], incs=[("cv", 1)], ms=True)
        ms_thb = E.emit(
            "tensor",
            lambda e: e.matmul(p_bc[:, 0:3], onesr[:], thsml[0:1, 0:3],
                               start=True, stop=True),
            waits=[("cv", ms_thr)], incs=[("ct", 1)], ms=True)
        mcp = E.emit("vector",
                     lambda e: e.tensor_copy(thc[:, 0:3], p_bc[:, 0:3]),
                     waits=[("ct", ms_thb)], incs=[("cv", 1)], ms=True)
        ms_thall = E.emit(
            "vector",
            lambda e: e.tensor_scalar(thc[:, 4:7], thc[:, 0:3], -1.0, None, ALU.mult),
            waits=[("cv", mcp)], incs=[("cv", 1)], ms=True)
        ms_th = [ms_thall, ms_thall, ms_thall]

        ms_x = [None] * MT
        ms_xfree = [None] * MT
        ms_norm = [None] * MT
        ms_h1free = [None] * MT
        ms_h1tp = [None] * MT
        ms_var = [None] * MT
        ms_sqrt = [None] * MT

        def emit_x_dma(m):
            if m >= MT:
                return
            waits = [("cv", ms_xfree[m - 2])] if m >= 2 else []
            ms_x[m] = E.emit(
                "scalar",
                lambda e, m=m: e.dma_start(
                    out=xin[:, (m % 2) * D:(m % 2) * D + D],
                    in_=x_d[m * 128:(m + 1) * 128, :]),
                waits=waits, incs=[(f"sX{m % 2}", 16)], ms=True)

        def attn_v1(m):
            xm = xin[:, (m % 2) * D:(m % 2) * D + D]
            ub = ubf[:, (m % 3) * D:(m % 3) * D + D]
            waits = [(f"sX{m % 2}", ms_x[m]), ("cv", ms_dvc)]
            E.emit("vector", lambda e: e.tensor_mul(sA[:], xm, w0bs[:]),
                   waits=waits, incs=[("cv", 1)])
            E.emit("vector",
                   lambda e: e.tensor_tensor_scan(sB[:], dv0_b, sA[:], 0.0,
                                                  ALU.mult, ALU.add),
                   incs=[("cv", 1)])
            E.emit("vector",
                   lambda e: e.tensor_tensor_scan(sC[:], dv1_b, xm, 0.0,
                                                  ALU.mult, ALU.add),
                   incs=[("cv", 1)])
            E.emit("vector", lambda e: e.tensor_mul(sD[:], sC[:], w1bs[:]),
                   incs=[("cv", 1)])
            E.emit("vector", lambda e: e.tensor_add(sC[:], sB[:], sD[:]),
                   incs=[("cv", 1)])
            ms_xfree[m] = E.emit("vector",
                                 lambda e: e.tensor_add(sA[:], xm, biasbs[:]),
                                 incs=[("cv", 1)], ms=True)
            wu = [("cv", ms_norm[m - 3])] if m >= 3 else []
            E.emit("vector",
                   lambda e: e.scalar_tensor_tensor(ub, sC[:], 1.0, sA[:],
                                                    ALU.bypass, ALU.add,
                                                    accum_out=stats[:, 0:1]),
                   waits=wu, incs=[("cv", 1)])
            mq = E.emit("vector",
                   lambda e: e.scalar_tensor_tensor(sB[:], ub, 1.0, ub,
                                                    ALU.bypass, ALU.mult,
                                                    accum_out=stats[:, 1:2]),
                   incs=[("cv", 1)], ms=True)
            o = 8 + (m % 2) * 4
            mq = E.emit("vector",
                   lambda e, o=o: e.tensor_scalar(stats[:, o:o + 1], stats[:, 0:1],
                                                  -1.0 / D, None, ALU.mult),
                   waits=[("cv", mq)], incs=[("cv", 1)], ms=True)
            mq = E.emit("vector",
                   lambda e: e.tensor_scalar(stats[:, 3:4], stats[:, 1:2],
                                             1.0 / D, EPS, ALU.mult, ALU.add),
                   waits=[("cv", mq)], incs=[("cv", 1)], ms=True)
            mq = E.emit("vector",
                   lambda e, o=o: e.tensor_scalar(stats[:, 4:5], stats[:, o:o + 1],
                                                  stats[:, o:o + 1], None, ALU.mult),
                   waits=[("cv", mq)], incs=[("cv", 1)], ms=True)
            mq = E.emit("vector",
                   lambda e: e.tensor_sub(stats[:, 5:6], stats[:, 3:4], stats[:, 4:5]),
                   waits=[("cv", mq)], incs=[("cv", 1)], ms=True)
            ms_var[m] = E.emit("vector",
                         lambda e, o=o: e.tensor_scalar(stats[:, o + 2:o + 3],
                                                        stats[:, 5:6],
                                                        1e-30, None, ALU.max),
                         waits=[("cv", mq)], incs=[("cv", 1)], ms=True)

        def attn_s1(m):
            o = 8 + (m % 2) * 4
            ms_sqrt[m] = E.emit("scalar",
                         lambda e, o=o: e.activation(stats[:, o + 3:o + 4],
                                                     stats[:, o + 2:o + 3], AF.Sqrt),
                         waits=[("cv", ms_var[m])], incs=[("cs", 1)], ms=True)

        def attn_v2(m):
            o = 8 + (m % 2) * 4
            ub = ubf[:, (m % 3) * D:(m % 3) * D + D]
            h1b = h1nb[:, (m % 2) * D:(m % 2) * D + D]
            msr = E.emit("vector",
                   lambda e, o=o: e.reciprocal(stats[:, o + 1:o + 2],
                                               stats[:, o + 3:o + 4]),
                   waits=[("cs", ms_sqrt[m])], incs=[("cv", 1)], ms=True)
            wh = ([(f"sHN{m % 2}", ms_h1free[m - 2])] if m >= 2 else []) + [("cv", msr)]
            ms_norm[m] = E.emit("vector",
                                lambda e, o=o, h1b=h1b, ub=ub: e.tensor_scalar(
                                    h1b, ub, stats[:, o:o + 1], stats[:, o + 1:o + 2],
                                    ALU.add, ALU.mult),
                                waits=wh, incs=[("cv", 1)], ms=True)

        def attn_s2(m):
            h1b = h1nb[:, (m % 2) * D:(m % 2) * D + D]
            w_tp = [("cv", ms_norm[m])] + \
                   ([(f"sHS{m % 2}", ms_h1tp[m - 2])] if m >= 2 else [])
            mtp = E.emit("scalar",
                   lambda e, m=m, h1b=h1b: e.dma_start_transpose(
                       tbuf[:, (m % 2) * D:(m % 2) * D + D]
                           .rearrange("p (t q) -> p t q", t=KB),
                       h1b),
                   waits=w_tp, incs=[(f"sHT{m % 2}", 16)], ms=True)
            ms_h1tp[m] = E.emit("scalar",
                   lambda e, m=m: e.dma_start(
                       out=h1t_d[:, m * 128:(m + 1) * 128]
                           .rearrange("(k p) q -> p k q", p=128),
                       in_=tbuf[:, (m % 2) * D:(m % 2) * D + D]
                           .rearrange("p (t q) -> p t q", t=KB)),
                   waits=[(f"sHT{m % 2}", mtp)], incs=[(f"sHS{m % 2}", 16)], ms=True)
            ms_h1free[m] = E.emit("scalar",
                                  lambda e, m=m, h1b=h1b: e.dma_start(
                                      out=h1n_d[m * 128:(m + 1) * 128, :], in_=h1b),
                                  incs=[(f"sHN{m % 2}", 16)], ms=True)

        att_v1_next = [0]
        att_fin_next = [0]

        def attn_tick():
            if att_fin_next[0] < att_v1_next[0] and att_fin_next[0] < MT:
                m = att_fin_next[0]
                attn_s1(m)
                attn_v2(m)
                attn_s2(m)
                att_fin_next[0] += 1
            if att_v1_next[0] < MT:
                m = att_v1_next[0]
                attn_v1(m)
                emit_x_dma(m + 2)
                att_v1_next[0] += 1

        jobs = []
        w1_ji = {}
        w2_ji = {}
        w3_last_ji = {}
        w3list = [(cc, b) for cc in range(3) for b in range(KB)]
        w3ptr = 0
        for k in range(HBK):
            w1_ji[k] = len(jobs)
            jobs.append((0, w1f_d[k * 128:(k + 1) * 128, :], D,
                         q1t_d[k * 128:(k + 1) * 128, :], False))
            w2_ji[k] = len(jobs)
            jobs.append((1, w2f_d[k * 128:(k + 1) * 128, :], D,
                         q2t_d[k * 128:(k + 1) * 128, :], False))
            if w3ptr < len(w3list):
                cc, b = w3list[w3ptr]
                Fc = 1536 if cc == 2 else 2048
                w3_last_ji[cc] = len(jobs)
                jobs.append((2, w3f_d[b * 128:(b + 1) * 128, cc * 2048:cc * 2048 + Fc],
                             Fc,
                             q3t_d[cc * 2048:cc * 2048 + Fc, b * 128:(b + 1) * 128]
                             .rearrange("(t p) q -> p t q", p=128), True))
                w3ptr += 1
        while w3ptr < len(w3list):
            cc, b = w3list[w3ptr]
            Fc = 1536 if cc == 2 else 2048
            w3_last_ji[cc] = len(jobs)
            jobs.append((2, w3f_d[b * 128:(b + 1) * 128, cc * 2048:cc * 2048 + Fc],
                         Fc,
                         q3t_d[cc * 2048:cc * 2048 + Fc, b * 128:(b + 1) * 128]
                         .rearrange("(t p) q -> p t q", p=128), True))
            w3ptr += 1
        NJ = len(jobs)
        ms_in = [None] * NJ
        ms_s1 = [None] * NJ
        ms_v2 = [None] * NJ
        ms_cmb = [None] * NJ
        ms_tp = [None] * NJ
        ms_st = [None] * NJ

        def q_in(ji):
            if ji >= NJ:
                return
            wi, src, F, dst, is3 = jobs[ji]
            pin_s = pin[:, (ji % 3) * D:(ji % 3) * D + F]
            w_in = []
            if ji >= 3:
                w_in += [("cs", ms_s1[ji - 3]), ("cv", ms_v2[ji - 3])]
            ms_in[ji] = E.emit("sync",
                               lambda e, pin_s=pin_s, src=src: e.dma_start(
                                   out=pin_s, in_=src),
                               waits=w_in, incs=[(f"sP{ji % 3}", 16)], ms=True)

        def q_sgn(ji):
            wi, src, F, dst, is3 = jobs[ji]
            pin_s = pin[:, (ji % 3) * D:(ji % 3) * D + F]
            s1_s = s1b[:, (ji % 2) * D:(ji % 2) * D + F]
            w_s = [(f"sP{ji % 3}", ms_in[ji]), ("cv", ms_th[wi])] + \
                  ([("cv", ms_cmb[ji - 2])] if ji >= 2 else [])
            ms_s1[ji] = E.emit("scalar",
                               lambda e, s1_s=s1_s, pin_s=pin_s, wi=wi: e.sign(
                                   s1_s, pin_s, bias=thc[:, 4 + wi:5 + wi]),
                               waits=w_s, incs=[("cs", 1)], ms=True)

        def q_lt(ji):
            wi, src, F, dst, is3 = jobs[ji]
            pin_s = pin[:, (ji % 3) * D:(ji % 3) * D + F]
            v2_s = v2b[:, (ji % 2) * D:(ji % 2) * D + F]
            w_v = [(f"sP{ji % 3}", ms_in[ji]), ("cv", ms_th[wi])] + \
                  ([("cv", ms_cmb[ji - 2])] if ji >= 2 else [])
            ms_v2[ji] = E.emit("vector",
                               lambda e, v2_s=v2_s, pin_s=pin_s, wi=wi: e.tensor_scalar(
                                   v2_s, pin_s, thc[:, 4 + wi:5 + wi], 0.5,
                                   ALU.is_lt, ALU.subtract),
                               waits=w_v, incs=[("cv", 1)], ms=True)

        def q_cmb(ji):
            if ji < 0 or ji >= NJ:
                return
            wi, src, F, dst, is3 = jobs[ji]
            s1_s = s1b[:, (ji % 2) * D:(ji % 2) * D + F]
            v2_s = v2b[:, (ji % 2) * D:(ji % 2) * D + F]
            qn_s = qnb[:, (ji % 2) * D:(ji % 2) * D + F]
            w_c = [("cs", ms_s1[ji])] + ([(f"sT{ji % 2}", ms_tp[ji - 2])] if ji >= 2 else [])
            ms_cmb[ji] = E.emit("vector",
                                lambda e, qn_s=qn_s, v2_s=v2_s, s1_s=s1_s:
                                e.scalar_tensor_tensor(qn_s, v2_s, -2.0, s1_s,
                                                       ALU.mult, ALU.add),
                                waits=w_c, incs=[("cv", 1)], ms=True)

        def q_tp(ji):
            if ji < 0 or ji >= NJ:
                return
            wi, src, F, dst, is3 = jobs[ji]
            qn_s = qnb[:, (ji % 2) * D:(ji % 2) * D + F]
            qt_s = qtb[:, (ji % 2) * D:(ji % 2) * D + F]
            w_t = [("cv", ms_cmb[ji])] + ([(f"sS{ji % 2}", ms_st[ji - 2])] if ji >= 2 else [])
            nt = F // 128
            ms_tp[ji] = E.emit("scalar",
                               lambda e, qt_s=qt_s, qn_s=qn_s, nt=nt: e.dma_start_transpose(
                                   qt_s.rearrange("p (t q) -> p t q", t=nt), qn_s),
                               waits=w_t, incs=[(f"sT{ji % 2}", 16)], ms=True)

        def q_st(ji):
            if ji < 0 or ji >= NJ:
                return
            wi, src, F, dst, is3 = jobs[ji]
            qt_s = qtb[:, (ji % 2) * D:(ji % 2) * D + F]
            nt = F // 128
            src_st = qt_s.rearrange("p (t q) -> p t q", t=nt) if is3 else qt_s
            ms_st[ji] = E.emit("sync",
                               lambda e, dst=dst, src_st=src_st: e.dma_start(
                                   out=dst, in_=src_st),
                               waits=[(f"sT{ji % 2}", ms_tp[ji])],
                               incs=[(f"sS{ji % 2}", 16)], ms=True)

        job_next = [0]

        def q_tick():
            ji = job_next[0]
            if ji >= NJ + 2:
                return False
            if ji == 0:
                q_in(0)
                q_in(1)
            q_in(ji + 2)
            q_st(ji - 2)
            if ji < NJ:
                q_sgn(ji)
                q_lt(ji)
            q_cmb(ji - 1)
            q_tp(ji - 1)
            job_next[0] += 1
            return True

        def prep_done_waits():
            w = [("cs", ms_s1[NJ - 1]), ("cv", ms_cmb[NJ - 1])]
            w += [(f"sP{i}", E.n[f"sP{i}"]) for i in range(3)]
            w += [(f"sT{i}", E.n[f"sT{i}"]) for i in range(2)]
            w += [(f"sS{i}", E.n[f"sS{i}"]) for i in range(2)]
            return w

        emit_x_dma(0)
        emit_x_dma(1)
        for _ in range(6):
            attn_tick()
            q_tick()

        attn_tail = [("cv", None)]

        def finish_attn():
            while att_fin_next[0] < MT:
                attn_tick()

        for _ in range(10):
            attn_tick()
            q_tick()
            q_tick()
        finish_attn()

        attn_waits = ([(f"sHT{i}", E.n[f"sHT{i}"]) for i in range(2)] +
                      [(f"sHS{i}", E.n[f"sHS{i}"]) for i in range(2)] +
                      [(f"sHN{i}", E.n[f"sHN{i}"]) for i in range(2)] +
                      [(f"sX{i}", E.n[f"sX{i}"]) for i in range(2)] +
                      [(f"sG{i}", E.n[f"sG{i}"]) for i in range(2)] +
                      [("cs", ms_gab[NCHUNK - 1])])
        attn_st.close()
        post_st = contextlib.ExitStack()
        hTq = post_st.enter_context(nc.sbuf_tensor("hTq", [128, HBK * QT], BF16))
        y2acc = post_st.enter_context(nc.sbuf_tensor("y2acc", [128, 4 * D], F32))
        w3t = post_st.enter_context(nc.sbuf_tensor("w3t", [128, 2 * G3 * 512], BF16))
        first_post = [True]

        ms_h1q = [None]
        ms_w1 = {}
        ms_w2 = {}
        ms_p1 = {}
        ms_p2 = {}
        ms_sig = {}
        ms_hT = {}
        ms_w3l = {}
        ms_f3u = {}
        ms_ev = {}
        ms_f3last = {}
        ln2_first = [True]
        ms_ln2free = [None] * 4
        ms_outd = [None] * 4
        ln2_bufs = {}

        def emit_h1q(q):
            w = [("cv", ms_norm[4 * q + 3]),
                 (f"sHS{(4 * q + 3) % 2}", ms_h1tp[4 * q + 3]),
                 (f"sHS{(4 * q + 2) % 2}", ms_h1tp[4 * q + 2])]
            if q > 0:
                w.append(("ct", ms_p2[(q - 1, HBK - 1)]))
            ms_h1q[0] = E.emit(
                "sync",
                lambda e, q=q: e.dma_start(
                    out=h1Th[:].rearrange("p (k n) -> p k n", k=KB),
                    in_=h1t_d[:, q * QT:(q + 1) * QT]
                        .rearrange("(k p) n -> p k n", p=128)),
                waits=w, incs=[("sH1", 16)], ms=True)

        def emit_w_load(q, hb):
            if hb >= HBK:
                return
            j1, j2 = w1_ji[hb], w2_ji[hb]
            assert ms_st[j1] is not None and ms_st[j2] is not None, (hb, j1, j2)
            w_1 = [(f"sS{j1 % 2}", ms_st[j1])]
            prev = (q, hb - 2) if hb >= 2 else ((q - 1, HBK - 2 + hb) if q > 0 else None)
            if prev in ms_p1:
                w_1.append(("ct", ms_p1[prev]))
            ms_w1[(q, hb)] = E.emit(
                "sync",
                lambda e, hb=hb: e.dma_start(
                    out=w1t[:, (hb % 2) * D:(hb % 2) * D + D],
                    in_=q1t_d[hb * 128:(hb + 1) * 128, :]),
                waits=w_1, incs=[(f"sW1{hb % 2}", 16)], ms=True)
            w_2 = [(f"sS{j2 % 2}", ms_st[j2])]
            if prev in ms_p2:
                w_2.append(("ct", ms_p2[prev]))
            ms_w2[(q, hb)] = E.emit(
                "sync",
                lambda e, hb=hb: e.dma_start(
                    out=w2t[:, (hb % 2) * D:(hb % 2) * D + D],
                    in_=q2t_d[hb * 128:(hb + 1) * 128, :]),
                waits=w_2, incs=[(f"sW2{hb % 2}", 16)], ms=True)

        def emit_f12(q, hb):
            wslab1 = w1t[:, (hb % 2) * D:(hb % 2) * D + D]
            wslab2 = w2t[:, (hb % 2) * D:(hb % 2) * D + D]
            w_mm = [(f"sW1{hb % 2}", ms_w1[(q, hb)]), ("sH1", ms_h1q[0])]
            prevsig = (q, hb - 1) if hb >= 1 else ((q - 1, HBK - 1) if q > 0 else None)
            if prevsig in ms_sig:
                w_mm.append(("cs", ms_sig[prevsig]))
            for k in range(KB):
                def mk(e, k=k, wslab=wslab1):
                    return e.matmul(px1[:], wslab[:, k * 128:(k + 1) * 128],
                                    h1Th[:, k * QT:(k + 1) * QT],
                                    start=(k == 0), stop=(k == KB - 1))
                if k == 0:
                    E.emit("tensor", mk, waits=w_mm)
                elif k == KB - 1:
                    ms_p1[(q, hb)] = E.emit("tensor", mk, incs=[("ct", 1)], ms=True)
                else:
                    E.emit("tensor", mk)
            w_mm2 = [(f"sW2{hb % 2}", ms_w2[(q, hb)])]
            prevhT = (q, hb - 1) if hb >= 1 else ((q - 1, HBK - 1) if q > 0 else None)
            if prevhT in ms_hT:
                w_mm2.append(("cv", ms_hT[prevhT]))
            for k in range(KB):
                def mk2(e, k=k, wslab=wslab2):
                    return e.matmul(px2[:], wslab[:, k * 128:(k + 1) * 128],
                                    h1Th[:, k * QT:(k + 1) * QT],
                                    start=(k == 0), stop=(k == KB - 1))
                if k == 0:
                    E.emit("tensor", mk2, waits=w_mm2)
                elif k == KB - 1:
                    ms_p2[(q, hb)] = E.emit("tensor", mk2, incs=[("ct", 1)], ms=True)
                else:
                    E.emit("tensor", mk2)
            ms_sig[(q, hb)] = E.emit(
                "scalar",
                lambda e, hb=hb: e.activation(
                    sgb[:, (hb % 2) * QT:(hb % 2) * QT + QT],
                    px1[:], AF.Sigmoid, scale=0.5),
                waits=[("ct", ms_p1[(q, hb)])], incs=[("cs", 1)], ms=True)
            w_ht = [("cs", ms_sig[(q, hb)]), ("ct", ms_p2[(q, hb)])]
            prevf3 = (q - 1, hb // G3)
            if prevf3 in ms_f3last:
                w_ht.append(("ct", ms_f3last[prevf3]))
            if first_post[0]:
                first_post[0] = False
                w_ht = attn_waits + w_ht
            ms_hT[(q, hb)] = E.emit(
                "vector",
                lambda e, hb=hb: e.tensor_mul(
                    hTq[:, hb * QT:(hb + 1) * QT],
                    sgb[:, (hb % 2) * QT:(hb % 2) * QT + QT], px2[:]),
                waits=w_ht, incs=[("cv", 1)], ms=True)

        def emit_w3_load(q, g, dq, prev):
            w = []
            if prev in ms_f3u:
                w.append(("ct", ms_f3u[prev]))
            lo_c = (g * G3 * 128) // 2048
            hi_c = ((g + 1) * G3 * 128 - 1) // 2048
            for ccx in range(lo_c, hi_c + 1):
                jid = w3_last_ji[ccx]
                assert ms_st[jid] is not None, (g, dq, ccx, jid)
                w.append((f"sS{jid % 2}", ms_st[jid]))
            ms_w3l[(q, g, dq)] = E.emit(
                "sync",
                lambda e, g=g, dq=dq: e.dma_start(
                    out=w3t[:, (dq % 2) * G3 * 512:(dq % 2) * G3 * 512 + G3 * 512]
                        .rearrange("p (t n) -> p t n", t=G3),
                    in_=q3t_d[g * G3 * 128:(g + 1) * G3 * 128, dq * 512:(dq + 1) * 512]
                        .rearrange("(t p) n -> p t n", p=128)),
                waits=w, incs=[(f"sW3{dq % 2}", 16)], ms=True)

        def emit_f3(q, g):
            unit = 0
            msu = None
            for dq in range(4):
                if dq == 2:
                    emit_w3_load(q, g, 2, (q, g, 0))
                    emit_w3_load(q, g, 3, (q, g, 1))
                for m in range(4):
                    pp = pyA if unit % 2 == 0 else pyB
                    w_u = [("cv", ms_hT[(q, (g + 1) * G3 - 1)]),
                           (f"sW3{dq % 2}", ms_w3l[(q, g, dq)])]
                    prev_ev = None
                    if unit >= 2:
                        prev_ev = ms_ev.get((q, g, unit - 2))
                    elif g >= 1:
                        prev_ev = ms_ev.get((q, g - 1, 14 + unit))
                    elif q > 0:
                        prev_ev = ms_ev.get((q - 1, NG3 - 1, 14 + unit))
                    if prev_ev is not None:
                        w_u.append(("cv", prev_ev))
                    for kk in range(G3):
                        def mk3(e, kk=kk, m=m, dq=dq, g=g, pp=pp):
                            return e.matmul(
                                pp[:],
                                hTq[:, (g * G3 + kk) * QT + m * 128:
                                       (g * G3 + kk) * QT + (m + 1) * 128],
                                w3t[:, (dq % 2) * G3 * 512 + kk * 512:
                                       (dq % 2) * G3 * 512 + kk * 512 + 512],
                                start=(kk == 0), stop=(kk == G3 - 1))
                        if kk == 0:
                            E.emit("tensor", mk3, waits=w_u)
                        elif kk == G3 - 1:
                            msu = E.emit("tensor", mk3, incs=[("ct", 1)], ms=True)
                        else:
                            E.emit("tensor", mk3)
                    ms_f3u[(q, g, dq)] = msu
                    ydst = y2acc[:, m * D + dq * 512: m * D + dq * 512 + 512]
                    w_e = [("ct", msu)]
                    if g == 0:
                        mse = E.emit("vector",
                                     lambda e, ydst=ydst, pp=pp: e.tensor_copy(ydst, pp[:]),
                                     waits=w_e, incs=[("cv", 1)], ms=True)
                    else:
                        mse = E.emit("vector",
                                     lambda e, ydst=ydst, pp=pp: e.scalar_tensor_tensor(
                                         ydst, pp[:], 1.0, ydst, ALU.bypass, ALU.add),
                                     waits=w_e, incs=[("cv", 1)], ms=True)
                    ms_ev[(q, g, unit)] = mse
                    unit += 1
            if g + 1 < NG3:
                emit_w3_load(q, g + 1, 0, (q, g, 2))
                emit_w3_load(q, g + 1, 1, (q, g, 3))
            elif q + 1 < NQ:
                emit_w3_load(q + 1, 0, 0, (q, g, 2))
                emit_w3_load(q + 1, 0, 1, (q, g, 3))
            ms_f3last[(q, g)] = msu

        def alloc_ln2():
            ln2_bufs["uU"] = pin[:, 0:D]
            ln2_bufs["outb"] = pin[:, D:3 * D]
            ln2_bufs["h1nb2"] = s1b[:, 0:2 * D]

        def emit_ln2(q):
            uU_a = ln2_bufs["uU"]
            h1nb2_a = ln2_bufs["h1nb2"]
            outb_a = ln2_bufs["outb"]
            last_ev = ms_ev[(q, NG3 - 1, 15)]
            for m in range(4):
                mg = q * 4 + m
                w_hn = [("cv", ms_ln2free[(m - 2) % 4])] if m >= 2 else []
                if ln2_first[0]:
                    ln2_first[0] = False
                    w_hn = prep_done_waits() + w_hn
                elif m < 2 and q > 0:
                    w_hn = [("cv", ms_ln2free[(m + 2) % 4])] + w_hn
                ms_hnm = E.emit(
                    "sync",
                    lambda e, mg=mg, m=m: e.dma_start(
                        out=h1nb2_a[:, (m % 2) * D:(m % 2) * D + D],
                        in_=h1n_d[mg * 128:(mg + 1) * 128, :]),
                    waits=w_hn, incs=[(f"sHB{m % 2}", 16)], ms=True)
                h1s = h1nb2_a[:, (m % 2) * D:(m % 2) * D + D]
                yv = y2acc[:, m * D:(m + 1) * D]
                ob = outb_a[:, (m % 2) * D:(m % 2) * D + D]
                E.emit("vector",
                       lambda e, h1s=h1s, yv=yv: e.scalar_tensor_tensor(
                           uU_a, h1s, 4.0, yv, ALU.mult, ALU.add,
                           accum_out=st2[:, 0:1]),
                       waits=[(f"sHB{m % 2}", ms_hnm), ("cv", last_ev)],
                       incs=[("cv", 1)])
                w_sq = [(f"sO{m % 2}", ms_outd[(m - 2) % 4])] if m >= 2 else \
                       ([(f"sO{m % 2}", ms_outd[(m + 2) % 4])] if q > 0 else [])
                mq2 = E.emit("vector",
                       lambda e, ob=ob: e.scalar_tensor_tensor(
                           ob, uU_a, 1.0, uU_a, ALU.bypass, ALU.mult,
                           accum_out=st2[:, 1:2]),
                       waits=w_sq, incs=[("cv", 1)], ms=True)
                mq2 = E.emit("vector",
                       lambda e: e.tensor_scalar(st2[:, 2:3], st2[:, 0:1],
                                                 -1.0 / D, None, ALU.mult),
                       waits=[("cv", mq2)], incs=[("cv", 1)], ms=True)
                mq2 = E.emit("vector",
                       lambda e: e.tensor_scalar(st2[:, 3:4], st2[:, 1:2],
                                                 1.0 / D, EPS16, ALU.mult, ALU.add),
                       waits=[("cv", mq2)], incs=[("cv", 1)], ms=True)
                mq2 = E.emit("vector",
                       lambda e: e.tensor_scalar(st2[:, 4:5], st2[:, 2:3],
                                                 st2[:, 2:3], None, ALU.mult),
                       waits=[("cv", mq2)], incs=[("cv", 1)], ms=True)
                mq2 = E.emit("vector",
                       lambda e: e.tensor_sub(st2[:, 5:6], st2[:, 3:4], st2[:, 4:5]),
                       waits=[("cv", mq2)], incs=[("cv", 1)], ms=True)
                msv2 = E.emit("vector",
                              lambda e: e.tensor_scalar(st2[:, 5:6], st2[:, 5:6],
                                                        1e-30, None, ALU.max),
                              waits=[("cv", mq2)], incs=[("cv", 1)], ms=True)
                mss2 = E.emit("scalar",
                              lambda e: e.activation(st2[:, 6:7], st2[:, 5:6], AF.Sqrt),
                              waits=[("cv", msv2)], incs=[("cs", 1)], ms=True)
                msr2 = E.emit("vector", lambda e: e.reciprocal(st2[:, 7:8], st2[:, 6:7]),
                       waits=[("cs", mss2)], incs=[("cv", 1)], ms=True)
                ms_ln2free[m % 4] = E.emit(
                    "vector",
                    lambda e, ob=ob: e.tensor_scalar(ob, uU_a, st2[:, 2:3],
                                                     st2[:, 7:8],
                                                     ALU.add, ALU.mult),
                    waits=[("cv", msr2)], incs=[("cv", 1)], ms=True)
                ms_outd[m % 4] = E.emit(
                    "sync",
                    lambda e, mg=mg, m=m: e.dma_start(
                        out=out_d[mg * 128:(mg + 1) * 128, :],
                        in_=outb_a[:, (m % 2) * D:(m % 2) * D + D]),
                    waits=[("cv", ms_ln2free[m % 4])], incs=[(f"sO{m % 2}", 16)], ms=True)
                if _DEBUG_Y2:
                    E.emit("sync",
                           lambda e, mg=mg, m=m: e.dma_start(
                               out=dbgy2_d[mg * 128:(mg + 1) * 128, :],
                               in_=y2acc[:, m * D:(m + 1) * D]),
                           waits=[("cv", ms_ev[(q, NG3 - 1, 15)])],
                           incs=[("sDBG", 16)], ms=True)

        with post_st:
            for q in range(NQ):
                emit_h1q(q)
                emit_w_load(q, 0)
                emit_w_load(q, 1)
                for hb in range(HBK):
                    emit_f12(q, hb)
                    if q == 0:
                        q_tick()
                        q_tick()
                        q_tick()
                    emit_w_load(q, hb + 2)
                if q == 0:
                    while job_next[0] < NJ + 2:
                        q_tick()
                    alloc_ln2()
                if q == 0:
                    emit_w3_load(0, 0, 0, None)
                    emit_w3_load(0, 0, 1, None)
                for g in range(NG3):
                    emit_f3(q, g)
                emit_ln2(q)

            all_dma = [s for s in sem_names if s.startswith("s")]
            E.emit("sync", lambda e: e.nop(),
                   waits=[(nm, E.n[nm]) for nm in all_dma if E.n[nm] > 0])
            E.run_block(nc)
        prep_st.close()
    return nc


def kernel(L0_attn_x, L0_attn_weight, L0_attn_bias, L0_attn_decay_value,
           L0_ffn_w1, L0_ffn_w2, L0_ffn_w3):
    x = np.ascontiguousarray(np.asarray(L0_attn_x, dtype=np.float32)
                             .reshape(NCORE * TOK, D))
    wgt = np.asarray(L0_attn_weight, dtype=np.float32)
    bias = np.asarray(L0_attn_bias, dtype=np.float32)
    dv = np.asarray(L0_attn_decay_value, dtype=np.float32)
    w1 = np.ascontiguousarray(np.asarray(L0_ffn_w1, dtype=np.float32))
    w2 = np.ascontiguousarray(np.asarray(L0_ffn_w2, dtype=np.float32))
    w3 = np.ascontiguousarray(np.asarray(L0_ffn_w3, dtype=np.float32))

    w0b = np.ascontiguousarray(np.broadcast_to(wgt[0:1, :], (128, D)))
    w1b = np.ascontiguousarray(np.broadcast_to(wgt[1:2, :], (128, D)))
    biasb = np.ascontiguousarray(np.broadcast_to(bias.reshape(1, D), (128, D)))
    dvrep = np.ascontiguousarray(np.broadcast_to(dv.reshape(1, 2), (128, 2)))
    onesrow = np.ones((1, 128), np.float32)
    onescol = np.ones((128, 1), np.float32)
    epsh = np.full((1, 1), 0.5e-8, np.float32)
    wflat = [w1.reshape(-1), w2.reshape(-1), w3.reshape(-1)]

    if "prog" not in _NC_CACHE:
        _NC_CACHE["prog"] = build_program()
    nc = _NC_CACHE["prog"]

    in_maps = []
    for c in range(NCORE):
        wsl = np.stack([wf[c * SLICE:(c + 1) * SLICE] for wf in wflat])
        in_maps.append({
            "x": x[c * TOK:(c + 1) * TOK],
            "w0b": w0b, "w1b": w1b, "biasb": biasb, "dvrep": dvrep,
            "onesrow": onesrow, "onescol": onescol, "epsh": epsh,
            "wsl": np.ascontiguousarray(wsl),
            "w1f": w1, "w2f": w2, "w3f": w3,
        })
    res = run_bass_kernel_spmd(nc, in_maps, core_ids=list(range(NCORE)))
    out = np.concatenate([np.asarray(r["out"]) for r in res.results], axis=0)
    return out.reshape(2, 8192, D)


if __name__ == "__main__":
    print("building program ...")
    build_program()
    print("built ok")


# version 08
# speedup vs baseline: 1.3631x; 1.0179x over previous
import contextlib
import numpy as np
import concourse.bass as bass
import concourse.mybir as mybir
from concourse.bass_utils import run_bass_kernel_spmd

F32 = mybir.dt.float32
BF16 = mybir.dt.bfloat16
AF = mybir.ActivationFunctionType
ALU = mybir.AluOpType

NCORE = 8
TOK = 2048
D = 2048
KB = 16
H = 5632
HBK = 44
MT = 16
QT = 512
NQ = 4
G3 = 11
NG3 = HBK // G3
EPS = 1e-5
EPS16 = 16.0 * EPS
WELEM = float(H * D)
SLICE = H * D // NCORE
GCH = 1408
NGC = SLICE // (128 * GCH)

_NC_CACHE = {}
_DEBUG_Y2 = False


class Emitter:
    def __init__(self, sems, counts=None):
        self.q = {k: [] for k in ("sync", "scalar", "vector", "tensor", "gpsimd")}
        self.n = dict(counts) if counts else {name: 0 for name in sems}
        self.sems = sems

    def emit(self, eng, fn, waits=(), incs=(), ms=False):
        waits = [(s, v) for (s, v) in waits if v is not None and v > 0]
        incs = list(incs)
        after = None
        for s, k in incs:
            self.n[s] += k
            after = self.n[s]
        self.q[eng].append((fn, waits, incs))
        return after if ms else None

    def replay(self, eng_name, eng):
        for fn, waits, incs in self.q[eng_name]:
            for s, v in waits:
                eng.wait_ge(self.sems[s], v)
            ins = fn(eng)
            for s, k in incs:
                ins.then_inc(self.sems[s], k)

    def run_block(self, nc):
        with nc.Block() as block:
            @block.sync
            def _(e):
                self.replay("sync", e)

            @block.scalar
            def _(e):
                self.replay("scalar", e)

            @block.vector
            def _(e):
                self.replay("vector", e)

            @block.tensor
            def _(e):
                self.replay("tensor", e)

            @block.gpsimd
            def _(e):
                self.replay("gpsimd", e)


def build_program():
    nc = bass.Bass("TRN2", target_bir_lowering=False, debug=False)

    x_d = nc.dram_tensor("x", [TOK, D], F32, kind="ExternalInput").ap()
    w0b_d = nc.dram_tensor("w0b", [128, D], F32, kind="ExternalInput").ap()
    w1b_d = nc.dram_tensor("w1b", [128, D], F32, kind="ExternalInput").ap()
    biasb_d = nc.dram_tensor("biasb", [128, D], F32, kind="ExternalInput").ap()
    dvrep_d = nc.dram_tensor("dvrep", [128, 2], F32, kind="ExternalInput").ap()
    onesrow_d = nc.dram_tensor("onesrow", [1, 128], F32, kind="ExternalInput").ap()
    onescol_d = nc.dram_tensor("onescol", [128, 1], F32, kind="ExternalInput").ap()
    epsh_d = nc.dram_tensor("epsh", [1, 1], F32, kind="ExternalInput").ap()
    wsl_d = nc.dram_tensor("wsl", [3, SLICE], F32, kind="ExternalInput").ap()
    w1f_d = nc.dram_tensor("w1f", [H, D], F32, kind="ExternalInput").ap()
    w2f_d = nc.dram_tensor("w2f", [H, D], F32, kind="ExternalInput").ap()
    w3f_d = nc.dram_tensor("w3f", [D, H], F32, kind="ExternalInput").ap()
    out_d = nc.dram_tensor("out", [TOK, D], F32, kind="ExternalOutput").ap()
    q1t_d = nc.dram_tensor("q1t", [H, D], BF16, kind="Internal").ap()
    q2t_d = nc.dram_tensor("q2t", [H, D], BF16, kind="Internal").ap()
    q3t_d = nc.dram_tensor("q3t", [H, D], BF16, kind="Internal").ap()
    h1t_d = nc.dram_tensor("h1t", [KB * 128, TOK], BF16, kind="Internal").ap()
    h1n_d = nc.dram_tensor("h1n", [TOK, D], BF16, kind="Internal").ap()
    ccin_d = nc.dram_tensor("ccin", [128, 128], F32, kind="Internal").ap()
    ccout_d = nc.dram_tensor("ccout", [128, 128], F32, kind="Internal").ap()
    dbgy2_d = nc.dram_tensor("dbgy2", [TOK, D], F32,
                             kind="ExternalOutput" if _DEBUG_Y2 else "Internal").ap()

    sem_names = (["cv", "cs", "ct", "cg", "sK", "sDBG", "sH1", "sCC"] +
                 [f"sX{i}" for i in range(2)] + [f"sG{i}" for i in range(2)] +
                 [f"sP{i}" for i in range(3)] + [f"sT{i}" for i in range(2)] +
                 [f"sS{i}" for i in range(2)] + [f"sHT{i}" for i in range(2)] +
                 [f"sHS{i}" for i in range(2)] + [f"sHN{i}" for i in range(2)] +
                 [f"sW1{i}" for i in range(2)] + [f"sW2{i}" for i in range(2)] +
                 [f"sW3{i}" for i in range(2)] +
                 [f"sHB{i}" for i in range(2)] + [f"sO{i}" for i in range(2)])
    stack = contextlib.ExitStack()
    sems = {nm: stack.enter_context(nc.semaphore(nm)) for nm in sem_names}

    with stack:
        thc = stack.enter_context(nc.sbuf_tensor("thc", [128, 8], F32))
        onesr = stack.enter_context(nc.sbuf_tensor("onesr", [1, 128], F32))
        onesc = stack.enter_context(nc.sbuf_tensor("onesc", [128, 1], F32))
        epshs = stack.enter_context(nc.sbuf_tensor("epshs", [1, 4], F32))
        gacc = stack.enter_context(nc.sbuf_tensor("gacc", [128, 4], F32))
        thsml = stack.enter_context(nc.sbuf_tensor("thsml", [1, 4], F32))
        stats = stack.enter_context(nc.sbuf_tensor("stats", [128, 16], F32))
        st2 = stack.enter_context(nc.sbuf_tensor("st2", [128, 8], F32))
        h1Th = stack.enter_context(nc.sbuf_tensor("h1Th", [128, KB * QT], BF16))
        w1t = stack.enter_context(nc.sbuf_tensor("w1t", [128, 2 * D], BF16))
        w2t = stack.enter_context(nc.sbuf_tensor("w2t", [128, 2 * D], BF16))
        sgb = stack.enter_context(nc.sbuf_tensor("sgb", [128, 2 * QT], BF16))
        prep_st = contextlib.ExitStack()
        pin = prep_st.enter_context(nc.sbuf_tensor("pin", [128, 3 * D], F32))
        s1b = prep_st.enter_context(nc.sbuf_tensor("s1b", [128, 2 * D], BF16))
        v2b = prep_st.enter_context(nc.sbuf_tensor("v2b", [128, 2 * D], BF16))
        qnb = prep_st.enter_context(nc.sbuf_tensor("qnb", [128, 2 * D], BF16))
        qtb = prep_st.enter_context(nc.sbuf_tensor("qtb", [128, 2 * D], BF16))
        attn_st = contextlib.ExitStack()
        xin = attn_st.enter_context(nc.sbuf_tensor("xin", [128, 2 * D], F32))
        sA = attn_st.enter_context(nc.sbuf_tensor("sA", [128, D], F32))
        sB = attn_st.enter_context(nc.sbuf_tensor("sB", [128, D], F32))
        sC = attn_st.enter_context(nc.sbuf_tensor("sC", [128, D], F32))
        sD = attn_st.enter_context(nc.sbuf_tensor("sD", [128, D], F32))
        ubf = attn_st.enter_context(nc.sbuf_tensor("ubf", [128, 3 * D], BF16))
        h1nb = attn_st.enter_context(nc.sbuf_tensor("h1nb", [128, 2 * D], BF16))
        tbuf = attn_st.enter_context(nc.sbuf_tensor("tbuf", [128, 2 * D], BF16))
        w0bs = attn_st.enter_context(nc.sbuf_tensor("w0bs", [128, D], BF16))
        w1bs = attn_st.enter_context(nc.sbuf_tensor("w1bs", [128, D], BF16))
        biasbs = attn_st.enter_context(nc.sbuf_tensor("biasbs", [128, D], BF16))
        dvreps = attn_st.enter_context(nc.sbuf_tensor("dvreps", [128, 2], F32))
        dvc = attn_st.enter_context(nc.sbuf_tensor("dvc", [128, 2], F32))
        gb = attn_st.enter_context(nc.sbuf_tensor("gb", [128, 2 * GCH], F32))
        gjunk = attn_st.enter_context(nc.sbuf_tensor("gjunk", [128, GCH], BF16))
        gch = attn_st.enter_context(nc.sbuf_tensor("gch", [128, 32], F32))
        p_th = stack.enter_context(nc.psum_tensor("p_th", [1, 4], F32))
        p_bc = stack.enter_context(nc.psum_tensor("p_bc", [128, 4], F32))
        px1 = stack.enter_context(nc.psum_tensor("px1", [128, QT], F32))
        px2 = stack.enter_context(nc.psum_tensor("px2", [128, QT], F32))
        pyA = stack.enter_context(nc.psum_tensor("pyA", [128, 512], F32))
        pyB = stack.enter_context(nc.psum_tensor("pyB", [128, 512], F32))

        E = Emitter(sems)

        d_rows = None
        for dst, src in ((dvreps[:], dvrep_d), (onesr[:], onesrow_d),
                         (onesc[:], onescol_d), (epshs[0:1, 0:1], epsh_d)):
            E.emit("scalar",
                   lambda e, dst=dst, src=src: e.dma_start(out=dst, in_=src),
                   incs=[("sK", 16)])
        for dst, src in ((w0bs[:], w0b_d), (w1bs[:], w1b_d), (biasbs[:], biasb_d)):
            d_rows = E.emit("gpsimd",
                            lambda e, dst=dst, src=src: e.dma_start(out=dst, in_=src),
                            incs=[("sK", 16)], ms=True)
        ms_dvc = E.emit("vector",
               lambda e: e.tensor_scalar(dvc[:], dvreps[:], 0.9, 1.0, ALU.max, ALU.min),
               waits=[("sK", d_rows)], incs=[("cv", 1)], ms=True)
        pstr = dvc[:].ap[0][0]
        dv0_b = bass.AP(dvc, 0, [[pstr, 128], [0, D]])
        dv1_b = bass.AP(dvc, 1, [[pstr, 128], [0, D]])

        flsl = wsl_d.rearrange("a b -> (a b)")
        NCHUNK = 3 * NGC
        ms_gab = [None] * NCHUNK
        for gidx in range(NCHUNK):
            slot = gidx % 2
            w_g = [("cs", ms_gab[gidx - 2])] if gidx >= 2 else []
            msg = E.emit("sync",
                         lambda e, gidx=gidx, slot=slot: e.dma_start(
                             out=gb[:, slot * GCH:slot * GCH + GCH],
                             in_=flsl[gidx * 128 * GCH:(gidx + 1) * 128 * GCH]
                             .rearrange("(p n) -> p n", p=128)),
                         waits=w_g, incs=[(f"sG{slot}", 16)], ms=True)
            ms_gab[gidx] = E.emit("scalar",
                                  lambda e, slot=slot, gidx=gidx: e.activation(
                                      gjunk[:], gb[:, slot * GCH:slot * GCH + GCH],
                                      AF.Abs, accum_out=gch[:, gidx:gidx + 1]),
                                  waits=[(f"sG{slot}", msg)] +
                                        ([("cs", ms_gab[gidx - 2])] if gidx >= 2 else []),
                                  incs=[("cs", 1)], ms=True)
        mgs = None
        for wi in range(3):
            mgs = E.emit("vector",
                         lambda e, wi=wi: e.reduce_sum(
                             gacc[:, wi:wi + 1], gch[:, wi * NGC:(wi + 1) * NGC],
                             axis=mybir.AxisListType.X),
                         waits=[("cs", ms_gab[NCHUNK - 1])] if wi == 0 else [],
                         incs=[("cv", 1)], ms=True)
        ms_ci = E.emit("sync",
                       lambda e: e.dma_start(out=ccin_d[:, 0:4], in_=gacc[:, 0:4]),
                       waits=[("cv", mgs)], incs=[("sCC", 16)], ms=True)
        ms_cc = E.emit("gpsimd",
                       lambda e: e.collective_compute(
                           "AllReduce", ALU.add,
                           [list(range(NCORE))],
                           [ccin_d.opt()], [ccout_d.opt()]),
                       waits=[("sCC", ms_ci)], incs=[("cg", 1)], ms=True)
        ms_co = E.emit("sync",
                       lambda e: e.dma_start(out=gacc[:, 0:4], in_=ccout_d[:, 0:4]),
                       waits=[("cg", ms_cc)], incs=[("sCC", 16)], ms=True)
        ms_thp = None
        for wi in range(3):
            ms_thp = E.emit(
                "tensor",
                lambda e, wi=wi: e.matmul(p_th[0:1, wi:wi + 1], gacc[:, wi:wi + 1],
                                          onesc[:], start=True, stop=True),
                waits=[("sCC", ms_co), ("sK", d_rows)], incs=[("ct", 1)], ms=True)
        ms_thr = E.emit(
            "vector",
            lambda e: e.tensor_scalar(thsml[0:1, 0:3], p_th[0:1, 0:3],
                                      0.5 / WELEM, 0.5e-8, ALU.mult, ALU.add),
            waits=[("ct", ms_thp)], incs=[("cv", 1)], ms=True)
        ms_thb = E.emit(
            "tensor",
            lambda e: e.matmul(p_bc[:, 0:3], onesr[:], thsml[0:1, 0:3],
                               start=True, stop=True),
            waits=[("cv", ms_thr)], incs=[("ct", 1)], ms=True)
        mcp = E.emit("vector",
                     lambda e: e.tensor_copy(thc[:, 0:3], p_bc[:, 0:3]),
                     waits=[("ct", ms_thb)], incs=[("cv", 1)], ms=True)
        ms_thall = E.emit(
            "vector",
            lambda e: e.tensor_scalar(thc[:, 4:7], thc[:, 0:3], -1.0, None, ALU.mult),
            waits=[("cv", mcp)], incs=[("cv", 1)], ms=True)
        ms_th = [ms_thall, ms_thall, ms_thall]

        ms_x = [None] * MT
        ms_xfree = [None] * MT
        ms_norm = [None] * MT
        ms_h1free = [None] * MT
        ms_h1tp = [None] * MT
        ms_var = [None] * MT
        ms_sqrt = [None] * MT

        def emit_x_dma(m):
            if m >= MT:
                return
            waits = [("cv", ms_xfree[m - 2])] if m >= 2 else []
            ms_x[m] = E.emit(
                "scalar",
                lambda e, m=m: e.dma_start(
                    out=xin[:, (m % 2) * D:(m % 2) * D + D],
                    in_=x_d[m * 128:(m + 1) * 128, :]),
                waits=waits, incs=[(f"sX{m % 2}", 16)], ms=True)

        def attn_v1(m):
            xm = xin[:, (m % 2) * D:(m % 2) * D + D]
            ub = ubf[:, (m % 3) * D:(m % 3) * D + D]
            waits = [(f"sX{m % 2}", ms_x[m]), ("cv", ms_dvc)]
            E.emit("vector", lambda e: e.tensor_mul(sA[:], xm, w0bs[:]),
                   waits=waits, incs=[("cv", 1)])
            E.emit("vector",
                   lambda e: e.tensor_tensor_scan(sB[:], dv0_b, sA[:], 0.0,
                                                  ALU.mult, ALU.add),
                   incs=[("cv", 1)])
            E.emit("vector",
                   lambda e: e.tensor_tensor_scan(sC[:], dv1_b, xm, 0.0,
                                                  ALU.mult, ALU.add),
                   incs=[("cv", 1)])
            E.emit("vector", lambda e: e.tensor_mul(sD[:], sC[:], w1bs[:]),
                   incs=[("cv", 1)])
            E.emit("vector", lambda e: e.tensor_add(sC[:], sB[:], sD[:]),
                   incs=[("cv", 1)])
            ms_xfree[m] = E.emit("vector",
                                 lambda e: e.tensor_add(sA[:], xm, biasbs[:]),
                                 incs=[("cv", 1)], ms=True)
            wu = [("cv", ms_norm[m - 3])] if m >= 3 else []
            E.emit("vector",
                   lambda e: e.scalar_tensor_tensor(ub, sC[:], 1.0, sA[:],
                                                    ALU.bypass, ALU.add,
                                                    accum_out=stats[:, 0:1]),
                   waits=wu, incs=[("cv", 1)])
            mq = E.emit("vector",
                   lambda e: e.scalar_tensor_tensor(sB[:], ub, 1.0, ub,
                                                    ALU.bypass, ALU.mult,
                                                    accum_out=stats[:, 1:2]),
                   incs=[("cv", 1)], ms=True)
            o = 8 + (m % 2) * 4
            mq = E.emit("vector",
                   lambda e, o=o: e.tensor_scalar(stats[:, o:o + 1], stats[:, 0:1],
                                                  -1.0 / D, None, ALU.mult),
                   waits=[("cv", mq)], incs=[("cv", 1)], ms=True)
            mq = E.emit("vector",
                   lambda e: e.tensor_scalar(stats[:, 3:4], stats[:, 1:2],
                                             1.0 / D, EPS, ALU.mult, ALU.add),
                   waits=[("cv", mq)], incs=[("cv", 1)], ms=True)
            mq = E.emit("vector",
                   lambda e, o=o: e.tensor_scalar(stats[:, 4:5], stats[:, o:o + 1],
                                                  stats[:, o:o + 1], None, ALU.mult),
                   waits=[("cv", mq)], incs=[("cv", 1)], ms=True)
            mq = E.emit("vector",
                   lambda e: e.tensor_sub(stats[:, 5:6], stats[:, 3:4], stats[:, 4:5]),
                   waits=[("cv", mq)], incs=[("cv", 1)], ms=True)
            ms_var[m] = E.emit("vector",
                         lambda e, o=o: e.tensor_scalar(stats[:, o + 2:o + 3],
                                                        stats[:, 5:6],
                                                        1e-30, None, ALU.max),
                         waits=[("cv", mq)], incs=[("cv", 1)], ms=True)

        def attn_s1(m):
            o = 8 + (m % 2) * 4
            ms_sqrt[m] = E.emit("scalar",
                         lambda e, o=o: e.activation(stats[:, o + 3:o + 4],
                                                     stats[:, o + 2:o + 3], AF.Sqrt),
                         waits=[("cv", ms_var[m])], incs=[("cs", 1)], ms=True)

        def attn_v2(m):
            o = 8 + (m % 2) * 4
            ub = ubf[:, (m % 3) * D:(m % 3) * D + D]
            h1b = h1nb[:, (m % 2) * D:(m % 2) * D + D]
            msr = E.emit("vector",
                   lambda e, o=o: e.reciprocal(stats[:, o + 1:o + 2],
                                               stats[:, o + 3:o + 4]),
                   waits=[("cs", ms_sqrt[m])], incs=[("cv", 1)], ms=True)
            wh = ([(f"sHN{m % 2}", ms_h1free[m - 2])] if m >= 2 else []) + [("cv", msr)]
            ms_norm[m] = E.emit("vector",
                                lambda e, o=o, h1b=h1b, ub=ub: e.tensor_scalar(
                                    h1b, ub, stats[:, o:o + 1], stats[:, o + 1:o + 2],
                                    ALU.add, ALU.mult),
                                waits=wh, incs=[("cv", 1)], ms=True)

        def attn_s2(m):
            h1b = h1nb[:, (m % 2) * D:(m % 2) * D + D]
            w_tp = [("cv", ms_norm[m])] + \
                   ([(f"sHS{m % 2}", ms_h1tp[m - 2])] if m >= 2 else [])
            mtp = E.emit("scalar",
                   lambda e, m=m, h1b=h1b: e.dma_start_transpose(
                       tbuf[:, (m % 2) * D:(m % 2) * D + D]
                           .rearrange("p (t q) -> p t q", t=KB),
                       h1b),
                   waits=w_tp, incs=[(f"sHT{m % 2}", 16)], ms=True)
            ms_h1tp[m] = E.emit("scalar",
                   lambda e, m=m: e.dma_start(
                       out=h1t_d[:, m * 128:(m + 1) * 128]
                           .rearrange("(k p) q -> p k q", p=128),
                       in_=tbuf[:, (m % 2) * D:(m % 2) * D + D]
                           .rearrange("p (t q) -> p t q", t=KB)),
                   waits=[(f"sHT{m % 2}", mtp)], incs=[(f"sHS{m % 2}", 16)], ms=True)
            ms_h1free[m] = E.emit("scalar",
                                  lambda e, m=m, h1b=h1b: e.dma_start(
                                      out=h1n_d[m * 128:(m + 1) * 128, :], in_=h1b),
                                  incs=[(f"sHN{m % 2}", 16)], ms=True)

        att_v1_next = [0]
        att_fin_next = [0]

        def attn_tick():
            if att_fin_next[0] < att_v1_next[0] and att_fin_next[0] < MT:
                m = att_fin_next[0]
                attn_s1(m)
                attn_v2(m)
                attn_s2(m)
                att_fin_next[0] += 1
            if att_v1_next[0] < MT:
                m = att_v1_next[0]
                attn_v1(m)
                emit_x_dma(m + 2)
                att_v1_next[0] += 1

        jobs = []
        w1_ji = {}
        w2_ji = {}
        w3_last_ji = {}
        w3list = [(cc, b) for cc in range(3) for b in range(KB)]
        w3ptr = 0
        for k in range(HBK):
            w1_ji[k] = len(jobs)
            jobs.append((0, w1f_d[k * 128:(k + 1) * 128, :], D,
                         q1t_d[k * 128:(k + 1) * 128, :], False))
            w2_ji[k] = len(jobs)
            jobs.append((1, w2f_d[k * 128:(k + 1) * 128, :], D,
                         q2t_d[k * 128:(k + 1) * 128, :], False))
            if w3ptr < len(w3list):
                cc, b = w3list[w3ptr]
                Fc = 1536 if cc == 2 else 2048
                w3_last_ji[cc] = len(jobs)
                jobs.append((2, w3f_d[b * 128:(b + 1) * 128, cc * 2048:cc * 2048 + Fc],
                             Fc,
                             q3t_d[cc * 2048:cc * 2048 + Fc, b * 128:(b + 1) * 128]
                             .rearrange("(t p) q -> p t q", p=128), True))
                w3ptr += 1
        while w3ptr < len(w3list):
            cc, b = w3list[w3ptr]
            Fc = 1536 if cc == 2 else 2048
            w3_last_ji[cc] = len(jobs)
            jobs.append((2, w3f_d[b * 128:(b + 1) * 128, cc * 2048:cc * 2048 + Fc],
                         Fc,
                         q3t_d[cc * 2048:cc * 2048 + Fc, b * 128:(b + 1) * 128]
                         .rearrange("(t p) q -> p t q", p=128), True))
            w3ptr += 1
        NJ = len(jobs)
        ms_in = [None] * NJ
        ms_s1 = [None] * NJ
        ms_v2 = [None] * NJ
        ms_cmb = [None] * NJ
        ms_tp = [None] * NJ
        ms_st = [None] * NJ

        def q_in(ji):
            if ji >= NJ:
                return
            wi, src, F, dst, is3 = jobs[ji]
            pin_s = pin[:, (ji % 3) * D:(ji % 3) * D + F]
            w_in = []
            if ji >= 3:
                w_in += [("cs", ms_s1[ji - 3]), ("cv", ms_v2[ji - 3])]
            ms_in[ji] = E.emit("sync" if ji % 2 == 0 else "gpsimd",
                               lambda e, pin_s=pin_s, src=src: e.dma_start(
                                   out=pin_s, in_=src),
                               waits=w_in, incs=[(f"sP{ji % 3}", 16)], ms=True)

        def q_sgn(ji):
            wi, src, F, dst, is3 = jobs[ji]
            pin_s = pin[:, (ji % 3) * D:(ji % 3) * D + F]
            s1_s = s1b[:, (ji % 2) * D:(ji % 2) * D + F]
            w_s = [(f"sP{ji % 3}", ms_in[ji]), ("cv", ms_th[wi])] + \
                  ([("cv", ms_cmb[ji - 2])] if ji >= 2 else [])
            ms_s1[ji] = E.emit("scalar",
                               lambda e, s1_s=s1_s, pin_s=pin_s, wi=wi: e.sign(
                                   s1_s, pin_s, bias=thc[:, 4 + wi:5 + wi]),
                               waits=w_s, incs=[("cs", 1)], ms=True)

        def q_lt(ji):
            wi, src, F, dst, is3 = jobs[ji]
            pin_s = pin[:, (ji % 3) * D:(ji % 3) * D + F]
            v2_s = v2b[:, (ji % 2) * D:(ji % 2) * D + F]
            w_v = [(f"sP{ji % 3}", ms_in[ji]), ("cv", ms_th[wi])] + \
                  ([("cv", ms_cmb[ji - 2])] if ji >= 2 else [])
            ms_v2[ji] = E.emit("vector",
                               lambda e, v2_s=v2_s, pin_s=pin_s, wi=wi: e.tensor_scalar(
                                   v2_s, pin_s, thc[:, 4 + wi:5 + wi], 0.5,
                                   ALU.is_lt, ALU.subtract),
                               waits=w_v, incs=[("cv", 1)], ms=True)

        def q_cmb(ji):
            if ji < 0 or ji >= NJ:
                return
            wi, src, F, dst, is3 = jobs[ji]
            s1_s = s1b[:, (ji % 2) * D:(ji % 2) * D + F]
            v2_s = v2b[:, (ji % 2) * D:(ji % 2) * D + F]
            qn_s = qnb[:, (ji % 2) * D:(ji % 2) * D + F]
            w_c = [("cs", ms_s1[ji])] + ([(f"sT{ji % 2}", ms_tp[ji - 2])] if ji >= 2 else [])
            ms_cmb[ji] = E.emit("vector",
                                lambda e, qn_s=qn_s, v2_s=v2_s, s1_s=s1_s:
                                e.scalar_tensor_tensor(qn_s, v2_s, -2.0, s1_s,
                                                       ALU.mult, ALU.add),
                                waits=w_c, incs=[("cv", 1)], ms=True)

        def q_tp(ji):
            if ji < 0 or ji >= NJ:
                return
            wi, src, F, dst, is3 = jobs[ji]
            qn_s = qnb[:, (ji % 2) * D:(ji % 2) * D + F]
            qt_s = qtb[:, (ji % 2) * D:(ji % 2) * D + F]
            w_t = [("cv", ms_cmb[ji])] + ([(f"sS{ji % 2}", ms_st[ji - 2])] if ji >= 2 else [])
            nt = F // 128
            ms_tp[ji] = E.emit("sync",
                               lambda e, qt_s=qt_s, qn_s=qn_s, nt=nt: e.dma_start_transpose(
                                   qt_s.rearrange("p (t q) -> p t q", t=nt), qn_s),
                               waits=w_t, incs=[(f"sT{ji % 2}", 16)], ms=True)

        def q_st(ji):
            if ji < 0 or ji >= NJ:
                return
            wi, src, F, dst, is3 = jobs[ji]
            qt_s = qtb[:, (ji % 2) * D:(ji % 2) * D + F]
            nt = F // 128
            src_st = qt_s.rearrange("p (t q) -> p t q", t=nt) if is3 else qt_s
            ms_st[ji] = E.emit("gpsimd",
                               lambda e, dst=dst, src_st=src_st: e.dma_start(
                                   out=dst, in_=src_st),
                               waits=[(f"sT{ji % 2}", ms_tp[ji])],
                               incs=[(f"sS{ji % 2}", 16)], ms=True)

        job_next = [0]

        def q_tick():
            ji = job_next[0]
            if ji >= NJ + 2:
                return False
            if ji == 0:
                q_in(0)
                q_in(1)
            q_in(ji + 2)
            q_st(ji - 2)
            if ji < NJ:
                q_sgn(ji)
                q_lt(ji)
            q_cmb(ji - 1)
            q_tp(ji - 1)
            job_next[0] += 1
            return True

        def prep_done_waits():
            w = [("cs", ms_s1[NJ - 1]), ("cv", ms_cmb[NJ - 1])]
            w += [(f"sP{i}", E.n[f"sP{i}"]) for i in range(3)]
            w += [(f"sT{i}", E.n[f"sT{i}"]) for i in range(2)]
            w += [(f"sS{i}", E.n[f"sS{i}"]) for i in range(2)]
            return w

        emit_x_dma(0)
        emit_x_dma(1)
        while att_fin_next[0] < MT:
            attn_tick()
        for _ in range(8):
            q_tick()

        attn_waits = ([(f"sHT{i}", E.n[f"sHT{i}"]) for i in range(2)] +
                      [(f"sHS{i}", E.n[f"sHS{i}"]) for i in range(2)] +
                      [(f"sHN{i}", E.n[f"sHN{i}"]) for i in range(2)] +
                      [(f"sX{i}", E.n[f"sX{i}"]) for i in range(2)] +
                      [(f"sG{i}", E.n[f"sG{i}"]) for i in range(2)] +
                      [("cs", ms_gab[NCHUNK - 1])])
        attn_st.close()
        post_st = contextlib.ExitStack()
        hTq = post_st.enter_context(nc.sbuf_tensor("hTq", [128, HBK * QT], BF16))
        y2acc = post_st.enter_context(nc.sbuf_tensor("y2acc", [128, 4 * D], F32))
        w3t = post_st.enter_context(nc.sbuf_tensor("w3t", [128, 2 * G3 * 512], BF16))
        first_post = [True]

        ms_h1q = [None]
        ms_w1 = {}
        ms_w2 = {}
        ms_p1 = {}
        ms_p2 = {}
        ms_sig = {}
        ms_hT = {}
        ms_w3l = {}
        ms_f3u = {}
        ms_ev = {}
        ms_f3last = {}
        ln2_first = [True]
        ms_ln2free = [None] * 4
        ms_outd = [None] * 4
        ln2_bufs = {}

        def emit_h1q(q):
            w = [("cv", ms_norm[4 * q + 3]),
                 (f"sHS{(4 * q + 3) % 2}", ms_h1tp[4 * q + 3]),
                 (f"sHS{(4 * q + 2) % 2}", ms_h1tp[4 * q + 2])]
            if q > 0:
                w.append(("ct", ms_p2[(q - 1, HBK - 1)]))
            ms_h1q[0] = E.emit(
                "sync",
                lambda e, q=q: e.dma_start(
                    out=h1Th[:].rearrange("p (k n) -> p k n", k=KB),
                    in_=h1t_d[:, q * QT:(q + 1) * QT]
                        .rearrange("(k p) n -> p k n", p=128)),
                waits=w, incs=[("sH1", 16)], ms=True)

        def emit_w_load(q, hb):
            if hb >= HBK:
                return
            j1, j2 = w1_ji[hb], w2_ji[hb]
            assert ms_st[j1] is not None and ms_st[j2] is not None, (hb, j1, j2)
            w_1 = [(f"sS{j1 % 2}", ms_st[j1])]
            prev = (q, hb - 2) if hb >= 2 else ((q - 1, HBK - 2 + hb) if q > 0 else None)
            if prev in ms_p1:
                w_1.append(("ct", ms_p1[prev]))
            ms_w1[(q, hb)] = E.emit(
                "sync",
                lambda e, hb=hb: e.dma_start(
                    out=w1t[:, (hb % 2) * D:(hb % 2) * D + D],
                    in_=q1t_d[hb * 128:(hb + 1) * 128, :]),
                waits=w_1, incs=[(f"sW1{hb % 2}", 16)], ms=True)
            w_2 = [(f"sS{j2 % 2}", ms_st[j2])]
            if prev in ms_p2:
                w_2.append(("ct", ms_p2[prev]))
            ms_w2[(q, hb)] = E.emit(
                "sync",
                lambda e, hb=hb: e.dma_start(
                    out=w2t[:, (hb % 2) * D:(hb % 2) * D + D],
                    in_=q2t_d[hb * 128:(hb + 1) * 128, :]),
                waits=w_2, incs=[(f"sW2{hb % 2}", 16)], ms=True)

        def emit_f12(q, hb):
            wslab1 = w1t[:, (hb % 2) * D:(hb % 2) * D + D]
            wslab2 = w2t[:, (hb % 2) * D:(hb % 2) * D + D]
            w_mm = [(f"sW1{hb % 2}", ms_w1[(q, hb)]), ("sH1", ms_h1q[0])]
            prevsig = (q, hb - 1) if hb >= 1 else ((q - 1, HBK - 1) if q > 0 else None)
            if prevsig in ms_sig:
                w_mm.append(("cs", ms_sig[prevsig]))
            for k in range(KB):
                def mk(e, k=k, wslab=wslab1):
                    return e.matmul(px1[:], wslab[:, k * 128:(k + 1) * 128],
                                    h1Th[:, k * QT:(k + 1) * QT],
                                    start=(k == 0), stop=(k == KB - 1))
                if k == 0:
                    E.emit("tensor", mk, waits=w_mm)
                elif k == KB - 1:
                    ms_p1[(q, hb)] = E.emit("tensor", mk, incs=[("ct", 1)], ms=True)
                else:
                    E.emit("tensor", mk)
            w_mm2 = [(f"sW2{hb % 2}", ms_w2[(q, hb)])]
            prevhT = (q, hb - 1) if hb >= 1 else ((q - 1, HBK - 1) if q > 0 else None)
            if prevhT in ms_hT:
                w_mm2.append(("cv", ms_hT[prevhT]))
            for k in range(KB):
                def mk2(e, k=k, wslab=wslab2):
                    return e.matmul(px2[:], wslab[:, k * 128:(k + 1) * 128],
                                    h1Th[:, k * QT:(k + 1) * QT],
                                    start=(k == 0), stop=(k == KB - 1))
                if k == 0:
                    E.emit("tensor", mk2, waits=w_mm2)
                elif k == KB - 1:
                    ms_p2[(q, hb)] = E.emit("tensor", mk2, incs=[("ct", 1)], ms=True)
                else:
                    E.emit("tensor", mk2)
            ms_sig[(q, hb)] = E.emit(
                "scalar",
                lambda e, hb=hb: e.activation(
                    sgb[:, (hb % 2) * QT:(hb % 2) * QT + QT],
                    px1[:], AF.Sigmoid, scale=0.5),
                waits=[("ct", ms_p1[(q, hb)])], incs=[("cs", 1)], ms=True)
            w_ht = [("cs", ms_sig[(q, hb)]), ("ct", ms_p2[(q, hb)])]
            prevf3 = (q - 1, hb // G3)
            if prevf3 in ms_f3last:
                w_ht.append(("ct", ms_f3last[prevf3]))
            if first_post[0]:
                first_post[0] = False
                w_ht = attn_waits + w_ht
            ms_hT[(q, hb)] = E.emit(
                "vector",
                lambda e, hb=hb: e.tensor_mul(
                    hTq[:, hb * QT:(hb + 1) * QT],
                    sgb[:, (hb % 2) * QT:(hb % 2) * QT + QT], px2[:]),
                waits=w_ht, incs=[("cv", 1)], ms=True)

        def emit_w3_load(q, g, dq, prev):
            w = []
            if prev in ms_f3u:
                w.append(("ct", ms_f3u[prev]))
            lo_c = (g * G3 * 128) // 2048
            hi_c = ((g + 1) * G3 * 128 - 1) // 2048
            for ccx in range(lo_c, hi_c + 1):
                jid = w3_last_ji[ccx]
                assert ms_st[jid] is not None, (g, dq, ccx, jid)
                w.append((f"sS{jid % 2}", ms_st[jid]))
            ms_w3l[(q, g, dq)] = E.emit(
                "sync",
                lambda e, g=g, dq=dq: e.dma_start(
                    out=w3t[:, (dq % 2) * G3 * 512:(dq % 2) * G3 * 512 + G3 * 512]
                        .rearrange("p (t n) -> p t n", t=G3),
                    in_=q3t_d[g * G3 * 128:(g + 1) * G3 * 128, dq * 512:(dq + 1) * 512]
                        .rearrange("(t p) n -> p t n", p=128)),
                waits=w, incs=[(f"sW3{dq % 2}", 16)], ms=True)

        def emit_f3(q, g):
            unit = 0
            msu = None
            for dq in range(4):
                if dq == 2:
                    emit_w3_load(q, g, 2, (q, g, 0))
                    emit_w3_load(q, g, 3, (q, g, 1))
                for m in range(4):
                    pp = pyA if unit % 2 == 0 else pyB
                    w_u = [("cv", ms_hT[(q, (g + 1) * G3 - 1)]),
                           (f"sW3{dq % 2}", ms_w3l[(q, g, dq)])]
                    prev_ev = None
                    if unit >= 2:
                        prev_ev = ms_ev.get((q, g, unit - 2))
                    elif g >= 1:
                        prev_ev = ms_ev.get((q, g - 1, 14 + unit))
                    elif q > 0:
                        prev_ev = ms_ev.get((q - 1, NG3 - 1, 14 + unit))
                    if prev_ev is not None:
                        w_u.append(("cv", prev_ev))
                    for kk in range(G3):
                        def mk3(e, kk=kk, m=m, dq=dq, g=g, pp=pp):
                            return e.matmul(
                                pp[:],
                                hTq[:, (g * G3 + kk) * QT + m * 128:
                                       (g * G3 + kk) * QT + (m + 1) * 128],
                                w3t[:, (dq % 2) * G3 * 512 + kk * 512:
                                       (dq % 2) * G3 * 512 + kk * 512 + 512],
                                start=(kk == 0), stop=(kk == G3 - 1))
                        if kk == 0:
                            E.emit("tensor", mk3, waits=w_u)
                        elif kk == G3 - 1:
                            msu = E.emit("tensor", mk3, incs=[("ct", 1)], ms=True)
                        else:
                            E.emit("tensor", mk3)
                    ms_f3u[(q, g, dq)] = msu
                    ydst = y2acc[:, m * D + dq * 512: m * D + dq * 512 + 512]
                    w_e = [("ct", msu)]
                    if g == 0:
                        mse = E.emit("vector",
                                     lambda e, ydst=ydst, pp=pp: e.tensor_copy(ydst, pp[:]),
                                     waits=w_e, incs=[("cv", 1)], ms=True)
                    else:
                        mse = E.emit("vector",
                                     lambda e, ydst=ydst, pp=pp: e.scalar_tensor_tensor(
                                         ydst, pp[:], 1.0, ydst, ALU.bypass, ALU.add),
                                     waits=w_e, incs=[("cv", 1)], ms=True)
                    ms_ev[(q, g, unit)] = mse
                    unit += 1
            if g + 1 < NG3:
                emit_w3_load(q, g + 1, 0, (q, g, 2))
                emit_w3_load(q, g + 1, 1, (q, g, 3))
            elif q + 1 < NQ:
                emit_w3_load(q + 1, 0, 0, (q, g, 2))
                emit_w3_load(q + 1, 0, 1, (q, g, 3))
            ms_f3last[(q, g)] = msu

        def alloc_ln2():
            ln2_bufs["uU"] = pin[:, 0:D]
            ln2_bufs["outb"] = pin[:, D:3 * D]
            ln2_bufs["h1nb2"] = s1b[:, 0:2 * D]

        def emit_ln2_m(q, m):
            uU_a = ln2_bufs["uU"]
            h1nb2_a = ln2_bufs["h1nb2"]
            outb_a = ln2_bufs["outb"]
            last_ev = ms_ev[(q, NG3 - 1, 15)]
            if True:
                mg = q * 4 + m
                w_hn = [("cv", ms_ln2free[(m - 2) % 4])] if m >= 2 else []
                if ln2_first[0]:
                    ln2_first[0] = False
                    w_hn = prep_done_waits() + w_hn
                elif m < 2 and q > 0:
                    w_hn = [("cv", ms_ln2free[(m + 2) % 4])] + w_hn
                ms_hnm = E.emit(
                    "sync",
                    lambda e, mg=mg, m=m: e.dma_start(
                        out=h1nb2_a[:, (m % 2) * D:(m % 2) * D + D],
                        in_=h1n_d[mg * 128:(mg + 1) * 128, :]),
                    waits=w_hn, incs=[(f"sHB{m % 2}", 16)], ms=True)
                h1s = h1nb2_a[:, (m % 2) * D:(m % 2) * D + D]
                yv = y2acc[:, m * D:(m + 1) * D]
                ob = outb_a[:, (m % 2) * D:(m % 2) * D + D]
                E.emit("vector",
                       lambda e, h1s=h1s, yv=yv: e.scalar_tensor_tensor(
                           uU_a, h1s, 4.0, yv, ALU.mult, ALU.add,
                           accum_out=st2[:, 0:1]),
                       waits=[(f"sHB{m % 2}", ms_hnm), ("cv", last_ev)],
                       incs=[("cv", 1)])
                w_sq = [(f"sO{m % 2}", ms_outd[(m - 2) % 4])] if m >= 2 else \
                       ([(f"sO{m % 2}", ms_outd[(m + 2) % 4])] if q > 0 else [])
                mq2 = E.emit("vector",
                       lambda e, ob=ob: e.scalar_tensor_tensor(
                           ob, uU_a, 1.0, uU_a, ALU.bypass, ALU.mult,
                           accum_out=st2[:, 1:2]),
                       waits=w_sq, incs=[("cv", 1)], ms=True)
                mq2 = E.emit("vector",
                       lambda e: e.tensor_scalar(st2[:, 2:3], st2[:, 0:1],
                                                 -1.0 / D, None, ALU.mult),
                       waits=[("cv", mq2)], incs=[("cv", 1)], ms=True)
                mq2 = E.emit("vector",
                       lambda e: e.tensor_scalar(st2[:, 3:4], st2[:, 1:2],
                                                 1.0 / D, EPS16, ALU.mult, ALU.add),
                       waits=[("cv", mq2)], incs=[("cv", 1)], ms=True)
                mq2 = E.emit("vector",
                       lambda e: e.tensor_scalar(st2[:, 4:5], st2[:, 2:3],
                                                 st2[:, 2:3], None, ALU.mult),
                       waits=[("cv", mq2)], incs=[("cv", 1)], ms=True)
                mq2 = E.emit("vector",
                       lambda e: e.tensor_sub(st2[:, 5:6], st2[:, 3:4], st2[:, 4:5]),
                       waits=[("cv", mq2)], incs=[("cv", 1)], ms=True)
                msv2 = E.emit("vector",
                              lambda e: e.tensor_scalar(st2[:, 5:6], st2[:, 5:6],
                                                        1e-30, None, ALU.max),
                              waits=[("cv", mq2)], incs=[("cv", 1)], ms=True)
                mss2 = E.emit("scalar",
                              lambda e: e.activation(st2[:, 6:7], st2[:, 5:6], AF.Sqrt),
                              waits=[("cv", msv2)], incs=[("cs", 1)], ms=True)
                msr2 = E.emit("vector", lambda e: e.reciprocal(st2[:, 7:8], st2[:, 6:7]),
                       waits=[("cs", mss2)], incs=[("cv", 1)], ms=True)
                ms_ln2free[m % 4] = E.emit(
                    "vector",
                    lambda e, ob=ob: e.tensor_scalar(ob, uU_a, st2[:, 2:3],
                                                     st2[:, 7:8],
                                                     ALU.add, ALU.mult),
                    waits=[("cv", msr2)], incs=[("cv", 1)], ms=True)
                ms_outd[m % 4] = E.emit(
                    "sync",
                    lambda e, mg=mg, m=m: e.dma_start(
                        out=out_d[mg * 128:(mg + 1) * 128, :],
                        in_=outb_a[:, (m % 2) * D:(m % 2) * D + D]),
                    waits=[("cv", ms_ln2free[m % 4])], incs=[(f"sO{m % 2}", 16)], ms=True)
                if _DEBUG_Y2:
                    E.emit("sync",
                           lambda e, mg=mg, m=m: e.dma_start(
                               out=dbgy2_d[mg * 128:(mg + 1) * 128, :],
                               in_=y2acc[:, m * D:(m + 1) * D]),
                           waits=[("cv", ms_ev[(q, NG3 - 1, 15)])],
                           incs=[("sDBG", 16)], ms=True)

        with post_st:
            alloc_ln2()
            for q in range(NQ):
                emit_h1q(q)
                emit_w_load(q, 0)
                emit_w_load(q, 1)
                for hb in range(HBK):
                    emit_f12(q, hb)
                    if q == 0:
                        q_tick()
                        q_tick()
                        q_tick()
                    emit_w_load(q, hb + 2)
                    if q > 0 and hb in (4, 14, 24, 34):
                        emit_ln2_m(q - 1, (hb - 4) // 10)
                if q == 0:
                    while job_next[0] < NJ + 2:
                        q_tick()
                if q == 0:
                    emit_w3_load(0, 0, 0, None)
                    emit_w3_load(0, 0, 1, None)
                for g in range(NG3):
                    emit_f3(q, g)
            emit_ln2_m(NQ - 1, 0)
            emit_ln2_m(NQ - 1, 1)
            emit_ln2_m(NQ - 1, 2)
            emit_ln2_m(NQ - 1, 3)

            all_dma = [s for s in sem_names if s.startswith("s")]
            E.emit("sync", lambda e: e.nop(),
                   waits=[(nm, E.n[nm]) for nm in all_dma if E.n[nm] > 0])
            E.run_block(nc)
        prep_st.close()
    return nc


def kernel(L0_attn_x, L0_attn_weight, L0_attn_bias, L0_attn_decay_value,
           L0_ffn_w1, L0_ffn_w2, L0_ffn_w3):
    x = np.ascontiguousarray(np.asarray(L0_attn_x, dtype=np.float32)
                             .reshape(NCORE * TOK, D))
    wgt = np.asarray(L0_attn_weight, dtype=np.float32)
    bias = np.asarray(L0_attn_bias, dtype=np.float32)
    dv = np.asarray(L0_attn_decay_value, dtype=np.float32)
    w1 = np.ascontiguousarray(np.asarray(L0_ffn_w1, dtype=np.float32))
    w2 = np.ascontiguousarray(np.asarray(L0_ffn_w2, dtype=np.float32))
    w3 = np.ascontiguousarray(np.asarray(L0_ffn_w3, dtype=np.float32))

    w0b = np.ascontiguousarray(np.broadcast_to(wgt[0:1, :], (128, D)))
    w1b = np.ascontiguousarray(np.broadcast_to(wgt[1:2, :], (128, D)))
    biasb = np.ascontiguousarray(np.broadcast_to(bias.reshape(1, D), (128, D)))
    dvrep = np.ascontiguousarray(np.broadcast_to(dv.reshape(1, 2), (128, 2)))
    onesrow = np.ones((1, 128), np.float32)
    onescol = np.ones((128, 1), np.float32)
    epsh = np.full((1, 1), 0.5e-8, np.float32)
    wflat = [w1.reshape(-1), w2.reshape(-1), w3.reshape(-1)]

    if "prog" not in _NC_CACHE:
        _NC_CACHE["prog"] = build_program()
    nc = _NC_CACHE["prog"]

    in_maps = []
    for c in range(NCORE):
        wsl = np.stack([wf[c * SLICE:(c + 1) * SLICE] for wf in wflat])
        in_maps.append({
            "x": x[c * TOK:(c + 1) * TOK],
            "w0b": w0b, "w1b": w1b, "biasb": biasb, "dvrep": dvrep,
            "onesrow": onesrow, "onescol": onescol, "epsh": epsh,
            "wsl": np.ascontiguousarray(wsl),
            "w1f": w1, "w2f": w2, "w3f": w3,
        })
    res = run_bass_kernel_spmd(nc, in_maps, core_ids=list(range(NCORE)))
    out = np.concatenate([np.asarray(r["out"]) for r in res.results], axis=0)
    return out.reshape(2, 8192, D)


if __name__ == "__main__":
    print("building program ...")
    build_program()
    print("built ok")


# version 11
# speedup vs baseline: 1.3766x; 1.0099x over previous
import contextlib
import numpy as np
import concourse.bass as bass
import concourse.mybir as mybir
from concourse.bass_utils import run_bass_kernel_spmd

F32 = mybir.dt.float32
BF16 = mybir.dt.bfloat16
AF = mybir.ActivationFunctionType
ALU = mybir.AluOpType

NCORE = 8
TOK = 2048
D = 2048
KB = 16
H = 5632
HBK = 44
MT = 16
QT = 512
NQ = 4
G3 = 11
NG3 = HBK // G3
EPS = 1e-5
EPS16 = 16.0 * EPS
WELEM = float(H * D)
SLICE = H * D // NCORE
GCH = 1408
NGC = SLICE // (128 * GCH)

_NC_CACHE = {}
_DEBUG_Y2 = False


class Emitter:
    def __init__(self, sems, counts=None):
        self.q = {k: [] for k in ("sync", "scalar", "vector", "tensor", "gpsimd")}
        self.n = dict(counts) if counts else {name: 0 for name in sems}
        self.sems = sems

    def emit(self, eng, fn, waits=(), incs=(), ms=False):
        waits = [(s, v) for (s, v) in waits if v is not None and v > 0]
        incs = list(incs)
        after = None
        for s, k in incs:
            self.n[s] += k
            after = self.n[s]
        self.q[eng].append((fn, waits, incs))
        return after if ms else None

    def replay(self, eng_name, eng):
        for fn, waits, incs in self.q[eng_name]:
            for s, v in waits:
                eng.wait_ge(self.sems[s], v)
            ins = fn(eng)
            for s, k in incs:
                ins.then_inc(self.sems[s], k)

    def run_block(self, nc):
        with nc.Block() as block:
            @block.sync
            def _(e):
                self.replay("sync", e)

            @block.scalar
            def _(e):
                self.replay("scalar", e)

            @block.vector
            def _(e):
                self.replay("vector", e)

            @block.tensor
            def _(e):
                self.replay("tensor", e)

            @block.gpsimd
            def _(e):
                self.replay("gpsimd", e)


def build_program():
    nc = bass.Bass("TRN2", target_bir_lowering=False, debug=False)

    x_d = nc.dram_tensor("x", [TOK, D], F32, kind="ExternalInput").ap()
    w0b_d = nc.dram_tensor("w0b", [128, D], F32, kind="ExternalInput").ap()
    w1b_d = nc.dram_tensor("w1b", [128, D], F32, kind="ExternalInput").ap()
    biasb_d = nc.dram_tensor("biasb", [128, D], F32, kind="ExternalInput").ap()
    dvrep_d = nc.dram_tensor("dvrep", [128, 2], F32, kind="ExternalInput").ap()
    onesrow_d = nc.dram_tensor("onesrow", [1, 128], F32, kind="ExternalInput").ap()
    onescol_d = nc.dram_tensor("onescol", [128, 1], F32, kind="ExternalInput").ap()
    epsh_d = nc.dram_tensor("epsh", [1, 1], F32, kind="ExternalInput").ap()
    wsl_d = nc.dram_tensor("wsl", [3, SLICE], F32, kind="ExternalInput").ap()
    w1f_d = nc.dram_tensor("w1f", [H, D], F32, kind="ExternalInput").ap()
    w2f_d = nc.dram_tensor("w2f", [H, D], F32, kind="ExternalInput").ap()
    w3f_d = nc.dram_tensor("w3f", [D, H], F32, kind="ExternalInput").ap()
    out_d = nc.dram_tensor("out", [TOK, D], F32, kind="ExternalOutput").ap()
    q1t_d = nc.dram_tensor("q1t", [H, D], BF16, kind="Internal").ap()
    q2t_d = nc.dram_tensor("q2t", [H, D], BF16, kind="Internal").ap()
    q3t_d = nc.dram_tensor("q3t", [H, D], BF16, kind="Internal").ap()
    h1t_d = nc.dram_tensor("h1t", [KB * 128, TOK], BF16, kind="Internal").ap()
    h1n_d = nc.dram_tensor("h1n", [TOK, D], BF16, kind="Internal").ap()
    ccin_d = nc.dram_tensor("ccin", [128, 4], F32, kind="Internal").ap()
    ccout_d = nc.dram_tensor("ccout", [128, 4], F32, kind="Internal").ap()
    dbgy2_d = nc.dram_tensor("dbgy2", [TOK, D], F32,
                             kind="ExternalOutput" if _DEBUG_Y2 else "Internal").ap()
    dbgth_d = nc.dram_tensor("dbgth", [128, 64], F32,
                             kind="ExternalOutput" if _DEBUG_Y2 else "Internal").ap()

    sem_names = (["cv", "cs", "ct", "cg", "sK", "sDBG", "sH1", "sCC", "sCC2"] +
                 [f"sX{i}" for i in range(2)] + [f"sG{i}" for i in range(2)] +
                 [f"sP{i}" for i in range(3)] + [f"sT{i}" for i in range(2)] +
                 [f"sS{i}" for i in range(2)] + [f"sHT{i}" for i in range(2)] +
                 [f"sHS{i}" for i in range(2)] + [f"sHN{i}" for i in range(2)] +
                 [f"sW1{i}" for i in range(2)] + [f"sW2{i}" for i in range(2)] +
                 [f"sW3{i}" for i in range(2)] +
                 [f"sHB{i}" for i in range(2)] + [f"sO{i}" for i in range(2)])
    stack = contextlib.ExitStack()
    sems = {nm: stack.enter_context(nc.semaphore(nm)) for nm in sem_names}

    with stack:
        thc = stack.enter_context(nc.sbuf_tensor("thc", [128, 8], F32))
        onesr = stack.enter_context(nc.sbuf_tensor("onesr", [1, 128], F32))
        onesc = stack.enter_context(nc.sbuf_tensor("onesc", [128, 1], F32))
        epshs = stack.enter_context(nc.sbuf_tensor("epshs", [1, 4], F32))
        gacc = stack.enter_context(nc.sbuf_tensor("gacc", [128, 4], F32))
        thsml = stack.enter_context(nc.sbuf_tensor("thsml", [1, 4], F32))
        stats = stack.enter_context(nc.sbuf_tensor("stats", [128, 16], F32))
        st2 = stack.enter_context(nc.sbuf_tensor("st2", [128, 8], F32))
        h1Th = stack.enter_context(nc.sbuf_tensor("h1Th", [128, KB * QT], BF16))
        w1t = stack.enter_context(nc.sbuf_tensor("w1t", [128, 2 * D], BF16))
        w2t = stack.enter_context(nc.sbuf_tensor("w2t", [128, 2 * D], BF16))
        sgb = stack.enter_context(nc.sbuf_tensor("sgb", [128, 2 * QT], BF16))
        prep_st = contextlib.ExitStack()
        pin = prep_st.enter_context(nc.sbuf_tensor("pin", [128, 3 * D], F32))
        s1b = prep_st.enter_context(nc.sbuf_tensor("s1b", [128, 2 * D], BF16))
        v2b = prep_st.enter_context(nc.sbuf_tensor("v2b", [128, 2 * D], BF16))
        qnb = prep_st.enter_context(nc.sbuf_tensor("qnb", [128, 2 * D], BF16))
        qtb = prep_st.enter_context(nc.sbuf_tensor("qtb", [128, 2 * D], BF16))
        attn_st = contextlib.ExitStack()
        xin = attn_st.enter_context(nc.sbuf_tensor("xin", [128, 2 * D], F32))
        sA = attn_st.enter_context(nc.sbuf_tensor("sA", [128, D], F32))
        sB = attn_st.enter_context(nc.sbuf_tensor("sB", [128, D], F32))
        sC = attn_st.enter_context(nc.sbuf_tensor("sC", [128, D], F32))
        sD = attn_st.enter_context(nc.sbuf_tensor("sD", [128, D], F32))
        ubf = attn_st.enter_context(nc.sbuf_tensor("ubf", [128, 3 * D], BF16))
        h1nb = attn_st.enter_context(nc.sbuf_tensor("h1nb", [128, 2 * D], BF16))
        tbuf = attn_st.enter_context(nc.sbuf_tensor("tbuf", [128, 2 * D], BF16))
        w0bs = attn_st.enter_context(nc.sbuf_tensor("w0bs", [128, D], BF16))
        w1bs = attn_st.enter_context(nc.sbuf_tensor("w1bs", [128, D], BF16))
        biasbs = attn_st.enter_context(nc.sbuf_tensor("biasbs", [128, D], BF16))
        dvreps = attn_st.enter_context(nc.sbuf_tensor("dvreps", [128, 2], F32))
        dvc = attn_st.enter_context(nc.sbuf_tensor("dvc", [128, 2], F32))
        gb = attn_st.enter_context(nc.sbuf_tensor("gb", [128, 2 * GCH], F32))
        gjunk = attn_st.enter_context(nc.sbuf_tensor("gjunk", [128, GCH], BF16))
        gch = attn_st.enter_context(nc.sbuf_tensor("gch", [128, 32], F32))
        p_th = stack.enter_context(nc.psum_tensor("p_th", [1, 4], F32))
        p_bc = stack.enter_context(nc.psum_tensor("p_bc", [128, 4], F32))
        px1 = stack.enter_context(nc.psum_tensor("px1", [128, QT], F32))
        px2 = stack.enter_context(nc.psum_tensor("px2", [128, QT], F32))
        pyA = stack.enter_context(nc.psum_tensor("pyA", [128, 512], F32))
        pyB = stack.enter_context(nc.psum_tensor("pyB", [128, 512], F32))

        E = Emitter(sems)

        d_rows = None
        for dst, src in ((dvreps[:], dvrep_d), (onesr[:], onesrow_d),
                         (onesc[:], onescol_d), (epshs[0:1, 0:1], epsh_d)):
            E.emit("scalar",
                   lambda e, dst=dst, src=src: e.dma_start(out=dst, in_=src),
                   incs=[("sK", 16)])
        for dst, src in ((w0bs[:], w0b_d), (w1bs[:], w1b_d), (biasbs[:], biasb_d)):
            d_rows = E.emit("gpsimd",
                            lambda e, dst=dst, src=src: e.dma_start(out=dst, in_=src),
                            incs=[("sK", 16)], ms=True)
        ms_dvc = E.emit("vector",
               lambda e: e.tensor_scalar(dvc[:], dvreps[:], 0.9, 1.0, ALU.max, ALU.min),
               waits=[("sK", d_rows)], incs=[("cv", 1)], ms=True)
        pstr = dvc[:].ap[0][0]
        dv0_b = bass.AP(dvc, 0, [[pstr, 128], [0, D]])
        dv1_b = bass.AP(dvc, 1, [[pstr, 128], [0, D]])

        flsl = wsl_d.rearrange("a b -> (a b)")
        NCHUNK = 3 * NGC
        ms_gab = [None] * NCHUNK
        for gidx in range(NCHUNK):
            slot = gidx % 2
            w_g = [("cs", ms_gab[gidx - 2])] if gidx >= 2 else []
            msg = E.emit("sync",
                         lambda e, gidx=gidx, slot=slot: e.dma_start(
                             out=gb[:, slot * GCH:slot * GCH + GCH],
                             in_=flsl[gidx * 128 * GCH:(gidx + 1) * 128 * GCH]
                             .rearrange("(p n) -> p n", p=128)),
                         waits=w_g, incs=[(f"sG{slot}", 16)], ms=True)
            ms_gab[gidx] = E.emit("scalar",
                                  lambda e, slot=slot, gidx=gidx: e.activation(
                                      gjunk[:], gb[:, slot * GCH:slot * GCH + GCH],
                                      AF.Abs, accum_out=gch[:, gidx:gidx + 1]),
                                  waits=[(f"sG{slot}", msg)] +
                                        ([("cs", ms_gab[gidx - 2])] if gidx >= 2 else []),
                                  incs=[("cs", 1)], ms=True)
        mgs = None
        for wi in range(3):
            mgs = E.emit("vector",
                         lambda e, wi=wi: e.reduce_sum(
                             gacc[:, wi:wi + 1], gch[:, wi * NGC:(wi + 1) * NGC],
                             axis=mybir.AxisListType.X),
                         waits=[("cs", ms_gab[NCHUNK - 1])] if wi == 0 else [],
                         incs=[("cv", 1)], ms=True)
        ms_ci = E.emit("sync",
                       lambda e: e.dma_start(out=ccin_d, in_=gacc[:]),
                       waits=[("cv", mgs)], incs=[("sCC", 16)], ms=True)
        ms_cc = E.emit("gpsimd",
                       lambda e: e.collective_compute(
                           "AllReduce", ALU.add,
                           [list(range(NCORE))],
                           [ccin_d.opt()], [ccout_d.opt()]),
                       waits=[("sCC", ms_ci)], incs=[("sCC2", 1)], ms=True)
        ms_co = E.emit("sync",
                       lambda e: e.dma_start(out=gacc[:], in_=ccout_d),
                       waits=[("sCC2", ms_cc)], incs=[("sCC", 16)], ms=True)
        ms_thp = None
        for wi in range(3):
            ms_thp = E.emit(
                "tensor",
                lambda e, wi=wi: e.matmul(p_th[0:1, wi:wi + 1], gacc[:, wi:wi + 1],
                                          onesc[:], start=True, stop=True),
                waits=[("sCC", ms_co), ("sK", d_rows)], incs=[("ct", 1)], ms=True)
        ms_thr = E.emit(
            "vector",
            lambda e: e.tensor_scalar(thsml[0:1, 0:3], p_th[0:1, 0:3],
                                      0.5 / WELEM, 0.5e-8, ALU.mult, ALU.add),
            waits=[("ct", ms_thp)], incs=[("cv", 1)], ms=True)
        ms_thb = E.emit(
            "tensor",
            lambda e: e.matmul(p_bc[:, 0:3], onesr[:], thsml[0:1, 0:3],
                               start=True, stop=True),
            waits=[("cv", ms_thr)], incs=[("ct", 1)], ms=True)
        mcp = E.emit("vector",
                     lambda e: e.tensor_copy(thc[:, 0:3], p_bc[:, 0:3]),
                     waits=[("ct", ms_thb)], incs=[("cv", 1)], ms=True)
        ms_thall = E.emit(
            "vector",
            lambda e: e.tensor_scalar(thc[:, 4:7], thc[:, 0:3], -1.0, None, ALU.mult),
            waits=[("cv", mcp)], incs=[("cv", 1)], ms=True)
        ms_th = [ms_thall, ms_thall, ms_thall]
        if _DEBUG_Y2:
            E.emit("sync", lambda e: e.dma_start(out=dbgth_d[:, 0:8], in_=thc[:]),
                   waits=[("cv", ms_thall)], incs=[("sDBG", 16)])
            E.emit("sync", lambda e: e.dma_start(out=dbgth_d[:, 8:40], in_=gch[:]),
                   waits=[("cv", ms_thall)], incs=[("sDBG", 16)])
            E.emit("sync", lambda e: e.dma_start(out=dbgth_d[:, 40:44], in_=gacc[:]),
                   waits=[("cv", ms_thall)], incs=[("sDBG", 16)])

        ms_x = [None] * MT
        ms_xfree = [None] * MT
        ms_norm = [None] * MT
        ms_h1free = [None] * MT
        ms_h1tp = [None] * MT
        ms_var = [None] * MT
        ms_sqrt = [None] * MT
        ms_uc = [None] * MT
        ms_vadd = [None] * MT

        def emit_x_dma(m):
            if m >= MT:
                return
            waits = [("cg", ms_xfree[m - 2])] if m >= 2 else []
            ms_x[m] = E.emit(
                "scalar",
                lambda e, m=m: e.dma_start(
                    out=xin[:, (m % 2) * D:(m % 2) * D + D],
                    in_=x_d[m * 128:(m + 1) * 128, :]),
                waits=waits, incs=[(f"sX{m % 2}", 16)], ms=True)

        def attn_v1(m):
            xm = xin[:, (m % 2) * D:(m % 2) * D + D]
            ub = ubf[:, (m % 3) * D:(m % 3) * D + D]
            waits = [(f"sX{m % 2}", ms_x[m]), ("cv", ms_dvc), ("sK", d_rows)]
            if m >= 1:
                waits.append(("cv", ms_uc[m - 1]))
            mg_u0 = E.emit("gpsimd", lambda e: e.tensor_mul(sA[:], xm, w0bs[:]),
                           waits=waits, incs=[("cg", 1)], ms=True)
            msc_r = E.emit("vector",
                   lambda e: e.tensor_tensor_scan(sB[:], dv0_b, sA[:], 0.0,
                                                  ALU.mult, ALU.add),
                   waits=[("cg", mg_u0)], incs=[("cv", 1)], ms=True)
            w_c = [("cg", ms_xfree[m - 1])] if m >= 1 else []
            msc_c = E.emit("vector",
                   lambda e: e.tensor_tensor_scan(sC[:], dv1_b, xm, 0.0,
                                                  ALU.mult, ALU.add),
                   waits=w_c, incs=[("cv", 1)], ms=True)
            mg_t = E.emit("gpsimd", lambda e: e.tensor_mul(sD[:], sC[:], w1bs[:]),
                          waits=[("cv", msc_c)], incs=[("cg", 1)], ms=True)
            mg_xb = E.emit("gpsimd", lambda e: e.tensor_add(sA[:], xm, biasbs[:]),
                           waits=[("cv", msc_r)], incs=[("cg", 1)], ms=True)
            ms_xfree[m] = mg_xb
            ms_vadd[m] = E.emit("vector", lambda e: e.tensor_add(sC[:], sB[:], sD[:]),
                   waits=[("cg", mg_t)], incs=[("cv", 1)], ms=True)
            wu = [("cv", ms_norm[m - 3])] if m >= 3 else []
            ms_uc[m] = E.emit("vector",
                   lambda e: e.scalar_tensor_tensor(ub, sC[:], 1.0, sA[:],
                                                    ALU.bypass, ALU.add,
                                                    accum_out=stats[:, 0:1]),
                   waits=wu + [("cg", mg_xb)], incs=[("cv", 1)], ms=True)
            mq = E.emit("vector",
                   lambda e: e.scalar_tensor_tensor(sB[:], ub, 1.0, ub,
                                                    ALU.bypass, ALU.mult,
                                                    accum_out=stats[:, 1:2]),
                   incs=[("cv", 1)], ms=True)
            o = 8 + (m % 2) * 4
            mq = E.emit("vector",
                   lambda e, o=o: e.tensor_scalar(stats[:, o:o + 1], stats[:, 0:1],
                                                  -1.0 / D, None, ALU.mult),
                   waits=[("cv", mq)], incs=[("cv", 1)], ms=True)
            mq = E.emit("vector",
                   lambda e: e.tensor_scalar(stats[:, 3:4], stats[:, 1:2],
                                             1.0 / D, EPS, ALU.mult, ALU.add),
                   waits=[("cv", mq)], incs=[("cv", 1)], ms=True)
            mq = E.emit("vector",
                   lambda e, o=o: e.tensor_scalar(stats[:, 4:5], stats[:, o:o + 1],
                                                  stats[:, o:o + 1], None, ALU.mult),
                   waits=[("cv", mq)], incs=[("cv", 1)], ms=True)
            mq = E.emit("vector",
                   lambda e: e.tensor_sub(stats[:, 5:6], stats[:, 3:4], stats[:, 4:5]),
                   waits=[("cv", mq)], incs=[("cv", 1)], ms=True)
            ms_var[m] = E.emit("vector",
                         lambda e, o=o: e.tensor_scalar(stats[:, o + 2:o + 3],
                                                        stats[:, 5:6],
                                                        1e-30, None, ALU.max),
                         waits=[("cv", mq)], incs=[("cv", 1)], ms=True)

        def attn_s1(m):
            o = 8 + (m % 2) * 4
            ms_sqrt[m] = E.emit("scalar",
                         lambda e, o=o: e.activation(stats[:, o + 3:o + 4],
                                                     stats[:, o + 2:o + 3], AF.Sqrt),
                         waits=[("cv", ms_var[m])], incs=[("cs", 1)], ms=True)

        def attn_v2(m):
            o = 8 + (m % 2) * 4
            ub = ubf[:, (m % 3) * D:(m % 3) * D + D]
            h1b = h1nb[:, (m % 2) * D:(m % 2) * D + D]
            msr = E.emit("vector",
                   lambda e, o=o: e.reciprocal(stats[:, o + 1:o + 2],
                                               stats[:, o + 3:o + 4]),
                   waits=[("cs", ms_sqrt[m])], incs=[("cv", 1)], ms=True)
            wh = ([(f"sHN{m % 2}", ms_h1free[m - 2])] if m >= 2 else []) + [("cv", msr)]
            ms_norm[m] = E.emit("vector",
                                lambda e, o=o, h1b=h1b, ub=ub: e.tensor_scalar(
                                    h1b, ub, stats[:, o:o + 1], stats[:, o + 1:o + 2],
                                    ALU.add, ALU.mult),
                                waits=wh, incs=[("cv", 1)], ms=True)

        def attn_s2(m):
            h1b = h1nb[:, (m % 2) * D:(m % 2) * D + D]
            w_tp = [("cv", ms_norm[m])] + \
                   ([(f"sHS{m % 2}", ms_h1tp[m - 2])] if m >= 2 else [])
            mtp = E.emit("scalar",
                   lambda e, m=m, h1b=h1b: e.dma_start_transpose(
                       tbuf[:, (m % 2) * D:(m % 2) * D + D]
                           .rearrange("p (t q) -> p t q", t=KB),
                       h1b),
                   waits=w_tp, incs=[(f"sHT{m % 2}", 16)], ms=True)
            ms_h1tp[m] = E.emit("scalar",
                   lambda e, m=m: e.dma_start(
                       out=h1t_d[:, m * 128:(m + 1) * 128]
                           .rearrange("(k p) q -> p k q", p=128),
                       in_=tbuf[:, (m % 2) * D:(m % 2) * D + D]
                           .rearrange("p (t q) -> p t q", t=KB)),
                   waits=[(f"sHT{m % 2}", mtp)], incs=[(f"sHS{m % 2}", 16)], ms=True)
            ms_h1free[m] = E.emit("scalar",
                                  lambda e, m=m, h1b=h1b: e.dma_start(
                                      out=h1n_d[m * 128:(m + 1) * 128, :], in_=h1b),
                                  incs=[(f"sHN{m % 2}", 16)], ms=True)

        att_v1_next = [0]
        att_fin_next = [0]

        def attn_tick():
            if att_fin_next[0] < att_v1_next[0] and att_fin_next[0] < MT:
                m = att_fin_next[0]
                attn_s1(m)
                attn_v2(m)
                attn_s2(m)
                att_fin_next[0] += 1
            if att_v1_next[0] < MT:
                m = att_v1_next[0]
                attn_v1(m)
                emit_x_dma(m + 2)
                att_v1_next[0] += 1

        ms_w1 = {}
        ms_w2 = {}
        ms_p1 = {}
        ms_p2 = {}
        jobs = []
        w1_ji = {}
        w2_ji = {}
        w3_last_ji = {}
        w3list = [(cc, b) for cc in range(3) for b in range(KB)]
        w3ptr = 0
        for k in range(HBK):
            w1_ji[k] = len(jobs)
            jobs.append((0, w1f_d[k * 128:(k + 1) * 128, :], D,
                         q1t_d[k * 128:(k + 1) * 128, :], False))
            w2_ji[k] = len(jobs)
            jobs.append((1, w2f_d[k * 128:(k + 1) * 128, :], D,
                         q2t_d[k * 128:(k + 1) * 128, :], False))
            if w3ptr < len(w3list):
                cc, b = w3list[w3ptr]
                Fc = 1536 if cc == 2 else 2048
                w3_last_ji[cc] = len(jobs)
                jobs.append((2, w3f_d[b * 128:(b + 1) * 128, cc * 2048:cc * 2048 + Fc],
                             Fc,
                             q3t_d[cc * 2048:cc * 2048 + Fc, b * 128:(b + 1) * 128]
                             .rearrange("(t p) q -> p t q", p=128), True))
                w3ptr += 1
        while w3ptr < len(w3list):
            cc, b = w3list[w3ptr]
            Fc = 1536 if cc == 2 else 2048
            w3_last_ji[cc] = len(jobs)
            jobs.append((2, w3f_d[b * 128:(b + 1) * 128, cc * 2048:cc * 2048 + Fc],
                         Fc,
                         q3t_d[cc * 2048:cc * 2048 + Fc, b * 128:(b + 1) * 128]
                         .rearrange("(t p) q -> p t q", p=128), True))
            w3ptr += 1
        NJ = len(jobs)
        job_hb = {}
        for k in range(HBK):
            job_hb[w1_ji[k]] = k
            job_hb[w2_ji[k]] = k
        ms_in = [None] * NJ
        ms_s1 = [None] * NJ
        ms_v2 = [None] * NJ
        ms_cmb = [None] * NJ
        ms_tp = [None] * NJ
        ms_st = [None] * NJ

        def q_in(ji):
            if ji >= NJ:
                return
            wi, src, F, dst, is3 = jobs[ji]
            pin_s = pin[:, (ji % 3) * D:(ji % 3) * D + F]
            w_in = []
            if ji >= 3:
                w_in += [("cs", ms_s1[ji - 3]), ("cv", ms_v2[ji - 3])]
            ms_in[ji] = E.emit("sync" if ji % 2 == 0 else "gpsimd",
                               lambda e, pin_s=pin_s, src=src: e.dma_start(
                                   out=pin_s, in_=src),
                               waits=w_in, incs=[(f"sP{ji % 3}", 16)], ms=True)

        def q_sgn(ji):
            wi, src, F, dst, is3 = jobs[ji]
            pin_s = pin[:, (ji % 3) * D:(ji % 3) * D + F]
            s1_s = s1b[:, (ji % 2) * D:(ji % 2) * D + F]
            w_s = [(f"sP{ji % 3}", ms_in[ji]), ("cv", ms_th[wi])] + \
                  ([("cv", ms_cmb[ji - 2])] if ji >= 2 else [])
            ms_s1[ji] = E.emit("scalar",
                               lambda e, s1_s=s1_s, pin_s=pin_s, wi=wi: e.sign(
                                   s1_s, pin_s, bias=thc[:, 4 + wi:5 + wi]),
                               waits=w_s, incs=[("cs", 1)], ms=True)

        def q_lt(ji):
            wi, src, F, dst, is3 = jobs[ji]
            pin_s = pin[:, (ji % 3) * D:(ji % 3) * D + F]
            v2_s = v2b[:, (ji % 2) * D:(ji % 2) * D + F]
            w_v = [(f"sP{ji % 3}", ms_in[ji]), ("cv", ms_th[wi])] + \
                  ([("cv", ms_cmb[ji - 2])] if ji >= 2 else [])
            ms_v2[ji] = E.emit("vector",
                               lambda e, v2_s=v2_s, pin_s=pin_s, wi=wi: e.tensor_scalar(
                                   v2_s, pin_s, thc[:, 4 + wi:5 + wi], 0.5,
                                   ALU.is_lt, ALU.subtract),
                               waits=w_v, incs=[("cv", 1)], ms=True)

        def q_cmb(ji):
            if ji < 0 or ji >= NJ:
                return
            wi, src, F, dst, is3 = jobs[ji]
            s1_s = s1b[:, (ji % 2) * D:(ji % 2) * D + F]
            v2_s = v2b[:, (ji % 2) * D:(ji % 2) * D + F]
            qn_s = qnb[:, (ji % 2) * D:(ji % 2) * D + F]
            w_c = [("cs", ms_s1[ji])]
            if ji >= 2:
                tsem, tval = ms_tp[ji - 2]
                w_c.append((tsem, tval))
            ms_cmb[ji] = E.emit("vector",
                                lambda e, qn_s=qn_s, v2_s=v2_s, s1_s=s1_s:
                                e.scalar_tensor_tensor(qn_s, v2_s, -2.0, s1_s,
                                                       ALU.mult, ALU.add),
                                waits=w_c, incs=[("cv", 1)], ms=True)

        def q_tp(ji):
            if ji < 0 or ji >= NJ:
                return
            wi, src, F, dst, is3 = jobs[ji]
            qn_s = qnb[:, (ji % 2) * D:(ji % 2) * D + F]
            nt = F // 128
            if wi == 2:
                qt_s = qtb[:, (ji % 2) * D:(ji % 2) * D + F]
                w_t = [("cv", ms_cmb[ji])]
                if ji >= 2:
                    w_t.append((f"sS{ji % 2}", ms_st[ji - 2]))
                ms_tp[ji] = (f"sT{ji % 2}", E.emit(
                    "sync",
                    lambda e, qt_s=qt_s, qn_s=qn_s, nt=nt: e.dma_start_transpose(
                        qt_s.rearrange("p (t q) -> p t q", t=nt), qn_s),
                    waits=w_t, incs=[(f"sT{ji % 2}", 16)], ms=True))
            else:
                hb = job_hb[ji]
                wt = w1t if wi == 0 else w2t
                sem = f"sW1{hb % 2}" if wi == 0 else f"sW2{hb % 2}"
                dstt = wt[:, (hb % 2) * D:(hb % 2) * D + D]
                w_t = [("cv", ms_cmb[ji])]
                if hb >= 2:
                    pms = ms_p1.get((0, hb - 2)) if wi == 0 else ms_p2.get((0, hb - 2))
                    assert pms is not None, (ji, hb)
                    w_t.append(("ct", pms))
                    prev_ji = w1_ji[hb - 2] if wi == 0 else w2_ji[hb - 2]
                    w_t.append((f"sS{prev_ji % 2}", ms_st[prev_ji]))
                val = E.emit(
                    "sync",
                    lambda e, dstt=dstt, qn_s=qn_s, nt=nt: e.dma_start_transpose(
                        dstt.rearrange("p (t q) -> p t q", t=nt), qn_s),
                    waits=w_t, incs=[(sem, 16)], ms=True)
                ms_tp[ji] = (sem, val)
                if wi == 0:
                    ms_w1[(0, hb)] = val
                else:
                    ms_w2[(0, hb)] = val

        def q_st(ji):
            if ji < 0 or ji >= NJ:
                return
            wi, src, F, dst, is3 = jobs[ji]
            nt = F // 128
            if wi == 2:
                qt_s = qtb[:, (ji % 2) * D:(ji % 2) * D + F]
                src_st = qt_s.rearrange("p (t q) -> p t q", t=nt)
            else:
                hb = job_hb[ji]
                wt = w1t if wi == 0 else w2t
                src_st = wt[:, (hb % 2) * D:(hb % 2) * D + D]
            tsem, tval = ms_tp[ji]
            ms_st[ji] = E.emit("gpsimd",
                               lambda e, dst=dst, src_st=src_st: e.dma_start(
                                   out=dst, in_=src_st),
                               waits=[(tsem, tval)],
                               incs=[(f"sS{ji % 2}", 16)], ms=True)

        job_next = [0]

        def q_tick():
            ji = job_next[0]
            if ji >= NJ + 2:
                return False
            if ji == 0:
                q_in(0)
                q_in(1)
            q_in(ji + 2)
            q_st(ji - 2)
            if ji < NJ:
                q_sgn(ji)
                q_lt(ji)
            q_cmb(ji - 1)
            q_tp(ji - 1)
            job_next[0] += 1
            return True

        def prep_done_waits():
            w = [("cs", ms_s1[NJ - 1]), ("cv", ms_cmb[NJ - 1])]
            w += [(f"sP{i}", E.n[f"sP{i}"]) for i in range(3)]
            w += [(f"sT{i}", E.n[f"sT{i}"]) for i in range(2)]
            w += [(f"sS{i}", E.n[f"sS{i}"]) for i in range(2)]
            return w

        emit_x_dma(0)
        emit_x_dma(1)
        while att_fin_next[0] < MT:
            attn_tick()
        for _ in range(6):
            q_tick()

        attn_waits = ([(f"sHT{i}", E.n[f"sHT{i}"]) for i in range(2)] +
                      [(f"sHS{i}", E.n[f"sHS{i}"]) for i in range(2)] +
                      [(f"sHN{i}", E.n[f"sHN{i}"]) for i in range(2)] +
                      [(f"sX{i}", E.n[f"sX{i}"]) for i in range(2)] +
                      [(f"sG{i}", E.n[f"sG{i}"]) for i in range(2)] +
                      [("cs", ms_gab[NCHUNK - 1])])
        attn_st.close()
        post_st = contextlib.ExitStack()
        hTq = post_st.enter_context(nc.sbuf_tensor("hTq", [128, HBK * QT], BF16))
        y2acc = post_st.enter_context(nc.sbuf_tensor("y2acc", [128, 4 * D], F32))
        w3t = post_st.enter_context(nc.sbuf_tensor("w3t", [128, 2 * G3 * 512], BF16))
        first_post = [True]

        ms_h1q = [None]
        ms_sig = {}
        ms_hT = {}
        ms_w3l = {}
        ms_f3u = {}
        ms_ev = {}
        ms_f3last = {}
        ln2_first = [True]
        ms_ln2free = [None] * 4
        ms_outd = [None] * 4
        ln2_bufs = {}

        def emit_h1q(q):
            w = [("cv", ms_norm[4 * q + 3]),
                 (f"sHS{(4 * q + 3) % 2}", ms_h1tp[4 * q + 3]),
                 (f"sHS{(4 * q + 2) % 2}", ms_h1tp[4 * q + 2])]
            if q > 0:
                w.append(("ct", ms_p2[(q - 1, HBK - 1)]))
            ms_h1q[0] = E.emit(
                "sync",
                lambda e, q=q: e.dma_start(
                    out=h1Th[:].rearrange("p (k n) -> p k n", k=KB),
                    in_=h1t_d[:, q * QT:(q + 1) * QT]
                        .rearrange("(k p) n -> p k n", p=128)),
                waits=w, incs=[("sH1", 16)], ms=True)

        def emit_w_load(q, hb):
            if hb >= HBK:
                return
            j1, j2 = w1_ji[hb], w2_ji[hb]
            assert ms_st[j1] is not None and ms_st[j2] is not None, (hb, j1, j2)
            w_1 = [(f"sS{j1 % 2}", ms_st[j1])]
            w_2x = []
            if q == 1:
                ps1 = w1_ji[HBK - 2 + hb] if hb < 2 else None
                ps2 = w2_ji[HBK - 2 + hb] if hb < 2 else None
                if ps1 is not None:
                    w_1.append((f"sS{ps1 % 2}", ms_st[ps1]))
                    w_2x.append((f"sS{ps2 % 2}", ms_st[ps2]))
            prev = (q, hb - 2) if hb >= 2 else ((q - 1, HBK - 2 + hb) if q > 0 else None)
            if prev in ms_p1:
                w_1.append(("ct", ms_p1[prev]))
            ms_w1[(q, hb)] = E.emit(
                "sync",
                lambda e, hb=hb: e.dma_start(
                    out=w1t[:, (hb % 2) * D:(hb % 2) * D + D],
                    in_=q1t_d[hb * 128:(hb + 1) * 128, :]),
                waits=w_1, incs=[(f"sW1{hb % 2}", 16)], ms=True)
            w_2 = [(f"sS{j2 % 2}", ms_st[j2])] + w_2x
            if prev in ms_p2:
                w_2.append(("ct", ms_p2[prev]))
            ms_w2[(q, hb)] = E.emit(
                "sync",
                lambda e, hb=hb: e.dma_start(
                    out=w2t[:, (hb % 2) * D:(hb % 2) * D + D],
                    in_=q2t_d[hb * 128:(hb + 1) * 128, :]),
                waits=w_2, incs=[(f"sW2{hb % 2}", 16)], ms=True)

        def emit_f12(q, hb):
            wslab1 = w1t[:, (hb % 2) * D:(hb % 2) * D + D]
            wslab2 = w2t[:, (hb % 2) * D:(hb % 2) * D + D]
            w_mm = [(f"sW1{hb % 2}", ms_w1[(q, hb)]), ("sH1", ms_h1q[0])]
            prevsig = (q, hb - 1) if hb >= 1 else ((q - 1, HBK - 1) if q > 0 else None)
            if prevsig in ms_sig:
                w_mm.append(("cs", ms_sig[prevsig]))
            for k in range(KB):
                def mk(e, k=k, wslab=wslab1):
                    return e.matmul(px1[:], wslab[:, k * 128:(k + 1) * 128],
                                    h1Th[:, k * QT:(k + 1) * QT],
                                    start=(k == 0), stop=(k == KB - 1))
                if k == 0:
                    E.emit("tensor", mk, waits=w_mm)
                elif k == KB - 1:
                    ms_p1[(q, hb)] = E.emit("tensor", mk, incs=[("ct", 1)], ms=True)
                else:
                    E.emit("tensor", mk)
            w_mm2 = [(f"sW2{hb % 2}", ms_w2[(q, hb)])]
            prevhT = (q, hb - 1) if hb >= 1 else ((q - 1, HBK - 1) if q > 0 else None)
            if prevhT in ms_hT:
                w_mm2.append(("cv", ms_hT[prevhT]))
            for k in range(KB):
                def mk2(e, k=k, wslab=wslab2):
                    return e.matmul(px2[:], wslab[:, k * 128:(k + 1) * 128],
                                    h1Th[:, k * QT:(k + 1) * QT],
                                    start=(k == 0), stop=(k == KB - 1))
                if k == 0:
                    E.emit("tensor", mk2, waits=w_mm2)
                elif k == KB - 1:
                    ms_p2[(q, hb)] = E.emit("tensor", mk2, incs=[("ct", 1)], ms=True)
                else:
                    E.emit("tensor", mk2)
            ms_sig[(q, hb)] = E.emit(
                "scalar",
                lambda e, hb=hb: e.activation(
                    sgb[:, (hb % 2) * QT:(hb % 2) * QT + QT],
                    px1[:], AF.Sigmoid, scale=0.5),
                waits=[("ct", ms_p1[(q, hb)])], incs=[("cs", 1)], ms=True)
            w_ht = [("cs", ms_sig[(q, hb)]), ("ct", ms_p2[(q, hb)])]
            prevf3 = (q - 1, hb // G3)
            if prevf3 in ms_f3last:
                w_ht.append(("ct", ms_f3last[prevf3]))
            if first_post[0]:
                first_post[0] = False
                w_ht = attn_waits + w_ht
            ms_hT[(q, hb)] = E.emit(
                "vector",
                lambda e, hb=hb: e.tensor_mul(
                    hTq[:, hb * QT:(hb + 1) * QT],
                    sgb[:, (hb % 2) * QT:(hb % 2) * QT + QT], px2[:]),
                waits=w_ht, incs=[("cv", 1)], ms=True)

        def emit_w3_load(q, g, dq, prev):
            w = []
            if prev in ms_f3u:
                w.append(("ct", ms_f3u[prev]))
            lo_c = (g * G3 * 128) // 2048
            hi_c = ((g + 1) * G3 * 128 - 1) // 2048
            for ccx in range(lo_c, hi_c + 1):
                jid = w3_last_ji[ccx]
                assert ms_st[jid] is not None, (g, dq, ccx, jid)
                w.append((f"sS{jid % 2}", ms_st[jid]))
            ms_w3l[(q, g, dq)] = E.emit(
                "sync",
                lambda e, g=g, dq=dq: e.dma_start(
                    out=w3t[:, (dq % 2) * G3 * 512:(dq % 2) * G3 * 512 + G3 * 512]
                        .rearrange("p (t n) -> p t n", t=G3),
                    in_=q3t_d[g * G3 * 128:(g + 1) * G3 * 128, dq * 512:(dq + 1) * 512]
                        .rearrange("(t p) n -> p t n", p=128)),
                waits=w, incs=[(f"sW3{dq % 2}", 16)], ms=True)

        def emit_f3(q, g):
            unit = 0
            msu = None
            for dq in range(4):
                if dq == 2:
                    emit_w3_load(q, g, 2, (q, g, 0))
                    emit_w3_load(q, g, 3, (q, g, 1))
                for m in range(4):
                    pp = pyA if unit % 2 == 0 else pyB
                    w_u = [("cv", ms_hT[(q, (g + 1) * G3 - 1)]),
                           (f"sW3{dq % 2}", ms_w3l[(q, g, dq)])]
                    prev_ev = None
                    if unit >= 2:
                        prev_ev = ms_ev.get((q, g, unit - 2))
                    elif g >= 1:
                        prev_ev = ms_ev.get((q, g - 1, 14 + unit))
                    elif q > 0:
                        prev_ev = ms_ev.get((q - 1, NG3 - 1, 14 + unit))
                    if prev_ev is not None:
                        w_u.append(("cv", prev_ev))
                    for kk in range(G3):
                        def mk3(e, kk=kk, m=m, dq=dq, g=g, pp=pp):
                            return e.matmul(
                                pp[:],
                                hTq[:, (g * G3 + kk) * QT + m * 128:
                                       (g * G3 + kk) * QT + (m + 1) * 128],
                                w3t[:, (dq % 2) * G3 * 512 + kk * 512:
                                       (dq % 2) * G3 * 512 + kk * 512 + 512],
                                start=(kk == 0), stop=(kk == G3 - 1))
                        if kk == 0:
                            E.emit("tensor", mk3, waits=w_u)
                        elif kk == G3 - 1:
                            msu = E.emit("tensor", mk3, incs=[("ct", 1)], ms=True)
                        else:
                            E.emit("tensor", mk3)
                    ms_f3u[(q, g, dq)] = msu
                    ydst = y2acc[:, m * D + dq * 512: m * D + dq * 512 + 512]
                    w_e = [("ct", msu)]
                    if g == 0:
                        mse = E.emit("vector",
                                     lambda e, ydst=ydst, pp=pp: e.tensor_copy(ydst, pp[:]),
                                     waits=w_e, incs=[("cv", 1)], ms=True)
                    else:
                        mse = E.emit("vector",
                                     lambda e, ydst=ydst, pp=pp: e.scalar_tensor_tensor(
                                         ydst, pp[:], 1.0, ydst, ALU.bypass, ALU.add),
                                     waits=w_e, incs=[("cv", 1)], ms=True)
                    ms_ev[(q, g, unit)] = mse
                    unit += 1
            if g + 1 < NG3:
                emit_w3_load(q, g + 1, 0, (q, g, 2))
                emit_w3_load(q, g + 1, 1, (q, g, 3))
            elif q + 1 < NQ:
                emit_w3_load(q + 1, 0, 0, (q, g, 2))
                emit_w3_load(q + 1, 0, 1, (q, g, 3))
            ms_f3last[(q, g)] = msu

        def alloc_ln2():
            ln2_bufs["uU"] = pin[:, 0:D]
            ln2_bufs["outb"] = pin[:, D:3 * D]
            ln2_bufs["h1nb2"] = s1b[:, 0:2 * D]

        def emit_ln2_m(q, m):
            uU_a = ln2_bufs["uU"]
            h1nb2_a = ln2_bufs["h1nb2"]
            outb_a = ln2_bufs["outb"]
            last_ev = ms_ev[(q, NG3 - 1, 15)]
            if True:
                mg = q * 4 + m
                w_hn = [("cv", ms_ln2free[(m - 2) % 4])] if m >= 2 else []
                if ln2_first[0]:
                    ln2_first[0] = False
                    w_hn = prep_done_waits() + w_hn
                elif m < 2 and q > 0:
                    w_hn = [("cv", ms_ln2free[(m + 2) % 4])] + w_hn
                ms_hnm = E.emit(
                    "sync",
                    lambda e, mg=mg, m=m: e.dma_start(
                        out=h1nb2_a[:, (m % 2) * D:(m % 2) * D + D],
                        in_=h1n_d[mg * 128:(mg + 1) * 128, :]),
                    waits=w_hn, incs=[(f"sHB{m % 2}", 16)], ms=True)
                h1s = h1nb2_a[:, (m % 2) * D:(m % 2) * D + D]
                yv = y2acc[:, m * D:(m + 1) * D]
                ob = outb_a[:, (m % 2) * D:(m % 2) * D + D]
                E.emit("vector",
                       lambda e, h1s=h1s, yv=yv: e.scalar_tensor_tensor(
                           uU_a, h1s, 4.0, yv, ALU.mult, ALU.add,
                           accum_out=st2[:, 0:1]),
                       waits=[(f"sHB{m % 2}", ms_hnm), ("cv", last_ev)],
                       incs=[("cv", 1)])
                w_sq = [(f"sO{m % 2}", ms_outd[(m - 2) % 4])] if m >= 2 else \
                       ([(f"sO{m % 2}", ms_outd[(m + 2) % 4])] if q > 0 else [])
                mq2 = E.emit("vector",
                       lambda e, ob=ob: e.scalar_tensor_tensor(
                           ob, uU_a, 1.0, uU_a, ALU.bypass, ALU.mult,
                           accum_out=st2[:, 1:2]),
                       waits=w_sq, incs=[("cv", 1)], ms=True)
                mq2 = E.emit("vector",
                       lambda e: e.tensor_scalar(st2[:, 2:3], st2[:, 0:1],
                                                 -1.0 / D, None, ALU.mult),
                       waits=[("cv", mq2)], incs=[("cv", 1)], ms=True)
                mq2 = E.emit("vector",
                       lambda e: e.tensor_scalar(st2[:, 3:4], st2[:, 1:2],
                                                 1.0 / D, EPS16, ALU.mult, ALU.add),
                       waits=[("cv", mq2)], incs=[("cv", 1)], ms=True)
                mq2 = E.emit("vector",
                       lambda e: e.tensor_scalar(st2[:, 4:5], st2[:, 2:3],
                                                 st2[:, 2:3], None, ALU.mult),
                       waits=[("cv", mq2)], incs=[("cv", 1)], ms=True)
                mq2 = E.emit("vector",
                       lambda e: e.tensor_sub(st2[:, 5:6], st2[:, 3:4], st2[:, 4:5]),
                       waits=[("cv", mq2)], incs=[("cv", 1)], ms=True)
                msv2 = E.emit("vector",
                              lambda e: e.tensor_scalar(st2[:, 5:6], st2[:, 5:6],
                                                        1e-30, None, ALU.max),
                              waits=[("cv", mq2)], incs=[("cv", 1)], ms=True)
                mss2 = E.emit("scalar",
                              lambda e: e.activation(st2[:, 6:7], st2[:, 5:6], AF.Sqrt),
                              waits=[("cv", msv2)], incs=[("cs", 1)], ms=True)
                msr2 = E.emit("vector", lambda e: e.reciprocal(st2[:, 7:8], st2[:, 6:7]),
                       waits=[("cs", mss2)], incs=[("cv", 1)], ms=True)
                ms_ln2free[m % 4] = E.emit(
                    "vector",
                    lambda e, ob=ob: e.tensor_scalar(ob, uU_a, st2[:, 2:3],
                                                     st2[:, 7:8],
                                                     ALU.add, ALU.mult),
                    waits=[("cv", msr2)], incs=[("cv", 1)], ms=True)
                ms_outd[m % 4] = E.emit(
                    "sync",
                    lambda e, mg=mg, m=m: e.dma_start(
                        out=out_d[mg * 128:(mg + 1) * 128, :],
                        in_=outb_a[:, (m % 2) * D:(m % 2) * D + D]),
                    waits=[("cv", ms_ln2free[m % 4])], incs=[(f"sO{m % 2}", 16)], ms=True)
                if _DEBUG_Y2:
                    E.emit("sync",
                           lambda e, mg=mg, m=m: e.dma_start(
                               out=dbgy2_d[mg * 128:(mg + 1) * 128, :],
                               in_=y2acc[:, m * D:(m + 1) * D]),
                           waits=[("cv", ms_ev[(q, NG3 - 1, 15)])],
                           incs=[("sDBG", 16)], ms=True)

        with post_st:
            alloc_ln2()
            for q in range(NQ):
                emit_h1q(q)
                if q > 0:
                    emit_w_load(q, 0)
                    emit_w_load(q, 1)
                for hb in range(HBK):
                    if q == 0:
                        assert (0, hb) in ms_w1 and (0, hb) in ms_w2, hb
                    emit_f12(q, hb)
                    if q == 0:
                        q_tick()
                        q_tick()
                        q_tick()
                    else:
                        emit_w_load(q, hb + 2)
                    if q > 0 and hb in (4, 14, 24, 34):
                        emit_ln2_m(q - 1, (hb - 4) // 10)
                if q == 0:
                    while job_next[0] < NJ + 2:
                        q_tick()
                if q == 0:
                    emit_w3_load(0, 0, 0, None)
                    emit_w3_load(0, 0, 1, None)
                for g in range(NG3):
                    emit_f3(q, g)
            emit_ln2_m(NQ - 1, 0)
            emit_ln2_m(NQ - 1, 1)
            emit_ln2_m(NQ - 1, 2)
            emit_ln2_m(NQ - 1, 3)

            all_dma = [s for s in sem_names if s.startswith("s")]
            E.emit("sync", lambda e: e.nop(),
                   waits=[(nm, E.n[nm]) for nm in all_dma if E.n[nm] > 0])
            E.run_block(nc)
        prep_st.close()
    return nc


def kernel(L0_attn_x, L0_attn_weight, L0_attn_bias, L0_attn_decay_value,
           L0_ffn_w1, L0_ffn_w2, L0_ffn_w3):
    x = np.ascontiguousarray(np.asarray(L0_attn_x, dtype=np.float32)
                             .reshape(NCORE * TOK, D))
    wgt = np.asarray(L0_attn_weight, dtype=np.float32)
    bias = np.asarray(L0_attn_bias, dtype=np.float32)
    dv = np.asarray(L0_attn_decay_value, dtype=np.float32)
    w1 = np.ascontiguousarray(np.asarray(L0_ffn_w1, dtype=np.float32))
    w2 = np.ascontiguousarray(np.asarray(L0_ffn_w2, dtype=np.float32))
    w3 = np.ascontiguousarray(np.asarray(L0_ffn_w3, dtype=np.float32))

    w0b = np.ascontiguousarray(np.broadcast_to(wgt[0:1, :], (128, D)))
    w1b = np.ascontiguousarray(np.broadcast_to(wgt[1:2, :], (128, D)))
    biasb = np.ascontiguousarray(np.broadcast_to(bias.reshape(1, D), (128, D)))
    dvrep = np.ascontiguousarray(np.broadcast_to(dv.reshape(1, 2), (128, 2)))
    onesrow = np.ones((1, 128), np.float32)
    onescol = np.ones((128, 1), np.float32)
    epsh = np.full((1, 1), 0.5e-8, np.float32)
    wflat = [w1.reshape(-1), w2.reshape(-1), w3.reshape(-1)]

    if "prog" not in _NC_CACHE:
        _NC_CACHE["prog"] = build_program()
    nc = _NC_CACHE["prog"]

    in_maps = []
    for c in range(NCORE):
        wsl = np.stack([wf[c * SLICE:(c + 1) * SLICE] for wf in wflat])
        in_maps.append({
            "x": x[c * TOK:(c + 1) * TOK],
            "w0b": w0b, "w1b": w1b, "biasb": biasb, "dvrep": dvrep,
            "onesrow": onesrow, "onescol": onescol, "epsh": epsh,
            "wsl": np.ascontiguousarray(wsl),
            "w1f": w1, "w2f": w2, "w3f": w3,
        })
    res = run_bass_kernel_spmd(nc, in_maps, core_ids=list(range(NCORE)))
    out = np.concatenate([np.asarray(r["out"]) for r in res.results], axis=0)
    return out.reshape(2, 8192, D)


if __name__ == "__main__":
    print("building program ...")
    build_program()
    print("built ok")
